# version 03
# baseline (speedup 1.0000x reference)
import contextlib
import numpy as np
import ml_dtypes
import concourse.bass as bass
import concourse.mybir as mybir
from concourse.bass_utils import run_bass_kernel_spmd

F32 = mybir.dt.float32
BF16 = mybir.dt.bfloat16
AF = mybir.ActivationFunctionType
OP = mybir.AluOpType

B, N, C, H, HD = 4, 4096, 1024, 16, 64
NT = N // 128
ONT = 16
NH_OWN = 8
BF = ml_dtypes.bfloat16

ENGNAME = dict(t="tensor", v="vector", s="scalar", y="sync", g="gpsimd")
ESEM = dict(t="pe", v="ve", s="se", y="dy", g="dg")


class Sec:
    def __init__(self, P, eng, tag):
        self.P, self.e, self.tag = P, eng, tag
        self.last = None
        self.ndma = 0

    def wait(self, sem, val=None):
        P = self.P
        P.sem(sem)
        val = P.counts[sem] if val is None else val
        if val <= 0:
            return
        w = P.waited[self.tag]
        if w.get(sem, -1) < val:
            self.e.wait_ge(P.sems[sem], val)
            w[sem] = val

    def wait_all(self):
        for sname in list(self.P.sems):
            self.wait(sname)

    def edge(self):
        if self.last is not None:
            P, tag = self.P, self.tag
            sname = ESEM[tag] if tag not in ("y", "g") else "gc"
            self.last.then_inc(P.sem(sname), 1)
            P.counts[sname] += 1
            self.last = None
            self.wait(sname)

    def __getattr__(self, n):
        a = getattr(self.e, n)
        if not callable(a):
            return a
        P, tag = self.P, self.tag
        if n == "dma_start":
            def wd(*args, **kw):
                r = a(*args, **kw)
                qs = ESEM[tag]
                r.then_inc(P.sem(qs), 16)
                P.counts[qs] += 16
                self.ndma += 1
                return r
            return wd
        if n == "collective_compute":
            def wc(*args, **kw):
                r = a(*args, **kw)
                r.then_inc(P.sem("cc"), 1)
                P.counts["cc"] += 1
                return r
            return wc

        def w(*args, **kw):
            r = a(*args, **kw)
            if hasattr(r, "then_inc"):
                self.last = r
            return r
        return w


class Prog:

    def __init__(self, name="prog"):
        self.nc = bass.Bass("TRN2", target_bir_lowering=False)
        self.es = contextlib.ExitStack()
        self.block = self.es.enter_context(self.nc.Block())
        self.sems = {}
        self.counts = {}
        self.waited = {k: {} for k in ENGNAME}
        self.consts = {}
        self.ps = [self.nc.alloc_psum_tensor(f"psb{i}", [128, 512], F32)
                   for i in range(7)]
        self.pt = self.nc.alloc_psum_tensor("psbt", [128, 1024], BF16)
        self.carena = self.nc.alloc_sbuf_tensor("carena", [128, 16], F32)
        self._ccols = {}

    def sem(self, name):
        if name not in self.sems:
            self.sems[name] = self.es.enter_context(self.nc.semaphore(name))
            self.counts[name] = 0
        return self.sems[name]

    def sec(self, tag, fn):
        holder = {}

        def run(e):
            w = Sec(self, e, tag)
            fn(w)
            holder["w"] = w
        getattr(self.block, ENGNAME[tag])(run)
        w = holder["w"]
        sname = ESEM[tag]
        if tag in ("y", "g"):
            if tag == "g" and w.last is not None:
                self.sem("gc")
                w.last.then_inc(self.sems["gc"], 1)
                self.counts["gc"] += 1
                return self.counts["gc"]
            return self.counts[sname]
        if w.last is not None:
            w.last.then_inc(self.sem(sname), 1)
            self.counts[sname] += 1
        return self.counts[sname]

    def barrier(self):
        for tag in ENGNAME:
            self.sec(tag, lambda w: w.wait_all())

    def val(self, sem):
        return self.counts.get(sem, 0)

    def inp(self, name, shape, dtype=F32):
        return self.nc.dram_tensor(name, list(shape), dtype, kind="ExternalInput").ap()

    def outp(self, name, shape, dtype=F32):
        return self.nc.dram_tensor(name, list(shape), dtype, kind="ExternalOutput").ap()

    def dram(self, name, shape, dtype=F32):
        return self.nc.dram_tensor(name, list(shape), dtype, kind="Internal").ap()

    def sb(self, stack, name, shape, dtype, side=None):
        self._n = getattr(self, "_n", 0) + 1
        return stack.enter_context(
            self.nc.sbuf_tensor(f"{name}_{self._n}", list(shape), dtype, side=side))

    def ccol(self, value, rows=128):
        key = float(value)
        if key not in self._ccols:
            slot = len(self._ccols)
            assert slot < 16
            ap = self.carena[:, slot:slot + 1]
            self.sec("g", lambda g: g.memset(ap, float(value)))
            self._ccols[key] = slot
        return self.carena[0:rows, self._ccols[key]:self._ccols[key] + 1]

    def finish(self):
        self.es.close()
        return self.nc



def emit_norm(P, xt, out, scr, *, eps, kind="ln", gain_bc=None, n=1024.0):
    st = scr["st"]
    sq = scr["sq"]
    F = 1024 if n == 1024.0 else int(n)
    inv = 1.0 / n
    c_se = P.sec("s", lambda s: (
        s.wait("ve"), s.wait("dy"), s.wait("dg"),
        s.activation(sq[:, 0:F], xt, AF.Square)))

    def vstats(v):
        v.wait("se", c_se)
        v.reduce_sum(st[:, 0:1], sq[:, 0:F], axis=mybir.AxisListType.X)
        if kind == "ln":
            v.reduce_sum(st[:, 1:2], xt, axis=mybir.AxisListType.X)
            v.edge()
            v.tensor_scalar(out=st[:, 2:3], in0=st[:, 1:2], scalar1=-inv,
                            scalar2=None, op0=OP.mult)
            v.tensor_scalar(out=st[:, 3:4], in0=st[:, 0:1], scalar1=inv,
                            scalar2=None, op0=OP.mult)
            v.edge()
            v.tensor_mul(st[:, 4:5], st[:, 2:3], st[:, 2:3])
            v.edge()
            v.tensor_sub(st[:, 5:6], st[:, 3:4], st[:, 4:5])
        else:
            v.edge()
            v.tensor_scalar(out=st[:, 5:6], in0=st[:, 0:1], scalar1=inv,
                            scalar2=None, op0=OP.mult)
    c_ve = P.sec("v", vstats)
    epsc = P.ccol(eps)
    c_se2 = P.sec("s", lambda s: (
        s.wait("ve", c_ve), s.wait("gc"),
        s.activation(st[:, 6:7], st[:, 5:6], AF.Ln, bias=epsc),
        s.edge(),
        s.activation(st[:, 7:8], st[:, 6:7], AF.Exp, scale=-0.5)))

    def vapply(v):
        v.wait("se", c_se2)
        if kind == "ln":
            v.tensor_scalar(out=out, in0=xt, scalar1=st[:, 2:3],
                            scalar2=st[:, 7:8], op0=OP.add, op1=OP.mult)
        else:
            v.tensor_scalar(out=out, in0=xt, scalar1=st[:, 7:8],
                            scalar2=None, op0=OP.mult)
            if gain_bc is not None:
                v.edge()
                v.tensor_mul(out, out, gain_bc)
    c_vap = P.sec("v", vapply)
    return c_vap


def emit_transpose_tile(P, src_bf, dst_fm, tt, c_ready, ident):
    for half in range(2):
        def tfn(t, half=half):
            t.wait("dg")
            t.wait("ve", c_ready)
            t.wait("ve")
            for i in range(4):
                c = half * 4 + i
                t.transpose(P.pt[:, i * 128:(i + 1) * 128],
                            src_bf[:, c * 128:(c + 1) * 128], ident[:, :])
        cpe = P.sec("t", tfn)

        def vfn(v, half=half):
            v.wait("pe", cpe)
            v.tensor_copy(dst_fm[:, half * 4:(half + 1) * 4,
                                 tt * 128:(tt + 1) * 128],
                          P.pt[:, 0:512].rearrange("p (i f) -> p i f", i=4))
        P.sec("v", vfn)


def mm_chain(t, psum_ap, pairs, tile_position=None):
    nchunks = len(pairs)
    for i, (l, r) in enumerate(pairs):
        kw = {}
        if tile_position is not None:
            kw["tile_position"] = tile_position
        t.matmul(psum_ap, l, r, start=(i == 0), stop=(i == nchunks - 1), **kw)



def emit_ln_to_fm(P, xb, hn, es1, ident, ntiles=NT, gain_bc=None, rms_eps=None):
    xt2 = P.sb(es1, "lnf_xt2", [128, 2, 1024], F32)
    hnt = P.sb(es1, "lnf_hnt", [128, 1024], BF16)
    ht = P.sb(es1, "lnf_ht", [128, 1024], F32) if rms_eps is not None else None
    sq = P.sb(es1, "lnf_sq", [128, 1024], F32)
    st = P.sb(es1, "lnf_st", [128, 8], F32)
    scr = dict(sq=sq, st=st)
    for tt in range(ntiles):
        buf = tt % 2
        P.sec("y", lambda y, tt=tt, buf=buf: (
            y.wait("ve"), y.wait("se"),
            y.dma_start(out=xt2[:, buf, :], in_=xb[tt * 128:(tt + 1) * 128, :])))
        if rms_eps is None:
            c = emit_norm(P, xt2[:, buf, :], hnt[:, :], scr, eps=1e-5, kind="ln")
        else:
            c0 = emit_norm(P, xt2[:, buf, :], ht[:, :], scr, eps=1e-5, kind="ln")
            c = emit_norm(P, ht[:, :], hnt[:, :], scr, eps=rms_eps, kind="rms",
                          gain_bc=gain_bc)
        emit_transpose_tile(P, hnt, hn, tt, c, ident)


def emit_partial_y(P, av, wot, yp, es5):
    w_o = P.sb(es5, "w_o", [128, 4, 1024], BF16)
    yt = P.sb(es5, "yt", [128, 2, 1024], F32)
    P.sec("g", lambda g: g.dma_start(out=w_o[:, :, :],
                                     in_=wot.rearrange("(c p) n -> p c n", p=128)))
    for tt in range(NT):
        buf = tt % 2
        def yfn(t, tt=tt):
            t.wait("dg")
            t.wait("ve")
            for j in range(2):
                mm_chain(t, P.ps[tt % 2 * 2 + j][:, :],
                         [(av[:, cch, tt * 128:(tt + 1) * 128],
                           w_o[:, cch, j * 512:(j + 1) * 512])
                          for cch in range(4)])
        cpe = P.sec("t", yfn)

        def evfn(v, tt=tt, buf=buf):
            v.wait("pe", cpe)
            v.wait("dy")
            v.tensor_copy(yt[:, buf, 0:512], P.ps[tt % 2 * 2][:, :])
            v.tensor_copy(yt[:, buf, 512:1024], P.ps[tt % 2 * 2 + 1][:, :])
        cv = P.sec("v", evfn)
        P.sec("y", lambda y, tt=tt, buf=buf: (
            y.wait("ve", cv),
            y.dma_start(out=yp[tt * 128:(tt + 1) * 128, :], in_=yt[:, buf, :])))


def emit_btab(P, nq, nk, btabs, btab_dram, btabr, bbc, onesr, lnscale_val, st8):
    def vbt(v):
        v.wait("se"); v.wait("ve"); v.wait("pe")
        v.reduce_max(st8[0:8, 0:1], nk, axis=mybir.AxisListType.X)
        for gg in range(8):
            v.reduce_max(btabs[:, gg:gg + 1], nq[0:8, gg * 512:(gg + 1) * 512],
                         axis=mybir.AxisListType.X)
        v.edge()
        v.tensor_scalar(out=btabs[:, :], in0=btabs[:, :], scalar1=st8[0:8, 0:1],
                        scalar2=None, op0=OP.mult)
    cbt = P.sec("v", vbt)
    guardc = P.ccol(1e-30, rows=8)
    lnsc = P.ccol(lnscale_val, rows=8)
    cbt2 = P.sec("s", lambda s: (
        s.wait("ve", cbt), s.wait("gc"),
        s.activation(btabs[:, :], btabs[:, :], AF.Ln, bias=guardc),
        s.edge(),
        s.activation(btabs[:, :], btabs[:, :], AF.Exp, scale=0.5, bias=lnsc),
        s.edge(),
        s.mul(btabs[:, :], btabs[:, :], -1.0)))
    P.sec("y", lambda y: (y.wait("se", cbt2),
                          y.dma_start(out=btab_dram.rearrange("o (g h) -> (o h) g", h=8),
                                      in_=btabs[:, :], allow_slow_non_contiguous=True)))
    cr = P.val("dy")
    P.sec("y", lambda y: (y.wait("dy", cr),
                          y.dma_start(out=btabr[:, :], in_=btab_dram)))
    btr_bf = btabr
    cbf = P.val("dy")
    cpe = P.sec("t", lambda t: (
        t.wait("dy", cbf), t.wait("gc"), t.wait("ve"),
        mm_chain(t, P.ps[3][:, 0:64], [(onesr[:, :], btabr[:, :])])))
    P.sec("v", lambda v: (v.wait("pe", cpe),
                          v.tensor_copy(bbc.ap().rearrange("p g h -> p (g h)"),
                                        P.ps[3][:, 0:64])))


def build_stageA1(debug=False):
    P = Prog("A1")
    nc = P.nc
    xb = P.inp("xb", [N, C])
    wq1t = P.inp("wq1t", [C, 32])
    wk1t = P.inp("wk1t", [C, 32])
    wq2t = P.inp("wq2t", [C, 64])
    wk2t = P.inp("wk2t", [C, 64])
    wvt = P.inp("wvt", [C, 512])
    wot = P.inp("wot", [512, C])
    identc = P.inp("identc", [128, 128])
    e1c = P.inp("e1c", [32, 2, 128])
    e2c = P.inp("e2c", [64, 2, 128])
    eh1c = P.inp("eh1c", [32, 8])
    eh2c = P.inp("eh2c", [64, 8])
    dmc = P.inp("dmc", [128, 4, 512])
    s4c = P.inp("s4c", [128, 2, 128])
    yp = P.outp("yp", [N, C])
    if debug:
        dbg_bt = P.outp("dbg_bt", [8, 8])
        dbg_qt = P.outp("dbg_qt", [128, 2 * 4096])
        dbg_kt = P.outp("dbg_kt", [128, 2 * 4096])
        dbg_av = P.outp("dbg_av", [128, 4 * 4096])
        dbg_nrm = P.outp("dbg_nrm", [8, 2 * 4096])
        dbg_hn = P.outp("dbg_hn", [128, 8 * 4096])

    btab_dram = P.dram("btab_dram", [1, 64])

    es0 = contextlib.ExitStack()
    ident = P.sb(es0, "ident", [128, 128], BF16)
    e1 = P.sb(es0, "e1", [32, 2, 128], BF16)
    e2 = P.sb(es0, "e2", [64, 2, 128], BF16)
    eh1 = P.sb(es0, "eh1", [32, 8], BF16)
    eh2 = P.sb(es0, "eh2", [64, 8], BF16)
    dm = P.sb(es0, "dm", [128, 4, 512], BF16)
    s4 = P.sb(es0, "s4", [128, 2, 128], F32)
    w_q1 = P.sb(es0, "w_q1", [128, 8, 32], BF16)
    w_k1 = P.sb(es0, "w_k1", [128, 8, 32], BF16)
    w_q2 = P.sb(es0, "w_q2", [128, 8, 64], BF16)
    w_k2 = P.sb(es0, "w_k2", [128, 8, 64], BF16)
    w_v = P.sb(es0, "w_v", [128, 8, 512], BF16)
    kt = P.sb(es0, "kt", [128, 2, 4096], BF16)
    qt = P.sb(es0, "qt", [128, 2, 4096], BF16)
    bbc = P.sb(es0, "bbc", [128, 8, 8], F32)

    def loads(g):
        g.dma_start(out=ident[:, :], in_=identc)
        g.dma_start(out=e1[:, :, :], in_=e1c)
        g.dma_start(out=e2[:, :, :], in_=e2c)
        g.dma_start(out=eh1[:, :], in_=eh1c)
        g.dma_start(out=eh2[:, :], in_=eh2c)
        g.dma_start(out=dm[:, :, :], in_=dmc)
        g.dma_start(out=s4[:, :, :], in_=s4c)
        g.dma_start(out=w_q1[:, :, :], in_=wq1t.rearrange("(c p) n -> p c n", p=128))
        g.dma_start(out=w_k1[:, :, :], in_=wk1t.rearrange("(c p) n -> p c n", p=128))
        g.dma_start(out=w_q2[:, :, :], in_=wq2t.rearrange("(c p) n -> p c n", p=128))
        g.dma_start(out=w_k2[:, :, :], in_=wk2t.rearrange("(c p) n -> p c n", p=128))
        g.dma_start(out=w_v[:, :, :], in_=wvt.rearrange("(c p) n -> p c n", p=128))
    P.sec("g", loads)

    es1 = contextlib.ExitStack()
    hn = P.sb(es1, "hn", [128, 8, 4096], BF16)
    xt2 = P.sb(es1, "xt2", [128, 2, 1024], F32)
    hnt = P.sb(es1, "hnt", [128, 1024], BF16)
    sq = P.sb(es1, "sq", [128, 1024], F32)
    st = P.sb(es1, "st", [128, 8], F32)
    scr = dict(sq=sq, st=st)
    for tt in range(NT):
        buf = tt % 2
        P.sec("y", lambda y, tt=tt, buf=buf: (
            y.wait("ve"),
            y.dma_start(out=xt2[:, buf, :], in_=xb[tt * 128:(tt + 1) * 128, :])))
        cld = P.val("dy")
        P.sec("v", lambda v: v.wait("dy", cld))
        c = emit_norm(P, xt2[:, buf, :], hnt[:, :], scr, eps=1e-5, kind="ln")
        emit_transpose_tile(P, hnt, hn, tt, c, ident)

    es2 = contextlib.ExitStack()
    q1s = P.sb(es2, "q1s", [32, 4096], BF16)
    k1s = P.sb(es2, "k1s", [32, 4096], BF16)
    q2s = P.sb(es2, "q2s", [64, 4096], BF16)
    k2s = P.sb(es2, "k2s", [64, 4096], BF16)
    nrm = P.sb(es2, "nrm", [8, 2, 4096], BF16)
    pp1 = P.sb(es2, "pp1", [128, 512], F32)
    pp2 = P.sb(es2, "pp2", [128, 512], F32)
    btabs = P.sb(es2, "btabs", [8, 8], F32)
    btabr = P.sb(es2, "btabr", [1, 64], F32)

    def proj(dst, wsb, m):
        for gg in range(8):
            cpe = P.sec("t", lambda t, gg=gg: (
                t.wait("gc"), t.wait("dg"), t.wait("ve"),
                mm_chain(t, P.ps[0][0:m, 0:512],
                         [(wsb[:, cch, :], hn[:, cch, gg * 512:(gg + 1) * 512])
                          for cch in range(8)])))
            P.sec("v", lambda v, gg=gg: (
                v.wait("pe", cpe),
                v.tensor_copy(dst[:, gg * 512:(gg + 1) * 512], P.ps[0][0:m, 0:512])))
    proj(q1s, w_q1, 32)
    proj(k1s, w_k1, 32)
    proj(q2s, w_q2, 64)
    proj(k2s, w_k2, 64)

    sq1 = P.sb(es2, "sq1", [32, 4096], BF16)
    sq2 = P.sb(es2, "sq2", [64, 4096], BF16)

    def emit_pairnorm(srcA, srcB, slot):
        cs = P.sec("s", lambda s: (
            s.wait("ve"),
            s.activation(sq1[:, :], srcA, AF.Square),
            s.activation(sq2[:, :], srcB, AF.Square)))
        for gg in range(8):
            cpe = P.sec("t", lambda t, gg=gg: (
                t.wait("se", cs), t.wait("ve"),
                mm_chain(t, P.ps[1][0:8, 0:512],
                         [(eh1[:, :], sq1[:, gg * 512:(gg + 1) * 512])]),
                mm_chain(t, P.ps[2][0:8, 0:512],
                         [(eh2[:, :], sq2[:, gg * 512:(gg + 1) * 512])])))
            P.sec("v", lambda v, gg=gg: (
                v.wait("pe", cpe),
                v.tensor_copy(pp1[0:8, :], P.ps[1][0:8, 0:512]),
                v.edge(),
                v.tensor_mul(nrm[:, slot, gg * 512:(gg + 1) * 512],
                             pp1[0:8, :], P.ps[2][0:8, 0:512])))
    emit_pairnorm(q1s[:, :], q2s[:, :], 0)
    emit_pairnorm(k1s[:, :], k2s[:, :], 1)
    SCALE = float((4 * 8) ** (-0.5))

    def vbt(v):
        v.wait("se")
        v.reduce_max(st[0:8, 0:1], nrm[:, 1, :], axis=mybir.AxisListType.X)
        for gg in range(8):
            v.reduce_max(btabs[:, gg:gg + 1], nrm[:, 0, gg * 512:(gg + 1) * 512],
                         axis=mybir.AxisListType.X)
        v.edge()
        v.tensor_scalar(out=btabs[:, :], in0=btabs[:, :], scalar1=st[0:8, 0:1],
                        scalar2=None, op0=OP.mult)
    cbt = P.sec("v", vbt)
    guardc = P.ccol(1e-30, rows=8)
    lnsc = P.ccol(float(np.log(SCALE)), rows=8)
    cbt2 = P.sec("s", lambda s: (
        s.wait("ve", cbt), s.wait("gc"),
        s.activation(btabs[:, :], btabs[:, :], AF.Ln, bias=guardc),
        s.edge(),
        s.activation(btabs[:, :], btabs[:, :], AF.Exp, scale=0.5,
                     bias=lnsc),
        s.edge(),
        s.mul(btabs[:, :], btabs[:, :], -1.0)))
    P.sec("y", lambda y: (y.wait("se", cbt2),
                          y.dma_start(out=btab_dram.rearrange("o (g h) -> (o h) g", h=8),
                                      in_=btabs[:, :], allow_slow_non_contiguous=True)))
    cr = P.val("dy")
    P.sec("y", lambda y: (y.wait("dy", cr),
                          y.dma_start(out=btabr[:, :], in_=btab_dram)))
    onesr = P.sb(es2, "onesr", [1, 128], BF16)
    onescol = P.sb(es0, "onescol", [128, 1], BF16, side="right")
    P.sec("g", lambda g: (g.memset(onesr[:, :], 1.0), g.memset(onescol[:, :], 1.0)))
    btr_bf = P.sb(es2, "btr_bf", [1, 64], BF16)
    cbf = P.sec("v", lambda v: (v.wait("dy"), v.wait("gc"),
                                v.tensor_copy(btr_bf[:, :], btabr[:, :])))
    cpe = P.sec("t", lambda t: (
        t.wait("ve", cbf),
        mm_chain(t, P.ps[3][:, 0:64], [(onesr[:, :], btr_bf[:, :])])))
    P.sec("v", lambda v: (v.wait("pe", cpe),
                          v.tensor_copy(bbc.ap().rearrange("p g h -> p (g h)"),
                                        P.ps[3][:, 0:64])))

    for cc in range(2):
        for gg in range(8):
            def tfn(t, cc=cc, gg=gg):
                t.wait("ve")
                mm_chain(t, P.ps[0][:, :], [(e1[:, cc, :], q1s[:, gg * 512:(gg + 1) * 512])])
                mm_chain(t, P.ps[1][:, :], [(e2[:, cc, :], q2s[:, gg * 512:(gg + 1) * 512])])
                mm_chain(t, P.ps[2][:, :], [(e1[:, cc, :], k1s[:, gg * 512:(gg + 1) * 512])])
                mm_chain(t, P.ps[3][:, :], [(e2[:, cc, :], k2s[:, gg * 512:(gg + 1) * 512])])
            cpe = P.sec("t", tfn)

            def vfn(v, cc=cc, gg=gg):
                v.wait("pe", cpe)
                v.tensor_copy(pp1[:, :], P.ps[0][:, :])
                v.tensor_copy(pp2[:, :], P.ps[2][:, :])
                v.edge()
                v.tensor_mul(qt[:, cc, gg * 512:(gg + 1) * 512], pp1[:, :], P.ps[1][:, :])
                v.tensor_mul(kt[:, cc, gg * 512:(gg + 1) * 512], pp2[:, :], P.ps[3][:, :])
            P.sec("v", vfn)
    if debug:
        P.sec("g", lambda g: (g.wait_all(),
                              g.dma_start(out=dbg_qt, in_=qt.ap().rearrange("p a b -> p (a b)")),
                              g.dma_start(out=dbg_kt, in_=kt.ap().rearrange("p a b -> p (a b)")),
                              g.dma_start(out=dbg_nrm, in_=nrm.ap().rearrange("p a b -> p (a b)")),
                              g.dma_start(out=dbg_bt, in_=btabs[:, :])))
    es2.close()
    P.barrier()

    es3 = contextlib.ExitStack()
    vsb = P.sb(es3, "vsb", [128, 32, 512], BF16, side="right")
    for kc in range(NT):
        cpe = P.sec("t", lambda t, kc=kc: (
            t.wait("ve"),
            mm_chain(t, P.ps[kc % 2][:, :],
                     [(hn[:, cch, kc * 128:(kc + 1) * 128], w_v[:, cch, :])
                      for cch in range(8)])))
        P.sec("v", lambda v, kc=kc: (
            v.wait("pe", cpe),
            v.tensor_copy(vsb[:, kc, :], P.ps[kc % 2][:, :])))
    if debug:
        P.sec("g", lambda g: (g.wait_all(),
                              g.dma_start(out=dbg_hn, in_=hn.ap().rearrange("p a b -> p (a b)"))))
    es1.close()
    P.barrier()

    es_late = contextlib.ExitStack()
    av = P.sb(es_late, "av", [128, 4, 4096], BF16, side="right")
    es4 = contextlib.ExitStack()
    ptb = P.sb(es4, "ptb", [128, 4, 512], BF16)
    lsb = P.sb(es4, "lsb", [128, 512], F32)
    lbc = P.sb(es4, "lbc", [128, 2, 512], F32)
    for hq in range(2):
        for gg in range(8):
            nkc = 4 * gg + 4
            for kc in range(nkc):
                diag = kc - 4 * gg
                def sfn_t(t, hq=hq, gg=gg, kc=kc):
                    t.wait("se")
                    t.wait("ve")
                    for i in range(4):
                        t.matmul(P.ps[i][:, :],
                                 kt[32 * i:32 * (i + 1), hq, kc * 128:(kc + 1) * 128],
                                 qt[32 * i:32 * (i + 1), hq, gg * 512:(gg + 1) * 512],
                                 start=True, stop=True,
                                 tile_position=(32 * i, 0))
                cs_pe = P.sec("t", sfn_t)

                def efn(s, hq=hq, gg=gg, kc=kc):
                    s.wait("pe", cs_pe)
                    for i in range(4):
                        s.activation(ptb[:, i, :], P.ps[i][:, :], AF.Exp,
                                     bias=bbc[:, gg, hq * 4 + i:hq * 4 + i + 1], scale=SCALE)
                ce = P.sec("s", efn)
                cm = ce
                if diag >= 0:
                    def mfn(v, diag=diag):
                        v.wait("se", ce)
                        for i in range(4):
                            v.tensor_mul(ptb[:, i, :], ptb[:, i, :], dm[:, diag, :])
                    cm = P.sec("v", mfn)
                    dep = ("ve", cm)
                else:
                    dep = ("se", ce)

                def avfn(t, hq=hq, gg=gg, kc=kc, dep=dep, nkc=nkc):
                    t.wait(*dep)
                    t.wait("gc")
                    if kc == 0:
                        t.wait("ve")
                    st_ = (kc == 0)
                    sp_ = (kc == nkc - 1)
                    for pair in range(2):
                        for j in range(2):
                            hh = 2 * pair + j
                            t.matmul(P.ps[4 + pair][64 * j:64 * (j + 1), :],
                                     vsb[:, kc, (hq * 4 + hh) * 64:(hq * 4 + hh) * 64 + 64],
                                     ptb[:, hh, :],
                                     start=st_, stop=sp_,
                                     tile_position=(0, 64 * j))
                    for i in range(4):
                        t.matmul(P.ps[6][32 * i:32 * i + 1, :],
                                 onescol[:, :], ptb[:, i, :],
                                 start=st_, stop=sp_,
                                 tile_position=(0, 32 * i))
                P.sec("t", avfn)
            cev = P.sec("v", lambda v: (
                v.wait("pe"),
                v.tensor_copy(lsb[:, :], P.ps[6][:, :])))
            cbcp = P.sec("t", lambda t: (
                t.wait("ve", cev),
                mm_chain(t, P.ps[0][:, :], [(s4[:, 0, :], lsb[:, :])]),
                mm_chain(t, P.ps[1][:, :], [(s4[:, 1, :], lsb[:, :])])))

            crc = P.sec("v", lambda v: (
                v.wait("pe", cbcp),
                v.reciprocal(lbc[:, 0, :], P.ps[0][:, :]),
                v.reciprocal(lbc[:, 1, :], P.ps[1][:, :])))

            def vdiv(v, hq=hq, gg=gg):
                v.wait("ve", crc)
                for pair in range(2):
                    v.tensor_mul(av[:, hq * 2 + pair, gg * 512:(gg + 1) * 512],
                                 P.ps[4 + pair][:, :], lbc[:, pair, :])
            P.sec("v", vdiv)
    if debug:
        P.sec("g", lambda g: (g.wait_all(),
                              g.dma_start(out=dbg_av, in_=av.ap().rearrange("p a b -> p (a b)"))))
    es4.close()
    P.barrier()

    es5 = contextlib.ExitStack()
    w_o = P.sb(es5, "w_o", [128, 4, 1024], BF16)
    yt = P.sb(es5, "yt", [128, 2, 1024], F32)
    P.sec("g", lambda g: g.dma_start(out=w_o[:, :, :],
                                     in_=wot.rearrange("(c p) n -> p c n", p=128)))
    for tt in range(NT):
        buf = tt % 2
        def yfn(t, tt=tt):
            t.wait("dg")
            t.wait("ve")
            for j in range(2):
                mm_chain(t, P.ps[tt % 2 * 2 + j][:, :],
                         [(av[:, cch, tt * 128:(tt + 1) * 128],
                           w_o[:, cch, j * 512:(j + 1) * 512])
                          for cch in range(4)])
        cpe = P.sec("t", yfn)

        def evfn(v, tt=tt, buf=buf):
            v.wait("pe", cpe)
            v.wait("dy")
            v.tensor_copy(yt[:, buf, 0:512], P.ps[tt % 2 * 2][:, :])
            v.tensor_copy(yt[:, buf, 512:1024], P.ps[tt % 2 * 2 + 1][:, :])
        cv = P.sec("v", evfn)
        P.sec("y", lambda y, tt=tt, buf=buf: (
            y.wait("ve", cv),
            y.dma_start(out=yp[tt * 128:(tt + 1) * 128, :], in_=yt[:, buf, :])))
    P.sec("y", lambda y: y.wait_all())
    es5.close()
    es_late.close()
    es3.close()
    es0.close()
    return P.finish()



def build_stageA2():
    P = Prog("A2")
    x_own = P.inp("x_own", [2048, C])
    ypA = P.inp("ypA", [2048, C])
    ypB = P.inp("ypB", [2048, C])
    bo = P.inp("bo", [1, C])
    xo = P.outp("xo", [2048, C])

    es = contextlib.ExitStack()
    bob = P.sb(es, "bob", [128, 1024], F32)
    bor = P.sb(es, "bor", [1, 1024], F32)
    onesr = P.sb(es, "onesr", [1, 128], F32)
    xt = P.sb(es, "xt", [128, 2, 1024], F32)
    ya = P.sb(es, "ya", [128, 2, 1024], F32)
    yb = P.sb(es, "yb", [128, 2, 1024], F32)
    sq = P.sb(es, "sq", [128, 1024], F32)
    st = P.sb(es, "st", [128, 8], F32)
    scr = dict(sq=sq, st=st)

    P.sec("g", lambda g: (g.memset(onesr[:, :], 1.0)))
    P.sec("y", lambda y: y.dma_start(out=bor[:, :], in_=bo))
    cpe = P.sec("t", lambda t: (
        t.wait("gc"), t.wait("dy"),
        mm_chain(t, P.ps[0][:, :], [(onesr[:, :], bor[:, 0:512])]),
        mm_chain(t, P.ps[1][:, :], [(onesr[:, :], bor[:, 512:1024])])))
    P.sec("v", lambda v: (
        v.wait("pe", cpe),
        v.tensor_copy(bob[:, 0:512], P.ps[0][:, :]),
        v.tensor_copy(bob[:, 512:1024], P.ps[1][:, :])))

    for j in range(ONT):
        buf = j % 2
        P.sec("y", lambda y, j=j, buf=buf: (
            y.wait("ve"),
            y.dma_start(out=xt[:, buf, :], in_=x_own[j * 128:(j + 1) * 128, :]),
            y.dma_start(out=ya[:, buf, :], in_=ypA[j * 128:(j + 1) * 128, :]),
            y.dma_start(out=yb[:, buf, :], in_=ypB[j * 128:(j + 1) * 128, :])))
        cld = P.val("dy")

        def addf(v, buf=buf):
            v.wait("dy", cld)
            v.tensor_add(ya[:, buf, :], ya[:, buf, :], yb[:, buf, :])
            v.edge()
            v.tensor_add(ya[:, buf, :], ya[:, buf, :], bob[:, :])
        P.sec("v", addf)
        c = emit_norm(P, ya[:, buf, :], yb[:, buf, :], scr, eps=1e-5, kind="ln")

        def resid(v, buf=buf):
            v.wait("ve", c)
            v.tensor_add(xt[:, buf, :], xt[:, buf, :], yb[:, buf, :])
        cr = P.sec("v", resid)
        P.sec("y", lambda y, j=j, buf=buf: (
            y.wait("ve", cr),
            y.dma_start(out=xo[j * 128:(j + 1) * 128, :], in_=xt[:, buf, :])))
    P.sec("y", lambda y: y.wait_all())
    es.close()
    return P.finish()



def _consts_A():
    ident = np.eye(128, dtype=np.float32)
    e1 = np.zeros((32, 2, 128), np.float32)
    e2 = np.zeros((64, 2, 128), np.float32)
    for hl in range(8):
        cc, i = divmod(hl, 4)
        for r1 in range(4):
            for r2 in range(8):
                m = 32 * i + 8 * r1 + r2
                e1[4 * hl + r1, cc, m] = 1.0
                e2[8 * hl + r2, cc, m] = 1.0
    eh1 = np.zeros((32, 8), np.float32)
    eh2 = np.zeros((64, 8), np.float32)
    for hl in range(8):
        eh1[4 * hl:4 * hl + 4, hl] = 1.0
        eh2[8 * hl:8 * hl + 8, hl] = 1.0
    dm = np.zeros((128, 4, 512), np.float32)
    for di in range(4):
        d0 = -128 * di
        p = np.arange(128)[:, None]
        f = np.arange(512)[None, :]
        dm[:, di, :] = (p <= d0 + f).astype(np.float32)
    s4 = np.zeros((128, 2, 128), np.float32)
    s4[0, 0, 0:64] = 1.0
    s4[32, 0, 64:128] = 1.0
    s4[64, 1, 0:64] = 1.0
    s4[96, 1, 64:128] = 1.0
    return dict(identc=ident, e1c=e1, e2c=e2, eh1c=eh1, eh2c=eh2, dmc=dm, s4c=s4)


def run_stage_A(x, W):
    consts = _consts_A()
    ncA1 = build_stageA1()
    in_maps = []
    for core in range(8):
        b, half = core // 2, core % 2
        h0 = half * 8
        sl4 = slice(h0 * 4, (h0 + 8) * 4)
        sl8 = slice(h0 * 8, (h0 + 8) * 8)
        sl64 = slice(h0 * 64, (h0 + 8) * 64)
        m = dict(
            xb=np.ascontiguousarray(x[b]),
            wq1t=np.ascontiguousarray(W["Wq1"][sl4].T),
            wk1t=np.ascontiguousarray(W["Wk1"][sl4].T),
            wq2t=np.ascontiguousarray(W["Wq2"][sl8].T),
            wk2t=np.ascontiguousarray(W["Wk2"][sl8].T),
            wvt=np.ascontiguousarray(W["Wv"][sl64].T),
            wot=np.ascontiguousarray(W["Wo"].T[sl64]),
            **consts,
        )
        in_maps.append(m)
    res = run_bass_kernel_spmd(ncA1, in_maps, core_ids=list(range(8)))
    yps = [r["yp"] for r in res.results]

    ncA2 = build_stageA2()
    in_maps2 = []
    for core in range(8):
        b, half = core // 2, core % 2
        tok = slice(half * 2048, (half + 1) * 2048)
        in_maps2.append(dict(
            x_own=np.ascontiguousarray(x[b, tok]),
            ypA=np.ascontiguousarray(yps[2 * b][tok]),
            ypB=np.ascontiguousarray(yps[2 * b + 1][tok]),
            bo=W["bo"].reshape(1, C),
        ))
    res2 = run_bass_kernel_spmd(ncA2, in_maps2, core_ids=list(range(8)))
    x1 = np.empty_like(x)
    for core in range(8):
        b, half = core // 2, core % 2
        x1[b, half * 2048:(half + 1) * 2048] = res2.results[core]["xo"]
    return x1


def kernel(**inputs):
    inputs = {k: np.asarray(v, np.float32) for k, v in inputs.items()}
    x = inputs["L0_attn_x"]
    WA = {k.replace("L0_attn_", ""): inputs[k] for k in inputs
          if k.startswith("L0_attn_") and k != "L0_attn_x"}
    WB = {k.replace("L0_ffn_", ""): inputs[k] for k in inputs
          if k.startswith("L0_ffn_")}
    WD = {k.replace("L1_attn_", ""): inputs[k] for k in inputs
          if k.startswith("L1_attn_")}
    WE = {k.replace("L1_ffn_", ""): inputs[k] for k in inputs
          if k.startswith("L1_ffn_")}
    if True:
        return kernel_fused(inputs)
    x1 = run_stage_A(x, WA)
    a = run_stage_B(x1, WB)
    x2 = run_stage_C(a, x1, WB)
    x3 = run_stage_D(x2, WD)
    x4 = run_stage_E(x3, WE)
    return x4




def emit_pair_attention(P, *, k_all, q_all, vsb, av, bbc, dm, s4, onescol,
                        scale, causal, es4):
    ptb = P.sb(es4, "pa_ptb", [128, 2, 512], BF16)
    lsb = P.sb(es4, "pa_lsb", [128, 512], F32)
    lbc = P.sb(es4, "pa_lbc", [128, 512], F32)
    for p in range(4):
        for gg in range(8):
            nkc = 4 * gg + 4 if causal else 32
            for kc in range(nkc):
                diag = kc - 4 * gg if causal else -1
                def sfn(t, p=p, gg=gg, kc=kc):
                    t.wait("se"); t.wait("ve"); t.wait("gc"); t.wait("dg")
                    for j in range(2):
                        t.matmul(P.ps[j][:, :],
                                 k_all[64 * j:64 * (j + 1), p, kc * 128:(kc + 1) * 128],
                                 q_all[64 * j:64 * (j + 1), p, gg * 512:(gg + 1) * 512],
                                 start=True, stop=True, tile_position=(64 * j, 0))
                cs_pe = P.sec("t", sfn)

                def efn(s, p=p, gg=gg):
                    s.wait("pe", cs_pe)
                    for j in range(2):
                        s.activation(ptb[:, j, :], P.ps[j][:, :], AF.Exp,
                                     bias=bbc[:, j, p, gg:gg + 1],
                                     scale=scale)
                ce = P.sec("s", efn)
                if diag >= 0:
                    def mfn(v, diag=diag):
                        v.wait("se", ce)
                        for j in range(2):
                            v.tensor_mul(ptb[:, j, :], ptb[:, j, :], dm[:, diag, :])
                    cm = P.sec("v", mfn)
                    dep = ("ve", cm)
                else:
                    dep = ("se", ce)

                def avfn(t, p=p, kc=kc, dep=dep, nkc=nkc):
                    t.wait(*dep)
                    if kc == 0:
                        t.wait("ve")
                    st_, sp_ = (kc == 0), (kc == nkc - 1)
                    for j in range(2):
                        t.matmul(P.ps[4][64 * j:64 * (j + 1), :],
                                 vsb[:, kc, (2 * p + j) * 64:(2 * p + j) * 64 + 64],
                                 ptb[:, j, :], start=st_, stop=sp_,
                                 tile_position=(0, 64 * j))
                    for j in range(2):
                        t.matmul(P.ps[6][32 * j:32 * j + 1, :], onescol[:, :],
                                 ptb[:, j, :], start=st_, stop=sp_,
                                 tile_position=(0, 32 * j))
                P.sec("t", avfn)
            cev = P.sec("v", lambda v: (v.wait("pe"),
                                        v.tensor_copy(lsb[:, :], P.ps[6][:, :])))
            cbcp = P.sec("t", lambda t: (
                t.wait("ve", cev),
                mm_chain(t, P.ps[0][:, :], [(s4[:, 0, :], lsb[:, :])])))
            crc = P.sec("v", lambda v: (
                v.wait("pe", cbcp),
                v.reciprocal(lbc[:, :], P.ps[0][:, :])))

            def vdiv(v, p=p, gg=gg):
                v.wait("ve", crc)
                v.tensor_mul(av[:, p, gg * 512:(gg + 1) * 512],
                             P.ps[4][:, :], lbc[:, :])
            P.sec("v", vdiv)


def emit_pair_norms(P, src_all, dst, eh64, sqq):
    for p in range(4):
        cs = P.sec("s", lambda s, p=p: (
            s.wait("ve"), s.wait("pe"), s.wait("dg"),
            s.activation(sqq[:, :], src_all[:, p, :], AF.Square)))
        for gg in range(8):
            cpe = P.sec("t", lambda t, gg=gg: (
                t.wait("se", cs), t.wait("ve"), t.wait("gc"),
                mm_chain(t, P.ps[2][0:2, 0:512],
                         [(eh64[:, :], sqq[:, gg * 512:(gg + 1) * 512])])))
            P.sec("v", lambda v, p=p, gg=gg: (
                v.wait("pe", cpe),
                v.tensor_copy(dst[32 * p:32 * p + 2, gg * 512:(gg + 1) * 512],
                              P.ps[2][0:2, 0:512])))


def emit_btab2(P, nq, nk, bbc, onesr, lnscale_val, es):
    btabs = P.sb(es, "bt_btabs", [2, 4, 8], F32)
    mk = P.sb(es, "bt_mk", [2, 4], F32)
    btabr = P.sb(es, "bt_btabr", [1, 64], F32)
    btab_dram = P.dram(f"bt_dram{P.val('se')}", [64])

    def vbt(v):
        v.wait("se"); v.wait("ve"); v.wait("pe")
        for p in range(4):
            v.reduce_max(mk[0:2, p:p + 1], nk[32 * p:32 * p + 2, :],
                         axis=mybir.AxisListType.X)
            for gg in range(8):
                v.reduce_max(btabs[0:2, p, gg:gg + 1],
                             nq[32 * p:32 * p + 2, gg * 512:(gg + 1) * 512],
                             axis=mybir.AxisListType.X)
        v.edge()
        for p in range(4):
            v.tensor_scalar(out=btabs[0:2, p, :], in0=btabs[0:2, p, :],
                            scalar1=mk[0:2, p:p + 1], scalar2=None, op0=OP.mult)
    cbt = P.sec("v", vbt)
    guardc = P.ccol(1e-30, rows=2)
    lnsc = P.ccol(lnscale_val, rows=2)
    cbt2 = P.sec("s", lambda s: (
        s.wait("ve", cbt), s.wait("gc"),
        s.activation(btabs[:, :, :], btabs[:, :, :], AF.Ln, bias=guardc),
        s.edge(),
        s.activation(btabs[:, :, :], btabs[:, :, :], AF.Exp, scale=0.5, bias=lnsc),
        s.edge(),
        s.mul(btabs[:, :, :], btabs[:, :, :], -1.0)))
    P.sec("y", lambda y: (y.wait("se", cbt2),
                          y.dma_start(out=btab_dram.rearrange("(j p g) -> j p g", j=2, p=4),
                                      in_=btabs[:, :, :])))
    cr = P.val("dy")
    P.sec("y", lambda y: (y.wait("dy", cr),
                          y.dma_start(out=btabr[:, :],
                                      in_=btab_dram.rearrange("(o f) -> o f", o=1))))
    cl = P.val("dy")
    cpe = P.sec("t", lambda t: (
        t.wait("dy", cl), t.wait("gc"), t.wait("ve"),
        mm_chain(t, P.ps[3][:, 0:64], [(onesr[:, :], btabr[:, :])])))
    P.sec("v", lambda v: (v.wait("pe", cpe),
                          v.tensor_copy(bbc.ap().rearrange("p j q g -> p (j q g)"),
                                        P.ps[3][:, 0:64])))


def emit_row_bcast(P, row_in, dst, width, onesr, es):
    tmp = P.sb(es, f"rb{P.val('dy')}", [1, width], F32)
    P.sec("y", lambda y: y.dma_start(out=tmp[:, :], in_=row_in))
    cl = P.val("dy")
    nj = (width + 511) // 512
    cpe = P.sec("t", lambda t: (
        t.wait("dy", cl), t.wait("gc"), t.wait("ve"), t.wait("se"),
        [mm_chain(t, P.ps[j][:, 0:min(512, width - 512 * j)],
                  [(onesr[:, :], tmp[:, 512 * j:min(512 * (j + 1), width)])])
         for j in range(nj)]))
    P.sec("v", lambda v: (
        v.wait("pe", cpe),
        [v.tensor_copy(dst[:, 512 * j:min(512 * (j + 1), width)],
                       P.ps[j][:, 0:min(512, width - 512 * j)]) for j in range(nj)]))



def build_stageB1():
    P = Prog("B1")
    xb = P.inp("xb", [N, C])
    ln1w = P.inp("ln1w", [1, C])
    wq = P.inp("wq", [C, 512])
    wqsw = P.inp("wqsw", [C, 512])
    wkv = P.inp("wkv", [C, 256])
    wkup = P.inp("wkup", [256, 512])
    wkupsw = P.inp("wkupsw", [256, 512])
    wvup = P.inp("wvup", [256, 512])
    wot = P.inp("wot", [512, C])
    identc = P.inp("identc", [128, 128])
    dmc = P.inp("dmc", [128, 4, 512])
    s4c = P.inp("s4c", [128, 2, 128])
    eh64c = P.inp("eh64c", [128, 2])
    crepc = P.inp("crepc", [128, N])
    srepc = P.inp("srepc", [128, N])
    yp = P.outp("yp", [N, C])

    es0 = contextlib.ExitStack()
    ident = P.sb(es0, "ident", [128, 128], BF16)
    dm = P.sb(es0, "dm", [128, 4, 512], BF16)
    s4 = P.sb(es0, "s4", [128, 2, 128], F32)
    eh64 = P.sb(es0, "eh64", [128, 2], BF16)
    onesr = P.sb(es0, "onesr", [1, 128], F32)
    onescol = P.sb(es0, "onescol", [128, 1], BF16)
    w_q = P.sb(es0, "w_q", [128, 8, 512], BF16)
    w_qsw = P.sb(es0, "w_qsw", [128, 8, 512], BF16)
    w_kv = P.sb(es0, "w_kv", [128, 8, 256], BF16)
    ln1bc = P.sb(es0, "ln1bc", [128, 1024], F32)
    kv = P.sb(es0, "kv", [128, 2, 4096], BF16)
    bbc = P.sb(es0, "bbc", [128, 2, 4, 8], F32)
    rs2 = P.sb(es0, "rs2", [128, 512], F32)
    bufA = P.sb(es0, "bufA", [128, 4, 4096], BF16, side="right")
    bufB = P.sb(es0, "bufB", [128, 4, 4096], BF16, side="right")
    q_all = bufA
    k_all = bufB

    def loads(g):
        g.dma_start(out=ident[:, :], in_=identc)
        g.dma_start(out=dm[:, :, :], in_=dmc)
        g.dma_start(out=s4[:, :, :], in_=s4c)
        g.dma_start(out=eh64[:, :], in_=eh64c)
        g.dma_start(out=w_q[:, :, :], in_=wq.rearrange("(c p) n -> p c n", p=128))
        g.dma_start(out=w_qsw[:, :, :], in_=wqsw.rearrange("(c p) n -> p c n", p=128))
        g.dma_start(out=w_kv[:, :, :], in_=wkv.rearrange("(c p) n -> p c n", p=128))
        g.memset(onesr[:, :], 1.0)
        g.memset(onescol[:, :], 1.0)
    P.sec("g", loads)
    emit_row_bcast(P, ln1w, ln1bc, 1024, onesr, es0)

    es1 = contextlib.ExitStack()
    rms1 = P.sb(es1, "rms1", [128, 8, 4096], BF16)
    emit_ln_to_fm(P, xb, rms1, es1, ident, rms_eps=1e-6, gain_bc=ln1bc[:, :])

    def rawproj(dstA, dstB, wA, wB, src_fm, nchunk):
        for p in range(4):
            for gg in range(8):
                cpe = P.sec("t", lambda t, p=p, gg=gg: (
                    t.wait("ve"), t.wait("dg"), t.wait("gc"),
                    mm_chain(t, P.ps[0][:, :],
                             [(wA[:, cch, 128 * p:128 * (p + 1)],
                               src_fm[:, cch, gg * 512:(gg + 1) * 512])
                              for cch in range(nchunk)]),
                    mm_chain(t, P.ps[1][:, :],
                             [(wB[:, cch, 128 * p:128 * (p + 1)],
                               src_fm[:, cch, gg * 512:(gg + 1) * 512])
                              for cch in range(nchunk)])))
                P.sec("v", lambda v, p=p, gg=gg: (
                    v.wait("pe", cpe),
                    v.tensor_copy(dstA[:, p, gg * 512:(gg + 1) * 512], P.ps[0][:, :]),
                    v.tensor_copy(dstB[:, p, gg * 512:(gg + 1) * 512], P.ps[1][:, :])))
    rawproj(bufA, bufB, w_q, w_qsw, rms1, 8)

    for gg in range(8):
        cpe = P.sec("t", lambda t, gg=gg: (
            t.wait("ve"),
            mm_chain(t, P.ps[0][:, :],
                     [(w_kv[:, cch, 0:128], rms1[:, cch, gg * 512:(gg + 1) * 512])
                      for cch in range(8)]),
            mm_chain(t, P.ps[1][:, :],
                     [(w_kv[:, cch, 128:256], rms1[:, cch, gg * 512:(gg + 1) * 512])
                      for cch in range(8)])))
        P.sec("v", lambda v, gg=gg: (
            v.wait("pe", cpe),
            v.tensor_scalar(out=kv[:, 0, gg * 512:(gg + 1) * 512], in0=P.ps[0][:, :],
                            scalar1=0.25, scalar2=None, op0=OP.mult),
            v.tensor_scalar(out=kv[:, 1, gg * 512:(gg + 1) * 512], in0=P.ps[1][:, :],
                            scalar1=0.25, scalar2=None, op0=OP.mult)))
    es1.close()
    P.barrier()

    estr = contextlib.ExitStack()
    crep = P.sb(estr, "crep", [128, N], F32)
    srep = P.sb(estr, "srep", [128, N], F32)
    w_kup = P.sb(estr, "w_kup", [128, 2, 512], BF16)
    w_kupsw = P.sb(estr, "w_kupsw", [128, 2, 512], BF16)
    w_vup = P.sb(estr, "w_vup", [128, 2, 512], BF16)
    P.sec("g", lambda g: (
        g.dma_start(out=crep[:, :], in_=crepc),
        g.dma_start(out=srep[:, :], in_=srepc),
        g.dma_start(out=w_kup[:, :, :], in_=wkup.rearrange("(c p) n -> p c n", p=128)),
        g.dma_start(out=w_kupsw[:, :, :], in_=wkupsw.rearrange("(c p) n -> p c n", p=128)),
        g.dma_start(out=w_vup[:, :, :], in_=wvup.rearrange("(c p) n -> p c n", p=128))))

    def rope_inplace(dst, sw):
        for p in range(4):
            for gg in range(8):
                def vr(v, p=p, gg=gg):
                    v.wait("dg")
                    v.wait("pe")
                    sl = (slice(None), p, slice(gg * 512, (gg + 1) * 512))
                    v.tensor_mul(rs2[:, :], sw[sl], srep[:, gg * 512:(gg + 1) * 512])
                    v.edge()
                    v.tensor_mul(dst[sl], dst[sl], crep[:, gg * 512:(gg + 1) * 512])
                    v.edge()
                    v.tensor_add(dst[sl], dst[sl], rs2[:, :])
                P.sec("v", vr)
    rope_inplace(bufA, bufB)

    for p in range(4):
        for gg in range(8):
            cpe = P.sec("t", lambda t, p=p, gg=gg: (
                t.wait("ve"), t.wait("dg"), t.wait("gc"),
                mm_chain(t, P.ps[0][:, :],
                         [(w_kup[:, lc, 128 * p:128 * (p + 1)],
                           kv[:, lc, gg * 512:(gg + 1) * 512]) for lc in range(2)]),
                mm_chain(t, P.ps[1][:, :],
                         [(w_kupsw[:, lc, 128 * p:128 * (p + 1)],
                           kv[:, lc, gg * 512:(gg + 1) * 512]) for lc in range(2)])))

            def vr(v, p=p, gg=gg):
                v.wait("pe", cpe)
                v.tensor_mul(rs2[:, :], P.ps[1][:, :], srep[:, gg * 512:(gg + 1) * 512])
                v.tensor_mul(k_all[:, p, gg * 512:(gg + 1) * 512], P.ps[0][:, :],
                             crep[:, gg * 512:(gg + 1) * 512])
                v.edge()
                v.tensor_add(k_all[:, p, gg * 512:(gg + 1) * 512],
                             k_all[:, p, gg * 512:(gg + 1) * 512], rs2[:, :])
            P.sec("v", vr)

    bufC = P.sb(es0, "bufC", [128, 32, 512], BF16, side="right")
    vsb = bufC
    for kc in range(NT):
        cpe = P.sec("t", lambda t, kc=kc: (
            t.wait("ve"),
            mm_chain(t, P.ps[kc % 2][:, :],
                     [(kv[:, lc, kc * 128:(kc + 1) * 128], w_vup[:, lc, :])
                      for lc in range(2)])))
        P.sec("v", lambda v, kc=kc: (
            v.wait("pe", cpe),
            v.tensor_copy(vsb[:, kc, :], P.ps[kc % 2][:, :])))
    estr.close()
    P.barrier()

    es3 = contextlib.ExitStack()
    nq = P.sb(es3, "nq", [128, 4096], BF16)
    nk = P.sb(es3, "nk", [128, 4096], BF16)
    sqq = P.sb(es3, "sqq", [128, 4096], BF16)
    emit_pair_norms(P, q_all, nq[:, :], eh64, sqq)
    emit_pair_norms(P, k_all, nk[:, :], eh64, sqq)
    emit_btab2(P, nq[:, :], nk[:, :], bbc, onesr, float(np.log(0.125)), es3)
    es3.close()
    P.barrier()

    es4 = contextlib.ExitStack()
    av = P.sb(es4, "av", [128, 4, 4096], BF16, side="right")
    emit_pair_attention(P, k_all=k_all, q_all=q_all, vsb=vsb, av=av,
                        bbc=bbc, dm=dm, s4=s4, onescol=onescol,
                        scale=0.125, causal=True, es4=es4)
    P.barrier()

    es5 = contextlib.ExitStack()
    emit_partial_y(P, av, wot, yp, es5)
    P.sec("y", lambda y: y.wait_all())
    es5.close()
    es4.close()
    es0.close()
    return P.finish()



def build_stageB2():
    P = Prog("B2")
    x_own = P.inp("x_own", [2048, C])
    ypA = P.inp("ypA", [2048, C])
    ypB = P.inp("ypB", [2048, C])
    ao = P.outp("ao", [2048, C])

    es = contextlib.ExitStack()
    xt = P.sb(es, "xt", [128, 2, 1024], F32)
    ya = P.sb(es, "ya", [128, 2, 1024], F32)
    yb = P.sb(es, "yb", [128, 2, 1024], F32)
    ht = P.sb(es, "ht", [128, 1024], F32)
    sq = P.sb(es, "sq", [128, 1024], F32)
    st = P.sb(es, "st", [128, 8], F32)
    scr = dict(sq=sq, st=st)
    for j in range(ONT):
        buf = j % 2
        P.sec("y", lambda y, j=j, buf=buf: (
            y.wait("ve"), y.wait("se"),
            y.dma_start(out=xt[:, buf, :], in_=x_own[j * 128:(j + 1) * 128, :]),
            y.dma_start(out=ya[:, buf, :], in_=ypA[j * 128:(j + 1) * 128, :]),
            y.dma_start(out=yb[:, buf, :], in_=ypB[j * 128:(j + 1) * 128, :])))
        cld = P.val("dy")
        c = emit_norm(P, xt[:, buf, :], ht[:, :], scr, eps=1e-5, kind="ln")

        def addf(v, buf=buf):
            v.wait("dy", cld)
            v.wait("ve", c)
            v.tensor_add(ya[:, buf, :], ya[:, buf, :], yb[:, buf, :])
            v.edge()
            v.tensor_add(ya[:, buf, :], ya[:, buf, :], ht[:, :])
        cr = P.sec("v", addf)
        P.sec("y", lambda y, j=j, buf=buf: (
            y.wait("ve", cr),
            y.dma_start(out=ao[j * 128:(j + 1) * 128, :], in_=ya[:, buf, :])))
    P.sec("y", lambda y: y.wait_all())
    es.close()
    return P.finish()



def build_stageD1():
    P = Prog("D1")
    xb = P.inp("xb", [N, C])
    wq = P.inp("wq", [C, 512])
    wk = P.inp("wk", [C, 512])
    wv = P.inp("wv", [C, 512])
    wot = P.inp("wot", [512, C])
    identc = P.inp("identc", [128, 128])
    s4c = P.inp("s4c", [128, 2, 128])
    eh64c = P.inp("eh64c", [128, 2])
    yp = P.outp("yp", [N, C])

    es0 = contextlib.ExitStack()
    ident = P.sb(es0, "ident", [128, 128], BF16)
    s4 = P.sb(es0, "s4", [128, 2, 128], F32)
    eh64 = P.sb(es0, "eh64", [128, 2], BF16)
    onesr = P.sb(es0, "onesr", [1, 128], F32)
    onescol = P.sb(es0, "onescol", [128, 1], BF16)
    w_q = P.sb(es0, "w_q", [128, 8, 512], BF16)
    w_k = P.sb(es0, "w_k", [128, 8, 512], BF16)
    w_v = P.sb(es0, "w_v", [128, 8, 512], BF16)
    bbc = P.sb(es0, "bbc", [128, 2, 4, 8], F32)
    q_all = P.sb(es0, "q_all", [128, 4, 4096], BF16, side="right")
    k_all = P.sb(es0, "k_all", [128, 4, 4096], BF16, side="right")
    vsb = P.sb(es0, "vsb", [128, 32, 512], BF16, side="right")

    def loads(g):
        g.dma_start(out=ident[:, :], in_=identc)
        g.dma_start(out=s4[:, :, :], in_=s4c)
        g.dma_start(out=eh64[:, :], in_=eh64c)
        g.dma_start(out=w_q[:, :, :], in_=wq.rearrange("(c p) n -> p c n", p=128))
        g.dma_start(out=w_k[:, :, :], in_=wk.rearrange("(c p) n -> p c n", p=128))
        g.dma_start(out=w_v[:, :, :], in_=wv.rearrange("(c p) n -> p c n", p=128))
        g.memset(onesr[:, :], 1.0)
        g.memset(onescol[:, :], 1.0)
    P.sec("g", loads)

    es1 = contextlib.ExitStack()
    hn = P.sb(es1, "hn", [128, 8, 4096], BF16)
    emit_ln_to_fm(P, xb, hn, es1, ident)

    def proj_pair(dst_all, wsb):
        for p in range(4):
            for gg in range(8):
                cpe = P.sec("t", lambda t, p=p, gg=gg: (
                    t.wait("ve"), t.wait("dg"), t.wait("gc"),
                    mm_chain(t, P.ps[0][:, :],
                             [(wsb[:, cch, 128 * p:128 * (p + 1)],
                               hn[:, cch, gg * 512:(gg + 1) * 512])
                              for cch in range(8)])))
                P.sec("v", lambda v, p=p, gg=gg: (
                    v.wait("pe", cpe),
                    v.tensor_copy(dst_all[:, p, gg * 512:(gg + 1) * 512],
                                  P.ps[0][:, :])))
    proj_pair(q_all, w_q)
    proj_pair(k_all, w_k)
    for kc in range(NT):
        cpe = P.sec("t", lambda t, kc=kc: (
            t.wait("ve"),
            mm_chain(t, P.ps[kc % 2][:, :],
                     [(hn[:, cch, kc * 128:(kc + 1) * 128], w_v[:, cch, :])
                      for cch in range(8)])))
        P.sec("v", lambda v, kc=kc: (
            v.wait("pe", cpe),
            v.tensor_copy(vsb[:, kc, :], P.ps[kc % 2][:, :])))
    es1.close()
    P.barrier()

    nq = P.sb(es0, "nq", [128, 4096], BF16, side="right")
    nk = P.sb(es0, "nk", [128, 4096], BF16, side="right")
    es3 = contextlib.ExitStack()
    sqq = P.sb(es3, "sqq", [128, 4096], BF16)
    emit_pair_norms(P, q_all, nq[:, :], eh64, sqq)
    emit_pair_norms(P, k_all, nk[:, :], eh64, sqq)
    emit_btab2(P, nq[:, :], nk[:, :], bbc, onesr, float(np.log(0.125)), es3)
    es3.close()
    P.barrier()

    es4 = contextlib.ExitStack()
    av = P.sb(es4, "av", [128, 4, 4096], BF16, side="right")
    emit_pair_attention(P, k_all=k_all, q_all=q_all, vsb=vsb, av=av,
                        bbc=bbc, dm=None, s4=s4, onescol=onescol,
                        scale=0.125, causal=False, es4=es4)
    P.barrier()

    es5 = contextlib.ExitStack()
    emit_partial_y(P, av, wot, yp, es5)
    P.sec("y", lambda y: y.wait_all())
    es5.close()
    es4.close()
    es0.close()
    return P.finish()


def _consts_BD():
    ident = np.eye(128, dtype=np.float32)
    dm = np.zeros((128, 4, 512), np.float32)
    for di in range(4):
        d0 = -128 * di
        p = np.arange(128)[:, None]
        f = np.arange(512)[None, :]
        dm[:, di, :] = (p <= d0 + f).astype(np.float32)
    s4 = np.zeros((128, 2, 128), np.float32)
    s4[0, 0, 0:64] = 1.0
    s4[32, 0, 64:128] = 1.0
    s4[64, 1, 0:64] = 1.0
    s4[96, 1, 64:128] = 1.0
    eh64 = np.zeros((128, 2), np.float32)
    eh64[0:64, 0] = 1.0
    eh64[64:128, 1] = 1.0
    return dict(identc=ident, dmc=dm, s4c=s4, eh64c=eh64)


def _rope_tables():
    DD = 64
    inv = 1.0 / (100000.0 ** (np.arange(0, DD, 2, dtype=np.float64) / DD))
    f = np.arange(N, dtype=np.float64)[:, None] * inv[None, :]
    c = np.cos(f).astype(np.float32).T
    s = np.sin(f).astype(np.float32).T
    crep = np.tile(c, (4, 1))
    srep = np.empty((128, N), np.float32)
    for j in range(2):
        srep[64 * j:64 * j + 32] = -s
        srep[64 * j + 32:64 * j + 64] = s
    return crep, srep


def run_stage_B(x1, W):
    consts = _consts_BD()
    crep, srep = _rope_tables()
    ncB1 = build_stageB1()
    in_maps = []
    for core in range(8):
        b, half = core // 2, core % 2
        h0 = half * 8
        wq = W["Wq"][:, h0 * 64:(h0 + 8) * 64]
        wqsw = wq.reshape(1024, 8, 2, 32)[:, :, ::-1, :].reshape(1024, 512)
        wkup = W["Wk_up"][h0:h0 + 8].transpose(1, 0, 2).reshape(256, 512)
        wkupsw = wkup.reshape(256, 8, 2, 32)[:, :, ::-1, :].reshape(256, 512)
        wvup = W["Wv_up"][h0:h0 + 8].transpose(1, 0, 2).reshape(256, 512)
        in_maps.append(dict(
            xb=np.ascontiguousarray(x1[b]),
            ln1w=W["ln1_w"].reshape(1, C),
            wq=np.ascontiguousarray(wq), wqsw=np.ascontiguousarray(wqsw),
            wkv=np.ascontiguousarray(W["Wkv"]),
            wkup=np.ascontiguousarray(wkup), wkupsw=np.ascontiguousarray(wkupsw),
            wvup=np.ascontiguousarray(wvup),
            wot=np.ascontiguousarray(W["Wo"][h0 * 64:(h0 + 8) * 64]),
            crepc=crep, srepc=srep, **consts))
    res = run_bass_kernel_spmd(ncB1, in_maps, core_ids=list(range(8)))
    yps = [r["yp"] for r in res.results]

    ncB2 = build_stageB2()
    in_maps2 = []
    for core in range(8):
        b, half = core // 2, core % 2
        tok = slice(half * 2048, (half + 1) * 2048)
        in_maps2.append(dict(
            x_own=np.ascontiguousarray(x1[b, tok]),
            ypA=np.ascontiguousarray(yps[2 * b][tok]),
            ypB=np.ascontiguousarray(yps[2 * b + 1][tok])))
    res2 = run_bass_kernel_spmd(ncB2, in_maps2, core_ids=list(range(8)))
    a = np.empty_like(x1)
    for core in range(8):
        b, half = core // 2, core % 2
        a[b, half * 2048:(half + 1) * 2048] = res2.results[core]["ao"]
    return a


def run_stage_D(x2, W):
    consts = _consts_BD()
    ncD1 = build_stageD1()
    in_maps = []
    for core in range(8):
        b, half = core // 2, core % 2
        h0 = half * 8
        sl = slice(h0 * 64, (h0 + 8) * 64)
        in_maps.append(dict(
            xb=np.ascontiguousarray(x2[b]),
            wq=np.ascontiguousarray(W["Wq"][:, sl]),
            wk=np.ascontiguousarray(W["Wkv"][:, 0:1024][:, sl]),
            wv=np.ascontiguousarray(W["Wkv"][:, 1024:2048][:, sl]),
            wot=np.ascontiguousarray(W["Wo"][sl]),
            **consts))
    res = run_bass_kernel_spmd(ncD1, in_maps, core_ids=list(range(8)))
    yps = [r["yp"] for r in res.results]

    ncA2 = build_stageA2()
    in_maps2 = []
    for core in range(8):
        b, half = core // 2, core % 2
        tok = slice(half * 2048, (half + 1) * 2048)
        in_maps2.append(dict(
            x_own=np.ascontiguousarray(x2[b, tok]),
            ypA=np.ascontiguousarray(yps[2 * b][tok]),
            ypB=np.ascontiguousarray(yps[2 * b + 1][tok]),
            bo=W["bo"].reshape(1, C)))
    res2 = run_bass_kernel_spmd(ncA2, in_maps2, core_ids=list(range(8)))
    x3 = np.empty_like(x2)
    for core in range(8):
        b, half = core // 2, core % 2
        x3[b, half * 2048:(half + 1) * 2048] = res2.results[core]["xo"]
    return x3



def emit_expert(P, *, e_idx, kind, rms2, h1, w1s, w3s, w2all, sil, acc, gam_ap,
                w1_d, w3_d, w2_d, b1col_fn=None):
    P.sec("g", lambda g: (
        g.wait("pe"),
        g.dma_start(out=w2all[:, :, :], in_=w2_d.rearrange("(m p) n -> p m n", p=128))))
    for half in range(2):
        tok0 = half * 1024
        for m in range(16):
            par = m % 2
            def lw(g, m=m, par=par):
                g.wait("pe")
                g.dma_start(out=w1s[:, par, :, :],
                            in_=w1_d[:, m * 128:(m + 1) * 128]
                            .rearrange("(c p) n -> p c n", p=128))
                if kind == "swiglu":
                    g.dma_start(out=w3s[:, par, :, :],
                                in_=w3_d[:, m * 128:(m + 1) * 128]
                                .rearrange("(c p) n -> p c n", p=128))
            P.sec("g", lw)
            cld = P.val("dg")
            for g2 in range(2):
                def up(t, m=m, par=par, g2=g2, cld=cld, tok0=tok0):
                    t.wait("dg", cld)
                    t.wait("se")
                    t.wait("ve")
                    cols = slice(tok0 + g2 * 512, tok0 + (g2 + 1) * 512)
                    mm_chain(t, P.ps[g2][:, :],
                             [(w1s[:, par, cch, :], rms2[:, cch, cols])
                              for cch in range(8)])
                    if kind == "swiglu":
                        mm_chain(t, P.ps[2 + g2][:, :],
                                 [(w3s[:, par, cch, :], rms2[:, cch, cols])
                                  for cch in range(8)])
                cpe = P.sec("t", up)
                if kind == "swiglu":
                    cs = P.sec("s", lambda s, g2=g2: (
                        s.wait("pe", cpe),
                        s.activation(sil[:, g2, :], P.ps[g2][:, :], AF.Silu)))
                    P.sec("v", lambda v, m=m, g2=g2: (
                        v.wait("se", cs),
                        v.tensor_mul(h1[:, m, g2 * 512:(g2 + 1) * 512],
                                     sil[:, g2, :], P.ps[2 + g2][:, :])))
                else:
                    P.sec("s", lambda s, m=m, g2=g2: (
                        s.wait("pe", cpe),
                        s.activation(h1[:, m, g2 * 512:(g2 + 1) * 512],
                                     P.ps[g2][:, :], AF.Gelu,
                                     bias=b1col_fn(m))))
        for th in range(8):
            tt = half * 8 + th
            def down(t, th=th):
                t.wait("se"); t.wait("ve"); t.wait("dg")
                for j in range(2):
                    mm_chain(t, P.ps[4 + j][:, :],
                             [(h1[:, m2, th * 128:(th + 1) * 128],
                               w2all[:, m2, j * 512:(j + 1) * 512])
                              for m2 in range(16)])
            cpe = P.sec("t", down)

            def accf(v, tt=tt):
                v.wait("pe", cpe)
                for j in range(2):
                    sl = acc[:, tt, j * 512:(j + 1) * 512]
                    if gam_ap is not None:
                        v.scalar_tensor_tensor(out=sl, in0=P.ps[4 + j][:, :],
                                               scalar=gam_ap(tt), in1=sl,
                                               op0=OP.mult, op1=OP.add)
                    else:
                        v.tensor_add(sl, P.ps[4 + j][:, :], sl)
            P.sec("v", accf)



def build_stageC():
    P = Prog("C")
    a_own = P.inp("a_own", [2048, C])
    x1_own = P.inp("x1_own", [2048, C])
    ln2w = P.inp("ln2w", [1, C])
    wr = P.inp("wr", [C, 8])
    rbias = P.inp("rbias", [1, 8])
    sw1 = P.inp("sw1", [C, 2048])
    sw2 = P.inp("sw2", [2048, C])
    sw3 = P.inp("sw3", [C, 2048])
    ew1 = P.inp("ew1", [8, C, 2048])
    ew2 = P.inp("ew2", [8, 2048, C])
    ew3 = P.inp("ew3", [8, C, 2048])
    identc = P.inp("identc", [128, 128])
    xo = P.outp("xo", [2048, C])

    es0 = contextlib.ExitStack()
    ident = P.sb(es0, "ident", [128, 128], BF16)
    onesr = P.sb(es0, "onesr", [1, 128], F32)
    ln2bc = P.sb(es0, "ln2bc", [128, 1024], F32)
    rbbc = P.sb(es0, "rbbc", [128, 8], F32)
    w_r = P.sb(es0, "w_r", [128, 8, 8], BF16)
    rms2 = P.sb(es0, "rms2", [128, 8, 2048], BF16)
    gam = P.sb(es0, "gam", [128, 16, 12], F32)
    w1s = P.sb(es0, "w1s", [128, 2, 8, 128], BF16)
    w3s = P.sb(es0, "w3s", [128, 2, 8, 128], BF16)
    sil = P.sb(es0, "sil", [128, 2, 512], F32)
    acc = P.sb(es0, "acc", [128, 16, 1024], F32, side="right")
    h1 = P.sb(es0, "h1", [128, 16, 1024], BF16, side="right")
    w2all = P.sb(es0, "w2all", [128, 16, 1024], BF16, side="right")

    P.sec("g", lambda g: (
        g.dma_start(out=ident[:, :], in_=identc),
        g.dma_start(out=w_r[:, :, :], in_=wr.rearrange("(c p) n -> p c n", p=128)),
        g.memset(onesr[:, :], 1.0)))
    emit_row_bcast(P, ln2w, ln2bc, 1024, onesr, es0)
    emit_row_bcast(P, rbias, rbbc, 8, onesr, es0)
    P.sec("y", lambda y: y.dma_start(
        out=acc[:, :, :], in_=a_own.rearrange("(t p) c -> p t c", p=128)))
    c_a = P.val("dy")

    es1 = contextlib.ExitStack()
    rmst = P.sb(es1, "rmst", [128, 1024], BF16)
    sq = P.sb(es1, "sq", [128, 1024], F32)
    st = P.sb(es1, "st", [128, 8], F32)
    scr = dict(sq=sq, st=st)
    P.sec("v", lambda v: v.wait("dy", c_a))
    for tt in range(ONT):
        c = emit_norm(P, acc[:, tt, :], rmst[:, :], scr, eps=1e-6, kind="rms",
                      gain_bc=ln2bc[:, :])
        emit_transpose_tile(P, rmst, rms2, tt, c, ident)

    lg = P.sb(es1, "lg", [128, 8], F32)
    pb = P.sb(es1, "pb", [128, 8], F32)
    mx8 = P.sb(es1, "mx8", [128, 8], F32)
    ix8 = P.sb(es1, "ix8", [128, 8], mybir.dt.uint32)
    ixf = P.sb(es1, "ixf", [128, 1], F32)
    tw = P.sb(es1, "tw", [128, 4], F32)
    for tt in range(ONT):
        cpe = P.sec("t", lambda t, tt=tt: (
            t.wait("ve"), t.wait("dg"), t.wait("gc"),
            mm_chain(t, P.ps[0][:, 0:8],
                     [(rms2[:, cch, tt * 128:(tt + 1) * 128], w_r[:, cch, :])
                      for cch in range(8)])))

        def vrouter(v, tt=tt):
            v.wait("pe", cpe)
            v.tensor_add(lg[:, :], P.ps[0][:, 0:8], rbbc[:, :])
            v.edge()
            v.tensor_scalar(out=lg[:, :], in0=lg[:, :], scalar1=50.0, scalar2=-50.0,
                            op0=OP.min, op1=OP.max)
        cv = P.sec("v", vrouter)
        cs = P.sec("s", lambda s: (
            s.wait("ve", cv),
            s.activation(pb[:, :], lg[:, :], AF.Sigmoid)))

        def vsel(v, tt=tt):
            v.wait("se", cs)
            v.max(mx8[:, :], pb[:, :])
            v.edge()
            v.max_index(ix8[:, :], mx8[:, :], pb[:, :])
            v.tensor_scalar(out=tw[:, 0:1], in0=mx8[:, 0:1], scalar1=1e-8,
                            scalar2=1.0 - 1e-8, op0=OP.max, op1=OP.min)
            v.edge()
            v.tensor_scalar(out=tw[:, 1:2], in0=tw[:, 0:1],
                            scalar1=0.5 + 1e-8, scalar2=None, op0=OP.add)
            v.tensor_copy(ixf[:, :], ix8[:, 0:1])
            v.edge()
            v.tensor_scalar(out=tw[:, 1:2], in0=tw[:, 1:2], scalar1=0.5,
                            scalar2=2.0, op0=OP.max, op1=OP.min)
            v.edge()
            v.reciprocal(tw[:, 2:3], tw[:, 1:2])
            v.edge()
            v.tensor_scalar(out=gam[:, tt, 8:9], in0=tw[:, 2:3], scalar1=0.5,
                            scalar2=None, op0=OP.mult)
            v.tensor_mul(tw[:, 3:4], tw[:, 0:1], tw[:, 2:3])
            v.edge()
            for e in range(8):
                v.tensor_scalar(out=gam[:, tt, e:e + 1], in0=ixf[:, :],
                                scalar1=float(e), scalar2=None, op0=OP.is_equal)
            v.edge()
            for e in range(8):
                v.tensor_mul(gam[:, tt, e:e + 1], gam[:, tt, e:e + 1], tw[:, 3:4])
        P.sec("v", vsel)
    es1.close()
    P.barrier()

    emit_expert(P, e_idx=-1, kind="swiglu", rms2=rms2, h1=h1, w1s=w1s, w3s=w3s,
                w2all=w2all, sil=sil, acc=acc,
                gam_ap=lambda tt: gam[:, tt, 8:9],
                w1_d=sw1, w3_d=sw3, w2_d=sw2)
    for e in range(8):
        emit_expert(P, e_idx=e, kind="swiglu", rms2=rms2, h1=h1, w1s=w1s, w3s=w3s,
                    w2all=w2all, sil=sil, acc=acc,
                    gam_ap=lambda tt, e=e: gam[:, tt, e:e + 1],
                    w1_d=ew1[e], w3_d=ew3[e], w2_d=ew2[e])
    P.barrier()

    es4 = contextlib.ExitStack()
    xt = P.sb(es4, "xt", [128, 2, 1024], F32)
    lnt = P.sb(es4, "lnt", [128, 1024], F32)
    sq4 = P.sb(es4, "sq4", [128, 1024], F32)
    st4 = P.sb(es4, "st4", [128, 8], F32)
    scr4 = dict(sq=sq4, st=st4)
    for tt in range(ONT):
        buf = tt % 2
        P.sec("y", lambda y, tt=tt, buf=buf: (
            y.wait("ve"),
            y.dma_start(out=xt[:, buf, :], in_=x1_own[tt * 128:(tt + 1) * 128, :])))
        cld = P.val("dy")
        c = emit_norm(P, acc[:, tt, :], lnt[:, :], scr4, eps=1e-5, kind="ln")

        def resid(v, buf=buf):
            v.wait("ve", c)
            v.wait("dy", cld)
            v.tensor_add(xt[:, buf, :], xt[:, buf, :], lnt[:, :])
        cr = P.sec("v", resid)
        P.sec("y", lambda y, tt=tt, buf=buf: (
            y.wait("ve", cr),
            y.dma_start(out=xo[tt * 128:(tt + 1) * 128, :], in_=xt[:, buf, :])))
    P.sec("y", lambda y: y.wait_all())
    es4.close()
    es0.close()
    return P.finish()



def build_stageE():
    P = Prog("E")
    x_own = P.inp("x_own", [2048, C])
    gwt = P.inp("gwt", [C, 8])
    gb = P.inp("gb", [1, 8])
    rmsw = P.inp("rmsw", [1, 8])
    w1 = P.inp("w1", [8, C, 2048])
    b1 = P.inp("b1", [128, 8, 16])
    w2 = P.inp("w2", [8, 2048, C])
    b2 = P.inp("b2", [8, C])
    identc = P.inp("identc", [128, 128])
    xo = P.outp("xo", [2048, C])

    es0 = contextlib.ExitStack()
    ident = P.sb(es0, "ident", [128, 128], BF16)
    onesr = P.sb(es0, "onesr", [1, 128], F32)
    onesc8 = P.sb(es0, "onesc8", [8, 1], F32)
    gbbc = P.sb(es0, "gbbc", [128, 8], F32)
    rmswbc = P.sb(es0, "rmswbc", [128, 8], F32)
    w_g = P.sb(es0, "w_g", [128, 8, 8], BF16)
    b1sb = P.sb(es0, "b1sb", [128, 8, 16], F32)
    b2sb = P.sb(es0, "b2sb", [8, 1024], F32)
    b2row = P.sb(es0, "b2row", [1, 1024], F32)
    b2bc = P.sb(es0, "b2bc", [128, 1024], F32)
    hn = P.sb(es0, "hn", [128, 8, 2048], BF16)
    ng = P.sb(es0, "ng", [128, 16, 1], F32)
    w1s = P.sb(es0, "w1s", [128, 2, 8, 128], BF16)
    acc = P.sb(es0, "acc", [128, 16, 1024], F32, side="right")
    h1 = P.sb(es0, "h1", [128, 16, 1024], BF16, side="right")
    w2all = P.sb(es0, "w2all", [128, 16, 1024], BF16, side="right")

    P.sec("g", lambda g: (
        g.dma_start(out=ident[:, :], in_=identc),
        g.dma_start(out=w_g[:, :, :], in_=gwt.rearrange("(c p) n -> p c n", p=128)),
        g.dma_start(out=b1sb[:, :, :], in_=b1),
        g.dma_start(out=b2sb[:, :], in_=b2),
        g.memset(onesr[:, :], 1.0),
        g.memset(onesc8[:, :], 1.0)))
    emit_row_bcast(P, gb, gbbc, 8, onesr, es0)
    emit_row_bcast(P, rmsw, rmswbc, 8, onesr, es0)
    cpe = P.sec("t", lambda t: (
        t.wait("dg"), t.wait("gc"), t.wait("ve"),
        mm_chain(t, P.ps[0][0:1, :], [(onesc8[:, :], b2sb[:, 0:512])]),
        mm_chain(t, P.ps[1][0:1, :], [(onesc8[:, :], b2sb[:, 512:1024])])))
    cv = P.sec("v", lambda v: (
        v.wait("pe", cpe),
        v.tensor_copy(b2row[:, 0:512], P.ps[0][0:1, :]),
        v.tensor_copy(b2row[:, 512:1024], P.ps[1][0:1, :])))
    cpe2 = P.sec("t", lambda t: (
        t.wait("ve", cv),
        mm_chain(t, P.ps[0][:, :], [(onesr[:, :], b2row[:, 0:512])]),
        mm_chain(t, P.ps[1][:, :], [(onesr[:, :], b2row[:, 512:1024])])))
    P.sec("v", lambda v: (
        v.wait("pe", cpe2),
        v.tensor_copy(b2bc[:, 0:512], P.ps[0][:, :]),
        v.tensor_copy(b2bc[:, 512:1024], P.ps[1][:, :])))

    P.sec("v", lambda v: (v.wait("ve"),
                          [v.tensor_copy(acc[:, tt, :], b2bc[:, :])
                           for tt in range(ONT)]))

    es1 = contextlib.ExitStack()
    xt2 = P.sb(es1, "xt2", [128, 2, 1024], F32)
    hnt = P.sb(es1, "hnt", [128, 1024], BF16)
    sq = P.sb(es1, "sq", [128, 1024], F32)
    st = P.sb(es1, "st", [128, 8], F32)
    scr = dict(sq=sq, st=st)
    for tt in range(ONT):
        buf = tt % 2
        P.sec("y", lambda y, tt=tt, buf=buf: (
            y.wait("ve"), y.wait("se"),
            y.dma_start(out=xt2[:, buf, :], in_=x_own[tt * 128:(tt + 1) * 128, :])))
        c = emit_norm(P, xt2[:, buf, :], hnt[:, :], scr, eps=1e-5, kind="ln")
        emit_transpose_tile(P, hnt, hn, tt, c, ident)

    lg = P.sb(es1, "lg", [128, 8], F32)
    gs = P.sb(es1, "gs", [128, 8], F32)
    EPSF = float(np.finfo(np.float32).eps)
    for tt in range(ONT):
        cpe = P.sec("t", lambda t, tt=tt: (
            t.wait("ve"), t.wait("dg"), t.wait("gc"),
            mm_chain(t, P.ps[0][:, 0:8],
                     [(hn[:, cch, tt * 128:(tt + 1) * 128], w_g[:, cch, :])
                      for cch in range(8)])))

        def vg1(v, tt=tt):
            v.wait("pe", cpe)
            v.tensor_add(lg[:, :], P.ps[0][:, 0:8], gbbc[:, :])
            v.edge()
            v.tensor_mul(gs[:, :], lg[:, :], lg[:, :])
            v.edge()
            v.reduce_sum(st[:, 0:1], gs[:, :], axis=mybir.AxisListType.X)
            v.edge()
            v.tensor_scalar(out=st[:, 1:2], in0=st[:, 0:1], scalar1=1.0 / 8,
                            scalar2=None, op0=OP.mult)
        cv1 = P.sec("v", vg1)
        epsc = P.ccol(EPSF)
        cs1 = P.sec("s", lambda s: (
            s.wait("ve", cv1), s.wait("gc"),
            s.activation(st[:, 2:3], st[:, 1:2], AF.Ln, bias=epsc),
            s.edge(),
            s.activation(st[:, 3:4], st[:, 2:3], AF.Exp, scale=-0.5)))

        def vg2(v, tt=tt):
            v.wait("se", cs1)
            v.tensor_scalar(out=lg[:, :], in0=lg[:, :], scalar1=st[:, 3:4],
                            scalar2=None, op0=OP.mult)
            v.edge()
            v.tensor_mul(lg[:, :], lg[:, :], rmswbc[:, :])
            v.edge()
            v.reduce_max(st[:, 4:5], lg[:, :], axis=mybir.AxisListType.X)
            v.edge()
            v.tensor_scalar(out=st[:, 5:6], in0=st[:, 4:5], scalar1=-2.0,
                            scalar2=None, op0=OP.mult)
        cv2 = P.sec("v", vg2)
        cs2 = P.sec("s", lambda s: (
            s.wait("ve", cv2),
            s.activation(gs[:, :], lg[:, :], AF.Exp, bias=st[:, 5:6], scale=2.0)))

        def vg3(v, tt=tt):
            v.wait("se", cs2)
            v.reduce_sum(st[:, 6:7], gs[:, :], axis=mybir.AxisListType.X)
            v.edge()
            v.tensor_scalar(out=st[:, 7:8], in0=st[:, 6:7], scalar1=1e-6,
                            scalar2=1.0, op0=OP.mult, op1=OP.add)
            v.edge()
            v.reciprocal(ng[:, tt, :], st[:, 7:8])
        P.sec("v", vg3)
    es1.close()
    P.barrier()

    for e in range(8):
        emit_expert(P, e_idx=e, kind="gelu", rms2=hn, h1=h1, w1s=w1s, w3s=None,
                    w2all=w2all, sil=None, acc=acc, gam_ap=None,
                    w1_d=w1[e], w3_d=None, w2_d=w2[e],
                    b1col_fn=lambda m, e=e: b1sb[:, e, m:m + 1])
    P.barrier()

    es4 = contextlib.ExitStack()
    xt4 = P.sb(es4, "xt4", [128, 2, 1024], F32)
    lnt = P.sb(es4, "lnt", [128, 1024], F32)
    sq4 = P.sb(es4, "sq4", [128, 1024], F32)
    st4 = P.sb(es4, "st4", [128, 8], F32)
    scr4 = dict(sq=sq4, st=st4)
    for tt in range(ONT):
        buf = tt % 2
        P.sec("y", lambda y, tt=tt, buf=buf: (
            y.wait("ve"),
            y.dma_start(out=xt4[:, buf, :], in_=x_own[tt * 128:(tt + 1) * 128, :])))
        cld = P.val("dy")
        csc = P.sec("v", lambda v, tt=tt: (
            v.wait("pe"), v.wait("se"),
            v.tensor_scalar(out=acc[:, tt, :], in0=acc[:, tt, :],
                            scalar1=ng[:, tt, :], scalar2=None, op0=OP.mult)))
        c = emit_norm(P, acc[:, tt, :], lnt[:, :], scr4, eps=1e-5, kind="ln")

        def resid(v, tt=tt, buf=buf, cld=cld):
            v.wait("ve", c)
            v.wait("dy", cld)
            v.tensor_add(xt4[:, buf, :], xt4[:, buf, :], lnt[:, :])
        cr = P.sec("v", resid)
        P.sec("y", lambda y, tt=tt, buf=buf: (
            y.wait("ve", cr),
            y.dma_start(out=xo[tt * 128:(tt + 1) * 128, :], in_=xt4[:, buf, :])))
    P.sec("y", lambda y: y.wait_all())
    es4.close()
    es0.close()
    return P.finish()


def run_stage_C(a, x1, W):
    ncC = build_stageC()
    ident = np.eye(128, dtype=np.float32)
    in_maps = []
    for core in range(8):
        b, half = core // 2, core % 2
        tok = slice(half * 2048, (half + 1) * 2048)
        in_maps.append(dict(
            a_own=np.ascontiguousarray(a[b, tok]),
            x1_own=np.ascontiguousarray(x1[b, tok]),
            ln2w=W["ln2_w"].reshape(1, C),
            wr=np.ascontiguousarray(W["Wr"]),
            rbias=W["r_bias"].reshape(1, 8),
            sw1=np.ascontiguousarray(W["sw1"]), sw2=np.ascontiguousarray(W["sw2"]),
            sw3=np.ascontiguousarray(W["sw3"]),
            ew1=np.ascontiguousarray(W["ew1"]), ew2=np.ascontiguousarray(W["ew2"]),
            ew3=np.ascontiguousarray(W["ew3"]),
            identc=ident))
    res = run_bass_kernel_spmd(ncC, in_maps, core_ids=list(range(8)))
    x2 = np.empty_like(x1)
    for core in range(8):
        b, half = core // 2, core % 2
        x2[b, half * 2048:(half + 1) * 2048] = res.results[core]["xo"]
    return x2


def run_stage_E(x3, W):
    ncE = build_stageE()
    ident = np.eye(128, dtype=np.float32)
    in_maps = []
    for core in range(8):
        b, half = core // 2, core % 2
        tok = slice(half * 2048, (half + 1) * 2048)
        in_maps.append(dict(
            x_own=np.ascontiguousarray(x3[b, tok]),
            gwt=np.ascontiguousarray(W["gate_w"].T),
            gb=W["gate_b"].reshape(1, 8),
            rmsw=W["rms_w"].reshape(1, 8),
            w1=np.ascontiguousarray(W["w1"]),
            b1=np.ascontiguousarray(W["b1"].reshape(8, 16, 128).transpose(2, 0, 1)),
            w2=np.ascontiguousarray(W["w2"]), b2=np.ascontiguousarray(W["b2"]),
            identc=ident))
    res = run_bass_kernel_spmd(ncE, in_maps, core_ids=list(range(8)))
    x4 = np.empty_like(x3)
    for core in range(8):
        b, half = core // 2, core % 2
        x4[b, half * 2048:(half + 1) * 2048] = res.results[core]["xo"]
    return x4



PAIRS = [[0, 1], [2, 3], [4, 5], [6, 7]]


def emit_gather_send(P, own_fm, gin, gout):
    P.sec("y", lambda y: (
        y.wait("ve"), y.wait("se"), y.wait("cc"),
        y.dma_start(out=gin.rearrange("(c p) t -> p c t", p=128),
                    in_=own_fm[:, :, :])))
    c = P.val("dy")
    P.sec("g", lambda g: (
        g.wait("dy", c), g.wait("cc"),
        g.collective_compute("AllGather", OP.bypass, replica_groups=PAIRS,
                             ins=[gin.opt()], outs=[gout.opt()])))
    return P.val("cc")


def emit_gather_recv(P, full_fm, gout, ccv):
    P.sec("y", lambda y: (
        y.wait("cc", ccv),
        y.dma_start(out=full_fm[:, :, 0:2048],
                    in_=gout[0:1024].rearrange("(c p) t -> p c t", p=128)),
        y.dma_start(out=full_fm[:, :, 2048:4096],
                    in_=gout[1024:2048].rearrange("(c p) t -> p c t", p=128))))
    return P.val("dy")


def emit_gather_fm(P, own_fm, full_fm, gin, gout, ntiles_own=16):
    ccv = emit_gather_send(P, own_fm, gin, gout)
    return emit_gather_recv(P, full_fm, gout, ccv)


def emit_rs_y(P, ypd, ysd):
    c = P.val("dy")
    P.sec("g", lambda g: (
        g.wait("dy", c), g.wait("cc"),
        g.collective_compute("ReduceScatter", OP.add, replica_groups=PAIRS,
                             ins=[ypd.opt()], outs=[ysd.opt()])))
    return P.val("cc")


def emit_back_AD(P, com, ysd, bo, xsrc, xdst, ccv):
    es = contextlib.ExitStack()
    onesr = com["onesr"]
    yt = P.sb(es, "bk_yt", [128, 2, 1024], F32)
    xt = P.sb(es, "bk_xt", [128, 2, 1024], F32)
    lno = P.sb(es, "bk_lno", [128, 1024], F32)
    sq = P.sb(es, "bk_sq", [128, 1024], F32)
    st = P.sb(es, "bk_st", [128, 8], F32)
    scr = dict(sq=sq, st=st)
    bob = None
    if bo is not None:
        bob = P.sb(es, "bk_bob", [128, 1024], F32)
        emit_row_bcast(P, bo, bob, 1024, onesr, es)
    for j in range(ONT):
        buf = j % 2
        P.sec("y", lambda y, j=j, buf=buf: (
            y.wait("ve"), y.wait("se"), y.wait("cc", ccv),
            y.dma_start(out=yt[:, buf, :], in_=ysd[j * 128:(j + 1) * 128, :]),
            y.dma_start(out=xt[:, buf, :], in_=xsrc[j * 128:(j + 1) * 128, :])))
        cld = P.val("dy")
        if bo is not None:
            P.sec("v", lambda v, buf=buf: (
                v.wait("dy", cld), v.wait("pe"),
                v.tensor_add(yt[:, buf, :], yt[:, buf, :], bob[:, :])))
        else:
            P.sec("v", lambda v: v.wait("dy", cld))
        c = emit_norm(P, yt[:, buf, :], lno[:, :], scr, eps=1e-5, kind="ln")

        def resid(v, buf=buf):
            v.wait("ve", c)
            v.tensor_add(xt[:, buf, :], xt[:, buf, :], lno[:, :])
        cr = P.sec("v", resid)
        P.sec("y", lambda y, j=j, buf=buf: (
            y.wait("ve", cr),
            y.dma_start(out=xdst[j * 128:(j + 1) * 128, :], in_=xt[:, buf, :])))
    es.close()


def emit_back_B(P, com, ysd, x1src, adst, ccv):
    es = contextlib.ExitStack()
    yt = P.sb(es, "bb_yt", [128, 2, 1024], F32)
    xt = P.sb(es, "bb_xt", [128, 2, 1024], F32)
    ht = P.sb(es, "bb_ht", [128, 1024], F32)
    sq = P.sb(es, "bb_sq", [128, 1024], F32)
    st = P.sb(es, "bb_st", [128, 8], F32)
    scr = dict(sq=sq, st=st)
    for j in range(ONT):
        buf = j % 2
        P.sec("y", lambda y, j=j, buf=buf: (
            y.wait("ve"), y.wait("se"), y.wait("cc", ccv),
            y.dma_start(out=yt[:, buf, :], in_=ysd[j * 128:(j + 1) * 128, :]),
            y.dma_start(out=xt[:, buf, :], in_=x1src[j * 128:(j + 1) * 128, :])))
        cld = P.val("dy")
        P.sec("v", lambda v: v.wait("dy", cld))
        c = emit_norm(P, xt[:, buf, :], ht[:, :], scr, eps=1e-5, kind="ln")

        def addf(v, buf=buf):
            v.wait("ve", c)
            v.tensor_add(yt[:, buf, :], yt[:, buf, :], ht[:, :])
        cr = P.sec("v", addf)
        P.sec("y", lambda y, j=j, buf=buf: (
            y.wait("ve", cr),
            y.dma_start(out=adst[j * 128:(j + 1) * 128, :], in_=yt[:, buf, :])))
    es.close()


def emit_midA(P, com, w, hn, yp):
    ident, onesr, onescol, dm, s4 = (com["ident"], com["onesr"], com["onescol"],
                                     com["dm"], com["s4"])
    es0 = contextlib.ExitStack()
    e1 = P.sb(es0, "a_e1", [32, 2, 128], BF16)
    e2 = P.sb(es0, "a_e2", [64, 2, 128], BF16)
    eh1 = P.sb(es0, "a_eh1", [32, 8], BF16)
    eh2 = P.sb(es0, "a_eh2", [64, 8], BF16)
    w_q1 = P.sb(es0, "a_wq1", [128, 8, 32], BF16)
    w_k1 = P.sb(es0, "a_wk1", [128, 8, 32], BF16)
    w_q2 = P.sb(es0, "a_wq2", [128, 8, 64], BF16)
    w_k2 = P.sb(es0, "a_wk2", [128, 8, 64], BF16)
    w_v = P.sb(es0, "a_wv", [128, 8, 512], BF16)
    kt = P.sb(es0, "a_kt", [128, 2, 4096], BF16)
    qt = P.sb(es0, "a_qt", [128, 2, 4096], BF16)
    bbc = P.sb(es0, "a_bbc", [128, 8, 8], F32)
    btab_dram = P.dram("a_btd", [1, 64])

    def loads(g):
        g.wait("cc")
        g.dma_start(out=e1[:, :, :], in_=w["e1c"])
        g.dma_start(out=e2[:, :, :], in_=w["e2c"])
        g.dma_start(out=eh1[:, :], in_=w["eh1c"])
        g.dma_start(out=eh2[:, :], in_=w["eh2c"])
        g.dma_start(out=w_q1[:, :, :], in_=w["wq1t"].rearrange("(c p) n -> p c n", p=128))
        g.dma_start(out=w_k1[:, :, :], in_=w["wk1t"].rearrange("(c p) n -> p c n", p=128))
        g.dma_start(out=w_q2[:, :, :], in_=w["wq2t"].rearrange("(c p) n -> p c n", p=128))
        g.dma_start(out=w_k2[:, :, :], in_=w["wk2t"].rearrange("(c p) n -> p c n", p=128))
        g.dma_start(out=w_v[:, :, :], in_=w["wvt"].rearrange("(c p) n -> p c n", p=128))
    P.sec("g", loads)

    es2 = contextlib.ExitStack()
    q1s = P.sb(es2, "a_q1s", [32, 4096], BF16)
    k1s = P.sb(es2, "a_k1s", [32, 4096], BF16)
    q2s = P.sb(es2, "a_q2s", [64, 4096], BF16)
    k2s = P.sb(es2, "a_k2s", [64, 4096], BF16)
    nrm = P.sb(es2, "a_nrm", [8, 2, 4096], BF16)
    pp1 = P.sb(es2, "a_pp1", [128, 512], F32)
    pp2 = P.sb(es2, "a_pp2", [128, 512], F32)
    btabs = P.sb(es2, "a_btabs", [8, 8], F32)
    btabr = P.sb(es2, "a_btabr", [1, 64], F32)
    st = P.sb(es2, "a_st", [128, 8], F32)
    sq1 = P.sb(es2, "a_sq1", [32, 4096], BF16)
    sq2 = P.sb(es2, "a_sq2", [64, 4096], BF16)

    def proj(dst, wsb, m):
        for gg in range(8):
            cpe = P.sec("t", lambda t, gg=gg: (
                t.wait("gc"), t.wait("dg"), t.wait("ve"),
                mm_chain(t, P.ps[0][0:m, 0:512],
                         [(wsb[:, cch, :], hn[:, cch, gg * 512:(gg + 1) * 512])
                          for cch in range(8)])))
            P.sec("v", lambda v, gg=gg: (
                v.wait("pe", cpe),
                v.tensor_copy(dst[:, gg * 512:(gg + 1) * 512], P.ps[0][0:m, 0:512])))
    proj(q1s, w_q1, 32)
    proj(k1s, w_k1, 32)
    proj(q2s, w_q2, 64)
    proj(k2s, w_k2, 64)

    def emit_pairnorm(srcA, srcB, slot):
        cs = P.sec("s", lambda s: (
            s.wait("ve"),
            s.activation(sq1[:, :], srcA, AF.Square),
            s.activation(sq2[:, :], srcB, AF.Square)))
        for gg in range(8):
            cpe = P.sec("t", lambda t, gg=gg: (
                t.wait("se", cs), t.wait("ve"),
                mm_chain(t, P.ps[1][0:8, 0:512],
                         [(eh1[:, :], sq1[:, gg * 512:(gg + 1) * 512])]),
                mm_chain(t, P.ps[2][0:8, 0:512],
                         [(eh2[:, :], sq2[:, gg * 512:(gg + 1) * 512])])))
            P.sec("v", lambda v, gg=gg, slot=slot: (
                v.wait("pe", cpe),
                v.tensor_copy(pp1[0:8, :], P.ps[1][0:8, 0:512]),
                v.edge(),
                v.tensor_mul(nrm[:, slot, gg * 512:(gg + 1) * 512],
                             pp1[0:8, :], P.ps[2][0:8, 0:512])))
    emit_pairnorm(q1s[:, :], q2s[:, :], 0)
    emit_pairnorm(k1s[:, :], k2s[:, :], 1)
    SCALE = float((4 * 8) ** (-0.5))

    def vbt(v):
        v.wait("se")
        v.reduce_max(st[0:8, 0:1], nrm[:, 1, :], axis=mybir.AxisListType.X)
        for gg in range(8):
            v.reduce_max(btabs[:, gg:gg + 1], nrm[:, 0, gg * 512:(gg + 1) * 512],
                         axis=mybir.AxisListType.X)
        v.edge()
        v.tensor_scalar(out=btabs[:, :], in0=btabs[:, :], scalar1=st[0:8, 0:1],
                        scalar2=None, op0=OP.mult)
    cbt = P.sec("v", vbt)
    guardc = P.ccol(1e-30, rows=8)
    lnsc = P.ccol(float(np.log(SCALE)), rows=8)
    cbt2 = P.sec("s", lambda s: (
        s.wait("ve", cbt), s.wait("gc"),
        s.activation(btabs[:, :], btabs[:, :], AF.Ln, bias=guardc),
        s.edge(),
        s.activation(btabs[:, :], btabs[:, :], AF.Exp, scale=0.5, bias=lnsc),
        s.edge(),
        s.mul(btabs[:, :], btabs[:, :], -1.0)))
    P.sec("y", lambda y: (y.wait("se", cbt2),
                          y.dma_start(out=btab_dram.rearrange("o (g h) -> (o h) g", h=8),
                                      in_=btabs[:, :], allow_slow_non_contiguous=True)))
    cr = P.val("dy")
    P.sec("y", lambda y: (y.wait("dy", cr),
                          y.dma_start(out=btabr[:, :], in_=btab_dram)))
    btr_bf = P.sb(es2, "a_btrbf", [1, 64], BF16)
    cbf = P.sec("v", lambda v: (v.wait("dy"), v.wait("gc"),
                                v.tensor_copy(btr_bf[:, :], btabr[:, :])))
    onesrb = P.sb(es2, "a_onesrb", [1, 128], BF16)
    P.sec("g", lambda g: g.memset(onesrb[:, :], 1.0))
    cpe = P.sec("t", lambda t: (
        t.wait("ve", cbf), t.wait("gc"),
        mm_chain(t, P.ps[3][:, 0:64], [(onesrb[:, :], btr_bf[:, :])])))
    P.sec("v", lambda v: (v.wait("pe", cpe),
                          v.tensor_copy(bbc.ap().rearrange("p g h -> p (g h)"),
                                        P.ps[3][:, 0:64])))

    for cc2 in range(2):
        for gg in range(8):
            def tfn(t, cc2=cc2, gg=gg):
                t.wait("ve")
                mm_chain(t, P.ps[0][:, :], [(e1[:, cc2, :], q1s[:, gg * 512:(gg + 1) * 512])])
                mm_chain(t, P.ps[1][:, :], [(e2[:, cc2, :], q2s[:, gg * 512:(gg + 1) * 512])])
                mm_chain(t, P.ps[2][:, :], [(e1[:, cc2, :], k1s[:, gg * 512:(gg + 1) * 512])])
                mm_chain(t, P.ps[3][:, :], [(e2[:, cc2, :], k2s[:, gg * 512:(gg + 1) * 512])])
            cpe = P.sec("t", tfn)

            def vfn(v, cc2=cc2, gg=gg):
                v.wait("pe", cpe)
                v.tensor_copy(pp1[:, :], P.ps[0][:, :])
                v.tensor_copy(pp2[:, :], P.ps[2][:, :])
                v.edge()
                v.tensor_mul(qt[:, cc2, gg * 512:(gg + 1) * 512], pp1[:, :], P.ps[1][:, :])
                v.tensor_mul(kt[:, cc2, gg * 512:(gg + 1) * 512], pp2[:, :], P.ps[3][:, :])
            P.sec("v", vfn)
    es2.close()
    P.barrier()

    es3 = contextlib.ExitStack()
    vsb = P.sb(es3, "a_vsb", [128, 32, 512], BF16, side="right")
    for kc in range(NT):
        cpe = P.sec("t", lambda t, kc=kc: (
            t.wait("ve"),
            mm_chain(t, P.ps[kc % 2][:, :],
                     [(hn[:, cch, kc * 128:(kc + 1) * 128], w_v[:, cch, :])
                      for cch in range(8)])))
        P.sec("v", lambda v, kc=kc: (
            v.wait("pe", cpe),
            v.tensor_copy(vsb[:, kc, :], P.ps[kc % 2][:, :])))
    P.barrier()

    esv = contextlib.ExitStack()
    av = P.sb(esv, "a_av", [128, 4, 4096], BF16, side="right")
    es4 = contextlib.ExitStack()
    ptb = P.sb(es4, "a_ptb", [128, 4, 512], BF16)
    lsb = P.sb(es4, "a_lsb", [128, 512], F32)
    lbc = P.sb(es4, "a_lbc", [128, 2, 512], F32)
    for hq in range(2):
        for gg in range(8):
            nkc = 4 * gg + 4
            for kc in range(nkc):
                diag = kc - 4 * gg
                def sfn_t(t, hq=hq, gg=gg, kc=kc):
                    t.wait("se")
                    t.wait("ve")
                    t.wait("gc")
                    for i in range(4):
                        t.matmul(P.ps[i][:, :],
                                 kt[32 * i:32 * (i + 1), hq, kc * 128:(kc + 1) * 128],
                                 qt[32 * i:32 * (i + 1), hq, gg * 512:(gg + 1) * 512],
                                 start=True, stop=True,
                                 tile_position=(32 * i, 0))
                cs_pe = P.sec("t", sfn_t)

                def efn(s, hq=hq, gg=gg):
                    s.wait("pe", cs_pe)
                    for i in range(4):
                        s.activation(ptb[:, i, :], P.ps[i][:, :], AF.Exp,
                                     bias=bbc[:, gg, hq * 4 + i:hq * 4 + i + 1],
                                     scale=SCALE)
                ce = P.sec("s", efn)
                if diag >= 0:
                    def mfn(v, diag=diag):
                        v.wait("se", ce)
                        for i in range(4):
                            v.tensor_mul(ptb[:, i, :], ptb[:, i, :], dm[:, diag, :])
                    cm = P.sec("v", mfn)
                    dep = ("ve", cm)
                else:
                    dep = ("se", ce)

                def avfn(t, hq=hq, gg=gg, kc=kc, dep=dep, nkc=nkc):
                    t.wait(*dep)
                    t.wait("gc")
                    if kc == 0:
                        t.wait("ve")
                    st_ = (kc == 0)
                    sp_ = (kc == nkc - 1)
                    for pair in range(2):
                        for j in range(2):
                            hh = 2 * pair + j
                            t.matmul(P.ps[4 + pair][64 * j:64 * (j + 1), :],
                                     vsb[:, kc, (hq * 4 + hh) * 64:(hq * 4 + hh) * 64 + 64],
                                     ptb[:, hh, :],
                                     start=st_, stop=sp_,
                                     tile_position=(0, 64 * j))
                    for i in range(4):
                        t.matmul(P.ps[6][32 * i:32 * i + 1, :],
                                 onescol[:, :], ptb[:, i, :],
                                 start=st_, stop=sp_,
                                 tile_position=(0, 32 * i))
                P.sec("t", avfn)
            cev = P.sec("v", lambda v: (
                v.wait("pe"),
                v.tensor_copy(lsb[:, :], P.ps[6][:, :])))
            cbcp = P.sec("t", lambda t: (
                t.wait("ve", cev),
                mm_chain(t, P.ps[0][:, :], [(s4[:, 0, :], lsb[:, :])]),
                mm_chain(t, P.ps[1][:, :], [(s4[:, 1, :], lsb[:, :])])))
            crc = P.sec("v", lambda v: (
                v.wait("pe", cbcp),
                v.reciprocal(lbc[:, 0, :], P.ps[0][:, :]),
                v.reciprocal(lbc[:, 1, :], P.ps[1][:, :])))

            def vdiv(v, hq=hq, gg=gg):
                v.wait("ve", crc)
                for pair in range(2):
                    v.tensor_mul(av[:, hq * 2 + pair, gg * 512:(gg + 1) * 512],
                                 P.ps[4 + pair][:, :], lbc[:, pair, :])
            P.sec("v", vdiv)
    es4.close()
    P.barrier()

    es5 = contextlib.ExitStack()
    emit_partial_y(P, av, w["wot"], yp, es5)
    es5.close()
    esv.close()
    es3.close()
    es0.close()
    P.barrier()


def emit_midB(P, com, w, rms1, bufA, bufB):
    es0 = contextlib.ExitStack()
    w_q = P.sb(es0, "b_wq", [128, 8, 512], BF16)
    w_qsw = P.sb(es0, "b_wqsw", [128, 8, 512], BF16)
    w_kv = P.sb(es0, "b_wkv", [128, 8, 256], BF16)
    kv = P.sb(es0, "b_kv", [128, 2, 4096], BF16)
    bbc = P.sb(es0, "b_bbc", [128, 2, 4, 8], F32)
    rs2 = P.sb(es0, "b_rs2", [128, 512], F32)
    q_all, k_all = bufA, bufB

    P.sec("g", lambda g: (
        g.wait("cc"),
        g.dma_start(out=w_q[:, :, :], in_=w["wq"].rearrange("(c p) n -> p c n", p=128)),
        g.dma_start(out=w_qsw[:, :, :], in_=w["wqsw"].rearrange("(c p) n -> p c n", p=128)),
        g.dma_start(out=w_kv[:, :, :], in_=w["wkv"].rearrange("(c p) n -> p c n", p=128))))

    def rawproj(dstA, dstB, wA, wB, src_fm, nchunk):
        for p in range(4):
            for gg in range(8):
                cpe = P.sec("t", lambda t, p=p, gg=gg: (
                    t.wait("ve"), t.wait("dg"), t.wait("gc"),
                    mm_chain(t, P.ps[0][:, :],
                             [(wA[:, cch, 128 * p:128 * (p + 1)],
                               src_fm[:, cch, gg * 512:(gg + 1) * 512])
                              for cch in range(nchunk)]),
                    mm_chain(t, P.ps[1][:, :],
                             [(wB[:, cch, 128 * p:128 * (p + 1)],
                               src_fm[:, cch, gg * 512:(gg + 1) * 512])
                              for cch in range(nchunk)])))
                P.sec("v", lambda v, p=p, gg=gg: (
                    v.wait("pe", cpe),
                    v.tensor_copy(dstA[:, p, gg * 512:(gg + 1) * 512], P.ps[0][:, :]),
                    v.tensor_copy(dstB[:, p, gg * 512:(gg + 1) * 512], P.ps[1][:, :])))
    rawproj(bufA, bufB, w_q, w_qsw, rms1, 8)

    for gg in range(8):
        cpe = P.sec("t", lambda t, gg=gg: (
            t.wait("ve"),
            mm_chain(t, P.ps[0][:, :],
                     [(w_kv[:, cch, 0:128], rms1[:, cch, gg * 512:(gg + 1) * 512])
                      for cch in range(8)]),
            mm_chain(t, P.ps[1][:, :],
                     [(w_kv[:, cch, 128:256], rms1[:, cch, gg * 512:(gg + 1) * 512])
                      for cch in range(8)])))
        P.sec("v", lambda v, gg=gg: (
            v.wait("pe", cpe),
            v.tensor_scalar(out=kv[:, 0, gg * 512:(gg + 1) * 512], in0=P.ps[0][:, :],
                            scalar1=0.25, scalar2=None, op0=OP.mult),
            v.tensor_scalar(out=kv[:, 1, gg * 512:(gg + 1) * 512], in0=P.ps[1][:, :],
                            scalar1=0.25, scalar2=None, op0=OP.mult)))
    P.barrier()
    return dict(es0=es0, q_all=q_all, k_all=k_all, kv=kv, bbc=bbc, rs2=rs2)


def emit_midB2(P, com, w, ctx, yp):
    dm, s4, eh64, onesr, onescol = (com["dm"], com["s4"], com["eh64"],
                                    com["onesr"], com["onescol"])
    q_all, k_all, kv, bbc, rs2 = (ctx["q_all"], ctx["k_all"], ctx["kv"],
                                  ctx["bbc"], ctx["rs2"])
    estr = contextlib.ExitStack()
    crep = P.sb(estr, "b_crep", [128, N], F32)
    srep = P.sb(estr, "b_srep", [128, N], F32)
    w_kup = P.sb(estr, "b_wkup", [128, 2, 512], BF16)
    w_kupsw = P.sb(estr, "b_wkupsw", [128, 2, 512], BF16)
    w_vup = P.sb(estr, "b_wvup", [128, 2, 512], BF16)
    P.sec("g", lambda g: (
        g.dma_start(out=crep[:, :], in_=w["crepc"]),
        g.dma_start(out=srep[:, :], in_=w["srepc"]),
        g.dma_start(out=w_kup[:, :, :], in_=w["wkup"].rearrange("(c p) n -> p c n", p=128)),
        g.dma_start(out=w_kupsw[:, :, :], in_=w["wkupsw"].rearrange("(c p) n -> p c n", p=128)),
        g.dma_start(out=w_vup[:, :, :], in_=w["wvup"].rearrange("(c p) n -> p c n", p=128))))

    for p in range(4):
        for gg in range(8):
            def vr(v, p=p, gg=gg):
                v.wait("dg")
                v.wait("pe")
                sl = (slice(None), p, slice(gg * 512, (gg + 1) * 512))
                v.tensor_mul(rs2[:, :], k_all[sl], srep[:, gg * 512:(gg + 1) * 512])
                v.edge()
                v.tensor_mul(q_all[sl], q_all[sl], crep[:, gg * 512:(gg + 1) * 512])
                v.edge()
                v.tensor_add(q_all[sl], q_all[sl], rs2[:, :])
            P.sec("v", vr)

    for p in range(4):
        for gg in range(8):
            cpe = P.sec("t", lambda t, p=p, gg=gg: (
                t.wait("ve"), t.wait("dg"), t.wait("gc"),
                mm_chain(t, P.ps[0][:, :],
                         [(w_kup[:, lc, 128 * p:128 * (p + 1)],
                           kv[:, lc, gg * 512:(gg + 1) * 512]) for lc in range(2)]),
                mm_chain(t, P.ps[1][:, :],
                         [(w_kupsw[:, lc, 128 * p:128 * (p + 1)],
                           kv[:, lc, gg * 512:(gg + 1) * 512]) for lc in range(2)])))

            def vr2(v, p=p, gg=gg):
                v.wait("pe", cpe)
                v.tensor_mul(rs2[:, :], P.ps[1][:, :], srep[:, gg * 512:(gg + 1) * 512])
                v.tensor_mul(k_all[:, p, gg * 512:(gg + 1) * 512], P.ps[0][:, :],
                             crep[:, gg * 512:(gg + 1) * 512])
                v.edge()
                v.tensor_add(k_all[:, p, gg * 512:(gg + 1) * 512],
                             k_all[:, p, gg * 512:(gg + 1) * 512], rs2[:, :])
            P.sec("v", vr2)

    bufC = P.sb(ctx["esR"], "b_bufC", [128, 32, 512], BF16, side="right")
    vsb = bufC
    for kc in range(NT):
        cpe = P.sec("t", lambda t, kc=kc: (
            t.wait("ve"),
            mm_chain(t, P.ps[kc % 2][:, :],
                     [(kv[:, lc, kc * 128:(kc + 1) * 128], w_vup[:, lc, :])
                      for lc in range(2)])))
        P.sec("v", lambda v, kc=kc: (
            v.wait("pe", cpe),
            v.tensor_copy(vsb[:, kc, :], P.ps[kc % 2][:, :])))
    estr.close()
    P.barrier()

    es3 = contextlib.ExitStack()
    nq = P.sb(es3, "b_nq", [128, 4096], BF16)
    nk = P.sb(es3, "b_nk", [128, 4096], BF16)
    sqq = P.sb(es3, "b_sqq", [128, 4096], BF16)
    emit_pair_norms(P, q_all, nq[:, :], eh64, sqq)
    emit_pair_norms(P, k_all, nk[:, :], eh64, sqq)
    emit_btab2(P, nq[:, :], nk[:, :], bbc, onesr, float(np.log(0.125)), es3)
    es3.close()
    P.barrier()

    esv = contextlib.ExitStack()
    av = P.sb(esv, "b_av", [128, 4, 4096], BF16, side="right")
    es4 = contextlib.ExitStack()
    emit_pair_attention(P, k_all=k_all, q_all=q_all, vsb=vsb, av=av,
                        bbc=bbc, dm=dm, s4=s4, onescol=onescol,
                        scale=0.125, causal=True, es4=es4)
    P.barrier()
    es5 = contextlib.ExitStack()
    emit_partial_y(P, av, w["wot"], yp, es5)
    es5.close()
    es4.close()
    esv.close()
    ctx["es0"].close()
    P.barrier()


def emit_midD(P, com, w, hn, q_all, k_all, vsb, yp):
    s4, eh64, onesr, onescol = com["s4"], com["eh64"], com["onesr"], com["onescol"]
    es0 = contextlib.ExitStack()
    w_q = P.sb(es0, "d_wq", [128, 8, 512], BF16)
    w_k = P.sb(es0, "d_wk", [128, 8, 512], BF16)
    w_v = P.sb(es0, "d_wv", [128, 8, 512], BF16)
    bbc = P.sb(es0, "d_bbc", [128, 2, 4, 8], F32)
    P.sec("g", lambda g: (
        g.wait("cc"),
        g.dma_start(out=w_q[:, :, :], in_=w["wq"].rearrange("(c p) n -> p c n", p=128)),
        g.dma_start(out=w_k[:, :, :], in_=w["wk"].rearrange("(c p) n -> p c n", p=128)),
        g.dma_start(out=w_v[:, :, :], in_=w["wv"].rearrange("(c p) n -> p c n", p=128))))

    def proj_pair(dst_all, wsb):
        for p in range(4):
            for gg in range(8):
                cpe = P.sec("t", lambda t, p=p, gg=gg: (
                    t.wait("ve"), t.wait("dg"), t.wait("gc"),
                    mm_chain(t, P.ps[0][:, :],
                             [(wsb[:, cch, 128 * p:128 * (p + 1)],
                               hn[:, cch, gg * 512:(gg + 1) * 512])
                              for cch in range(8)])))
                P.sec("v", lambda v, p=p, gg=gg: (
                    v.wait("pe", cpe),
                    v.tensor_copy(dst_all[:, p, gg * 512:(gg + 1) * 512],
                                  P.ps[0][:, :])))
    proj_pair(q_all, w_q)
    proj_pair(k_all, w_k)
    for kc in range(NT):
        cpe = P.sec("t", lambda t, kc=kc: (
            t.wait("ve"),
            mm_chain(t, P.ps[kc % 2][:, :],
                     [(hn[:, cch, kc * 128:(kc + 1) * 128], w_v[:, cch, :])
                      for cch in range(8)])))
        P.sec("v", lambda v, kc=kc: (
            v.wait("pe", cpe),
            v.tensor_copy(vsb[:, kc, :], P.ps[kc % 2][:, :])))
    P.barrier()
    if "hn_done" in w:
        w["hn_done"]()

    es3 = contextlib.ExitStack()
    nq = P.sb(es3, "d_nq", [128, 4096], BF16)
    nk = P.sb(es3, "d_nk", [128, 4096], BF16)
    sqq = P.sb(es3, "d_sqq", [128, 4096], BF16)
    emit_pair_norms(P, q_all, nq[:, :], eh64, sqq)
    emit_pair_norms(P, k_all, nk[:, :], eh64, sqq)
    emit_btab2(P, nq[:, :], nk[:, :], bbc, onesr, float(np.log(0.125)), es3)
    es3.close()
    P.barrier()

    es4 = contextlib.ExitStack()
    av2 = P.sb(es4, "d_av", [128, 4, 4096], BF16, side="right")
    emit_pair_attention(P, k_all=k_all, q_all=q_all, vsb=vsb, av=av2,
                        bbc=bbc, dm=None, s4=s4, onescol=onescol,
                        scale=0.125, causal=False, es4=es4)
    P.barrier()
    es5 = contextlib.ExitStack()
    emit_partial_y(P, av2, w["wot"], yp, es5)
    es5.close()
    es4.close()
    es0.close()
    P.barrier()


def emit_bodyC(P, com, w, a_src, x1_src, xdst):
    ident, onesr = com["ident"], com["onesr"]
    es0 = contextlib.ExitStack()
    ln2bc = P.sb(es0, "c_ln2bc", [128, 1024], F32)
    rbbc = P.sb(es0, "c_rbbc", [128, 8], F32)
    w_r = P.sb(es0, "c_wr", [128, 8, 8], BF16)
    rms2 = P.sb(es0, "c_rms2", [128, 8, 2048], BF16)
    gam = P.sb(es0, "c_gam", [128, 16, 12], F32)
    w1s = P.sb(es0, "c_w1s", [128, 2, 8, 128], BF16)
    w3s = P.sb(es0, "c_w3s", [128, 2, 8, 128], BF16)
    sil = P.sb(es0, "c_sil", [128, 2, 512], F32)
    acc = P.sb(es0, "c_acc", [128, 16, 1024], F32, side="right")
    h1 = P.sb(es0, "c_h1", [128, 16, 1024], BF16, side="right")
    w2all = P.sb(es0, "c_w2all", [128, 16, 1024], BF16, side="right")

    P.sec("g", lambda g: (
        g.wait("cc"),
        g.dma_start(out=w_r[:, :, :], in_=w["wr"].rearrange("(c p) n -> p c n", p=128))))
    emit_row_bcast(P, w["ln2w"], ln2bc, 1024, onesr, es0)
    emit_row_bcast(P, w["rbias"], rbbc, 8, onesr, es0)
    P.sec("y", lambda y: (
        y.wait("ve"), y.wait("se"), y.wait("cc"),
        y.dma_start(out=acc[:, :, :], in_=a_src.rearrange("(t p) c -> p t c", p=128))))
    c_a = P.val("dy")

    es1 = contextlib.ExitStack()
    rmst = P.sb(es1, "c_rmst", [128, 1024], BF16)
    sq = P.sb(es1, "c_sq", [128, 1024], F32)
    st = P.sb(es1, "c_st", [128, 8], F32)
    scr = dict(sq=sq, st=st)
    P.sec("v", lambda v: v.wait("dy", c_a))
    for tt in range(ONT):
        c = emit_norm(P, acc[:, tt, :], rmst[:, :], scr, eps=1e-6, kind="rms",
                      gain_bc=ln2bc[:, :])
        emit_transpose_tile(P, rmst, rms2, tt, c, ident)

    lg = P.sb(es1, "c_lg", [128, 8], F32)
    pb = P.sb(es1, "c_pb", [128, 8], F32)
    mx8 = P.sb(es1, "c_mx8", [128, 8], F32)
    ix8 = P.sb(es1, "c_ix8", [128, 8], mybir.dt.uint32)
    ixf = P.sb(es1, "c_ixf", [128, 1], F32)
    tw = P.sb(es1, "c_tw", [128, 4], F32)
    for tt in range(ONT):
        cpe = P.sec("t", lambda t, tt=tt: (
            t.wait("ve"), t.wait("dg"), t.wait("gc"),
            mm_chain(t, P.ps[0][:, 0:8],
                     [(rms2[:, cch, tt * 128:(tt + 1) * 128], w_r[:, cch, :])
                      for cch in range(8)])))

        def vrouter(v, tt=tt):
            v.wait("pe", cpe)
            v.tensor_add(lg[:, :], P.ps[0][:, 0:8], rbbc[:, :])
            v.edge()
            v.tensor_scalar(out=lg[:, :], in0=lg[:, :], scalar1=50.0, scalar2=-50.0,
                            op0=OP.min, op1=OP.max)
        cv = P.sec("v", vrouter)
        cs = P.sec("s", lambda s: (
            s.wait("ve", cv),
            s.activation(pb[:, :], lg[:, :], AF.Sigmoid)))

        def vsel(v, tt=tt):
            v.wait("se", cs)
            v.max(mx8[:, :], pb[:, :])
            v.edge()
            v.max_index(ix8[:, :], mx8[:, :], pb[:, :])
            v.tensor_scalar(out=tw[:, 0:1], in0=mx8[:, 0:1], scalar1=1e-8,
                            scalar2=1.0 - 1e-8, op0=OP.max, op1=OP.min)
            v.edge()
            v.tensor_scalar(out=tw[:, 1:2], in0=tw[:, 0:1],
                            scalar1=0.5 + 1e-8, scalar2=None, op0=OP.add)
            v.tensor_copy(ixf[:, :], ix8[:, 0:1])
            v.edge()
            v.tensor_scalar(out=tw[:, 1:2], in0=tw[:, 1:2], scalar1=0.5,
                            scalar2=2.0, op0=OP.max, op1=OP.min)
            v.edge()
            v.reciprocal(tw[:, 2:3], tw[:, 1:2])
            v.edge()
            v.tensor_scalar(out=gam[:, tt, 8:9], in0=tw[:, 2:3], scalar1=0.5,
                            scalar2=None, op0=OP.mult)
            v.tensor_mul(tw[:, 3:4], tw[:, 0:1], tw[:, 2:3])
            v.edge()
            for e in range(8):
                v.tensor_scalar(out=gam[:, tt, e:e + 1], in0=ixf[:, :],
                                scalar1=float(e), scalar2=None, op0=OP.is_equal)
            v.edge()
            for e in range(8):
                v.tensor_mul(gam[:, tt, e:e + 1], gam[:, tt, e:e + 1], tw[:, 3:4])
        P.sec("v", vsel)
    es1.close()
    P.barrier()

    emit_expert(P, e_idx=-1, kind="swiglu", rms2=rms2, h1=h1, w1s=w1s, w3s=w3s,
                w2all=w2all, sil=sil, acc=acc,
                gam_ap=lambda tt: gam[:, tt, 8:9],
                w1_d=w["sw1"], w3_d=w["sw3"], w2_d=w["sw2"])
    for e in range(8):
        emit_expert(P, e_idx=e, kind="swiglu", rms2=rms2, h1=h1, w1s=w1s, w3s=w3s,
                    w2all=w2all, sil=sil, acc=acc,
                    gam_ap=lambda tt, e=e: gam[:, tt, e:e + 1],
                    w1_d=w["ew1"][e], w3_d=w["ew3"][e], w2_d=w["ew2"][e])
    P.barrier()

    es4 = contextlib.ExitStack()
    xt = P.sb(es4, "c_xt", [128, 2, 1024], F32)
    lnt = P.sb(es4, "c_lnt", [128, 1024], F32)
    sq4 = P.sb(es4, "c_sq4", [128, 1024], F32)
    st4 = P.sb(es4, "c_st4", [128, 8], F32)
    scr4 = dict(sq=sq4, st=st4)
    for tt in range(ONT):
        buf = tt % 2
        P.sec("y", lambda y, tt=tt, buf=buf: (
            y.wait("ve"),
            y.dma_start(out=xt[:, buf, :], in_=x1_src[tt * 128:(tt + 1) * 128, :])))
        cld = P.val("dy")
        c = emit_norm(P, acc[:, tt, :], lnt[:, :], scr4, eps=1e-5, kind="ln")

        def resid(v, buf=buf, cld=cld):
            v.wait("ve", c)
            v.wait("dy", cld)
            v.tensor_add(xt[:, buf, :], xt[:, buf, :], lnt[:, :])
        cr = P.sec("v", resid)
        P.sec("y", lambda y, tt=tt, buf=buf: (
            y.wait("ve", cr),
            y.dma_start(out=xdst[tt * 128:(tt + 1) * 128, :], in_=xt[:, buf, :])))
    es4.close()
    es0.close()
    P.barrier()


def emit_bodyE(P, com, w, xsrc, xdst):
    ident, onesr = com["ident"], com["onesr"]
    es0 = contextlib.ExitStack()
    onesc8 = P.sb(es0, "e_onesc8", [8, 1], F32)
    gbbc = P.sb(es0, "e_gbbc", [128, 8], F32)
    rmswbc = P.sb(es0, "e_rmswbc", [128, 8], F32)
    w_g = P.sb(es0, "e_wg", [128, 8, 8], BF16)
    b1sb = P.sb(es0, "e_b1sb", [128, 8, 16], F32)
    b2sb = P.sb(es0, "e_b2sb", [8, 1024], F32)
    b2row = P.sb(es0, "e_b2row", [1, 1024], F32)
    b2bc = P.sb(es0, "e_b2bc", [128, 1024], F32)
    hn = P.sb(es0, "e_hn", [128, 8, 2048], BF16)
    ng = P.sb(es0, "e_ng", [128, 16, 1], F32)
    w1s = P.sb(es0, "e_w1s", [128, 2, 8, 128], BF16)
    acc = P.sb(es0, "e_acc", [128, 16, 1024], F32, side="right")
    h1 = P.sb(es0, "e_h1", [128, 16, 1024], BF16, side="right")
    w2all = P.sb(es0, "e_w2all", [128, 16, 1024], BF16, side="right")

    P.sec("g", lambda g: (
        g.wait("cc"),
        g.dma_start(out=w_g[:, :, :], in_=w["gwt"].rearrange("(c p) n -> p c n", p=128)),
        g.dma_start(out=b1sb[:, :, :], in_=w["b1"]),
        g.dma_start(out=b2sb[:, :], in_=w["b2"]),
        g.memset(onesc8[:, :], 1.0)))
    emit_row_bcast(P, w["gb"], gbbc, 8, onesr, es0)
    emit_row_bcast(P, w["rmsw"], rmswbc, 8, onesr, es0)
    cpe = P.sec("t", lambda t: (
        t.wait("dg"), t.wait("gc"), t.wait("ve"),
        mm_chain(t, P.ps[0][0:1, :], [(onesc8[:, :], b2sb[:, 0:512])]),
        mm_chain(t, P.ps[1][0:1, :], [(onesc8[:, :], b2sb[:, 512:1024])])))
    cv = P.sec("v", lambda v: (
        v.wait("pe", cpe),
        v.tensor_copy(b2row[:, 0:512], P.ps[0][0:1, :]),
        v.tensor_copy(b2row[:, 512:1024], P.ps[1][0:1, :])))
    cpe2 = P.sec("t", lambda t: (
        t.wait("ve", cv),
        mm_chain(t, P.ps[0][:, :], [(onesr[:, :], b2row[:, 0:512])]),
        mm_chain(t, P.ps[1][:, :], [(onesr[:, :], b2row[:, 512:1024])])))
    P.sec("v", lambda v: (
        v.wait("pe", cpe2),
        v.tensor_copy(b2bc[:, 0:512], P.ps[0][:, :]),
        v.tensor_copy(b2bc[:, 512:1024], P.ps[1][:, :])))
    P.sec("v", lambda v: (v.wait("ve"),
                          [v.tensor_copy(acc[:, tt, :], b2bc[:, :])
                           for tt in range(ONT)]))

    es1 = contextlib.ExitStack()
    xt2 = P.sb(es1, "e_xt2", [128, 2, 1024], F32)
    hnt = P.sb(es1, "e_hnt", [128, 1024], BF16)
    sq = P.sb(es1, "e_sq", [128, 1024], F32)
    st = P.sb(es1, "e_st", [128, 8], F32)
    scr = dict(sq=sq, st=st)
    for tt in range(ONT):
        buf = tt % 2
        P.sec("y", lambda y, tt=tt, buf=buf: (
            y.wait("ve"), y.wait("se"),
            y.dma_start(out=xt2[:, buf, :], in_=xsrc[tt * 128:(tt + 1) * 128, :])))
        c = emit_norm(P, xt2[:, buf, :], hnt[:, :], scr, eps=1e-5, kind="ln")
        emit_transpose_tile(P, hnt, hn, tt, c, ident)

    lg = P.sb(es1, "e_lg", [128, 8], F32)
    gs = P.sb(es1, "e_gs", [128, 8], F32)
    EPSF = float(np.finfo(np.float32).eps)
    for tt in range(ONT):
        cpe = P.sec("t", lambda t, tt=tt: (
            t.wait("ve"), t.wait("dg"), t.wait("gc"),
            mm_chain(t, P.ps[0][:, 0:8],
                     [(hn[:, cch, tt * 128:(tt + 1) * 128], w_g[:, cch, :])
                      for cch in range(8)])))

        def vg1(v, tt=tt):
            v.wait("pe", cpe)
            v.tensor_add(lg[:, :], P.ps[0][:, 0:8], gbbc[:, :])
            v.edge()
            v.tensor_mul(gs[:, :], lg[:, :], lg[:, :])
            v.edge()
            v.reduce_sum(st[:, 0:1], gs[:, :], axis=mybir.AxisListType.X)
            v.edge()
            v.tensor_scalar(out=st[:, 1:2], in0=st[:, 0:1], scalar1=1.0 / 8,
                            scalar2=None, op0=OP.mult)
        cv1 = P.sec("v", vg1)
        epsc = P.ccol(EPSF)
        cs1 = P.sec("s", lambda s: (
            s.wait("ve", cv1), s.wait("gc"),
            s.activation(st[:, 2:3], st[:, 1:2], AF.Ln, bias=epsc),
            s.edge(),
            s.activation(st[:, 3:4], st[:, 2:3], AF.Exp, scale=-0.5)))

        def vg2(v, tt=tt):
            v.wait("se", cs1)
            v.tensor_scalar(out=lg[:, :], in0=lg[:, :], scalar1=st[:, 3:4],
                            scalar2=None, op0=OP.mult)
            v.edge()
            v.tensor_mul(lg[:, :], lg[:, :], rmswbc[:, :])
            v.edge()
            v.reduce_max(st[:, 4:5], lg[:, :], axis=mybir.AxisListType.X)
            v.edge()
            v.tensor_scalar(out=st[:, 5:6], in0=st[:, 4:5], scalar1=-2.0,
                            scalar2=None, op0=OP.mult)
        cv2 = P.sec("v", vg2)
        cs2 = P.sec("s", lambda s: (
            s.wait("ve", cv2),
            s.activation(gs[:, :], lg[:, :], AF.Exp, bias=st[:, 5:6], scale=2.0)))

        def vg3(v, tt=tt):
            v.wait("se", cs2)
            v.reduce_sum(st[:, 6:7], gs[:, :], axis=mybir.AxisListType.X)
            v.edge()
            v.tensor_scalar(out=st[:, 7:8], in0=st[:, 6:7], scalar1=1e-6,
                            scalar2=1.0, op0=OP.mult, op1=OP.add)
            v.edge()
            v.reciprocal(ng[:, tt, :], st[:, 7:8])
        P.sec("v", vg3)
    es1.close()
    P.barrier()

    for e in range(8):
        emit_expert(P, e_idx=e, kind="gelu", rms2=hn, h1=h1, w1s=w1s, w3s=None,
                    w2all=w2all, sil=None, acc=acc, gam_ap=None,
                    w1_d=w["w1"][e], w3_d=None, w2_d=w["w2"][e],
                    b1col_fn=lambda m, e=e: b1sb[:, e, m:m + 1])
    P.barrier()

    es4 = contextlib.ExitStack()
    xt4 = P.sb(es4, "e_xt4", [128, 2, 1024], F32)
    lnt = P.sb(es4, "e_lnt", [128, 1024], F32)
    sq4 = P.sb(es4, "e_sq4", [128, 1024], F32)
    st4 = P.sb(es4, "e_st4", [128, 8], F32)
    scr4 = dict(sq=sq4, st=st4)
    for tt in range(ONT):
        buf = tt % 2
        P.sec("y", lambda y, tt=tt, buf=buf: (
            y.wait("ve"),
            y.dma_start(out=xt4[:, buf, :], in_=xsrc[tt * 128:(tt + 1) * 128, :])))
        cld = P.val("dy")
        csc = P.sec("v", lambda v, tt=tt: (
            v.wait("pe"), v.wait("se"),
            v.tensor_scalar(out=acc[:, tt, :], in0=acc[:, tt, :],
                            scalar1=ng[:, tt, :], scalar2=None, op0=OP.mult)))
        c = emit_norm(P, acc[:, tt, :], lnt[:, :], scr4, eps=1e-5, kind="ln")

        def resid(v, tt=tt, buf=buf, cld=cld):
            v.wait("ve", c)
            v.wait("dy", cld)
            v.tensor_add(xt4[:, buf, :], xt4[:, buf, :], lnt[:, :])
        cr = P.sec("v", resid)
        P.sec("y", lambda y, tt=tt, buf=buf: (
            y.wait("ve", cr),
            y.dma_start(out=xdst[tt * 128:(tt + 1) * 128, :], in_=xt4[:, buf, :])))
    es4.close()
    es0.close()


def build_fused():
    P = Prog("MK")
    x_own = P.inp("x_own", [2048, C])
    xo = P.outp("xo", [2048, C])
    wa = dict(
        wq1t=P.inp("a_wq1t", [C, 32]), wk1t=P.inp("a_wk1t", [C, 32]),
        wq2t=P.inp("a_wq2t", [C, 64]), wk2t=P.inp("a_wk2t", [C, 64]),
        wvt=P.inp("a_wvt", [C, 512]), wot=P.inp("a_wot", [512, C]),
        e1c=P.inp("a_e1c", [32, 2, 128]), e2c=P.inp("a_e2c", [64, 2, 128]),
        eh1c=P.inp("a_eh1c", [32, 8]), eh2c=P.inp("a_eh2c", [64, 8]))
    bo_a = P.inp("a_bo", [1, C])
    wb = dict(
        wq=P.inp("b_wq", [C, 512]), wqsw=P.inp("b_wqsw", [C, 512]),
        wkv=P.inp("b_wkv", [C, 256]), wkup=P.inp("b_wkup", [256, 512]),
        wkupsw=P.inp("b_wkupsw", [256, 512]), wvup=P.inp("b_wvup", [256, 512]),
        wot=P.inp("b_wot", [512, C]),
        crepc=P.inp("b_crepc", [128, N]), srepc=P.inp("b_srepc", [128, N]))
    ln1w = P.inp("b_ln1w", [1, C])
    wc = dict(
        ln2w=P.inp("c_ln2w", [1, C]), wr=P.inp("c_wr", [C, 8]),
        rbias=P.inp("c_rbias", [1, 8]),
        sw1=P.inp("c_sw1", [C, 2048]), sw2=P.inp("c_sw2", [2048, C]),
        sw3=P.inp("c_sw3", [C, 2048]),
        ew1=P.inp("c_ew1", [8, C, 2048]), ew2=P.inp("c_ew2", [8, 2048, C]),
        ew3=P.inp("c_ew3", [8, C, 2048]))
    wd = dict(
        wq=P.inp("d_wq", [C, 512]), wk=P.inp("d_wk", [C, 512]),
        wv=P.inp("d_wv", [C, 512]), wot=P.inp("d_wot", [512, C]))
    bo_d = P.inp("d_bo", [1, C])
    we = dict(
        gwt=P.inp("e_gwt", [C, 8]), gb=P.inp("e_gb", [1, 8]),
        rmsw=P.inp("e_rmsw", [1, 8]), w1=P.inp("e_w1", [8, C, 2048]),
        b1=P.inp("e_b1", [128, 8, 16]), w2=P.inp("e_w2", [8, 2048, C]),
        b2=P.inp("e_b2", [8, C]))
    identc = P.inp("identc", [128, 128])
    dmc = P.inp("dmc", [128, 4, 512])
    s4c = P.inp("s4c", [128, 2, 128])
    eh64c = P.inp("eh64c", [128, 2])

    x1s = P.dram("x1s", [2048, C])
    x2s = P.dram("x2s", [2048, C])
    x3s = P.dram("x3s", [2048, C])
    a_s = P.dram("a_s", [2048, C])
    gin = P.dram("gin", [1024, 2048], BF16)
    gout = P.dram("gout", [2048, 2048], BF16)
    ypd = P.dram("ypd", [N, C])
    ysd = P.dram("ysd", [2048, C])

    esc = contextlib.ExitStack()
    ident = P.sb(esc, "ident", [128, 128], BF16)
    onesr = P.sb(esc, "onesr", [1, 128], F32)
    onescol = P.sb(esc, "onescol", [128, 1], BF16)
    dm = P.sb(esc, "dm", [128, 4, 512], BF16)
    s4 = P.sb(esc, "s4", [128, 2, 128], F32)
    eh64 = P.sb(esc, "eh64", [128, 2], BF16)
    P.sec("g", lambda g: (
        g.dma_start(out=ident[:, :], in_=identc),
        g.dma_start(out=dm[:, :, :], in_=dmc),
        g.dma_start(out=s4[:, :, :], in_=s4c),
        g.dma_start(out=eh64[:, :], in_=eh64c),
        g.memset(onesr[:, :], 1.0),
        g.memset(onescol[:, :], 1.0)))
    com = dict(ident=ident, onesr=onesr, onescol=onescol, dm=dm, s4=s4, eh64=eh64)
    P.sem("cc")

    esA = contextlib.ExitStack()
    hn_own = P.sb(esA, "A_hnown", [128, 8, 2048], BF16)
    emit_ln_to_fm(P, x_own, hn_own, esA, ident, ntiles=ONT)
    ccg = emit_gather_send(P, hn_own, gin, gout)
    P.barrier()
    esA.close()
    esAf = contextlib.ExitStack()
    hn = P.sb(esAf, "A_hn", [128, 8, 4096], BF16, side="right")
    emit_gather_recv(P, hn, gout, ccg)
    P.barrier()
    emit_midA(P, com, wa, hn, ypd)
    ccv = emit_rs_y(P, ypd, ysd)
    esAf.close()
    emit_back_AD(P, com, ysd, bo_a, x_own, x1s, ccv)
    P.barrier()

    esB = contextlib.ExitStack()
    ln1bc = P.sb(esB, "B_ln1bc", [128, 1024], F32)
    emit_row_bcast(P, ln1w, ln1bc, 1024, onesr, esB)
    esBR = contextlib.ExitStack()
    bufA = P.sb(esBR, "B_bufA", [128, 4, 4096], BF16, side="right")
    bufB = P.sb(esBR, "B_bufB", [128, 4, 4096], BF16, side="right")
    esB1 = contextlib.ExitStack()
    rms1own = P.sb(esB1, "B_rms1own", [128, 8, 2048], BF16)
    emit_ln_to_fm(P, x1s, rms1own, esB1, ident, ntiles=ONT,
                  rms_eps=1e-6, gain_bc=ln1bc[:, :])
    ccg = emit_gather_send(P, rms1own, gin, gout)
    P.barrier()
    esB1.close()
    esBf = contextlib.ExitStack()
    rms1 = P.sb(esBf, "B_rms1", [128, 8, 4096], BF16, side="right")
    emit_gather_recv(P, rms1, gout, ccg)
    P.barrier()
    ctxB = emit_midB(P, com, wb, rms1, bufA, bufB)
    ctxB["esR"] = esBR
    esBf.close()
    P.barrier()
    emit_midB2(P, com, wb, ctxB, ypd)
    ccv = emit_rs_y(P, ypd, ysd)
    emit_back_B(P, com, ysd, x1s, a_s, ccv)
    esB.close()
    esBR.close()
    P.barrier()

    emit_bodyC(P, com, wc, a_s, x1s, x2s)

    esDR = contextlib.ExitStack()
    qD = P.sb(esDR, "D_q", [128, 4, 4096], BF16, side="right")
    kD = P.sb(esDR, "D_k", [128, 4, 4096], BF16, side="right")
    vD = P.sb(esDR, "D_v", [128, 32, 512], BF16, side="right")
    esD = contextlib.ExitStack()
    hn_ownD = P.sb(esD, "D_hnown", [128, 8, 2048], BF16)
    emit_ln_to_fm(P, x2s, hn_ownD, esD, ident, ntiles=ONT)
    ccg = emit_gather_send(P, hn_ownD, gin, gout)
    P.barrier()
    esD.close()
    esDf = contextlib.ExitStack()
    hnD = P.sb(esDf, "D_hn", [128, 8, 4096], BF16, side="right")
    emit_gather_recv(P, hnD, gout, ccg)
    P.barrier()
    wd["hn_done"] = lambda: esDf.close()
    emit_midD(P, com, wd, hnD, qD, kD, vD, ypd)
    ccv = emit_rs_y(P, ypd, ysd)
    emit_back_AD(P, com, ysd, bo_d, x2s, x3s, ccv)
    esDR.close()
    P.barrier()

    emit_bodyE(P, com, we, x3s, xo)
    P.sec("y", lambda y: y.wait_all())
    esc.close()
    return P.finish()


def kernel_fused(inputs):
    x = inputs["L0_attn_x"]
    consts = _consts_A()
    cbd = _consts_BD()
    crep, srep = _rope_tables()
    nc = build_fused()
    in_maps = []
    for core in range(8):
        b, half = core // 2, core % 2
        h0 = half * 8
        sl4 = slice(h0 * 4, (h0 + 8) * 4)
        sl8 = slice(h0 * 8, (h0 + 8) * 8)
        sl64 = slice(h0 * 64, (h0 + 8) * 64)
        tok = slice(half * 2048, (half + 1) * 2048)
        m = dict(
            x_own=np.ascontiguousarray(x[b, tok]),
            a_wq1t=np.ascontiguousarray(inputs["L0_attn_Wq1"][sl4].T),
            a_wk1t=np.ascontiguousarray(inputs["L0_attn_Wk1"][sl4].T),
            a_wq2t=np.ascontiguousarray(inputs["L0_attn_Wq2"][sl8].T),
            a_wk2t=np.ascontiguousarray(inputs["L0_attn_Wk2"][sl8].T),
            a_wvt=np.ascontiguousarray(inputs["L0_attn_Wv"][sl64].T),
            a_wot=np.ascontiguousarray(inputs["L0_attn_Wo"].T[sl64]),
            a_bo=inputs["L0_attn_bo"].reshape(1, C),
            a_e1c=consts["e1c"], a_e2c=consts["e2c"],
            a_eh1c=consts["eh1c"], a_eh2c=consts["eh2c"],
            identc=consts["identc"], dmc=consts["dmc"], s4c=consts["s4c"],
            eh64c=cbd["eh64c"],
        )
        wq = inputs["L0_ffn_Wq"][:, h0 * 64:(h0 + 8) * 64]
        wqsw = wq.reshape(1024, 8, 2, 32)[:, :, ::-1, :].reshape(1024, 512)
        wkup = inputs["L0_ffn_Wk_up"][h0:h0 + 8].transpose(1, 0, 2).reshape(256, 512)
        wkupsw = wkup.reshape(256, 8, 2, 32)[:, :, ::-1, :].reshape(256, 512)
        wvup = inputs["L0_ffn_Wv_up"][h0:h0 + 8].transpose(1, 0, 2).reshape(256, 512)
        m.update(
            b_wq=np.ascontiguousarray(wq), b_wqsw=np.ascontiguousarray(wqsw),
            b_wkv=np.ascontiguousarray(inputs["L0_ffn_Wkv"]),
            b_wkup=np.ascontiguousarray(wkup), b_wkupsw=np.ascontiguousarray(wkupsw),
            b_wvup=np.ascontiguousarray(wvup),
            b_wot=np.ascontiguousarray(inputs["L0_ffn_Wo"][h0 * 64:(h0 + 8) * 64]),
            b_crepc=crep, b_srepc=srep, b_ln1w=inputs["L0_ffn_ln1_w"].reshape(1, C),
            c_ln2w=inputs["L0_ffn_ln2_w"].reshape(1, C),
            c_wr=np.ascontiguousarray(inputs["L0_ffn_Wr"]),
            c_rbias=inputs["L0_ffn_r_bias"].reshape(1, 8),
            c_sw1=np.ascontiguousarray(inputs["L0_ffn_sw1"]),
            c_sw2=np.ascontiguousarray(inputs["L0_ffn_sw2"]),
            c_sw3=np.ascontiguousarray(inputs["L0_ffn_sw3"]),
            c_ew1=np.ascontiguousarray(inputs["L0_ffn_ew1"]),
            c_ew2=np.ascontiguousarray(inputs["L0_ffn_ew2"]),
            c_ew3=np.ascontiguousarray(inputs["L0_ffn_ew3"]),
            d_wq=np.ascontiguousarray(inputs["L1_attn_Wq"][:, sl64]),
            d_wk=np.ascontiguousarray(inputs["L1_attn_Wkv"][:, 0:1024][:, sl64]),
            d_wv=np.ascontiguousarray(inputs["L1_attn_Wkv"][:, 1024:2048][:, sl64]),
            d_wot=np.ascontiguousarray(inputs["L1_attn_Wo"][sl64]),
            d_bo=inputs["L1_attn_bo"].reshape(1, C),
            e_gwt=np.ascontiguousarray(inputs["L1_ffn_gate_w"].T),
            e_gb=inputs["L1_ffn_gate_b"].reshape(1, 8),
            e_rmsw=inputs["L1_ffn_rms_w"].reshape(1, 8),
            e_w1=np.ascontiguousarray(inputs["L1_ffn_w1"]),
            e_b1=np.ascontiguousarray(inputs["L1_ffn_b1"].reshape(8, 16, 128).transpose(2, 0, 1)),
            e_w2=np.ascontiguousarray(inputs["L1_ffn_w2"]),
            e_b2=np.ascontiguousarray(inputs["L1_ffn_b2"]),
        )
        in_maps.append(m)
    res = run_bass_kernel_spmd(nc, in_maps, core_ids=list(range(8)))
    x4 = np.empty_like(x)
    for core in range(8):
        b, half = core // 2, core % 2
        x4[b, half * 2048:(half + 1) * 2048] = res.results[core]["xo"]
    return x4


# version 04
# speedup vs baseline: 1.1097x; 1.1097x over previous
import contextlib
import numpy as np
import ml_dtypes
import concourse.bass as bass
import concourse.mybir as mybir
from concourse.bass_utils import run_bass_kernel_spmd

F32 = mybir.dt.float32
BF16 = mybir.dt.bfloat16
AF = mybir.ActivationFunctionType
OP = mybir.AluOpType

B, N, C, H, HD = 4, 4096, 1024, 16, 64
NT = N // 128
ONT = 16
NH_OWN = 8
BF = ml_dtypes.bfloat16

ENGNAME = dict(t="tensor", v="vector", s="scalar", y="sync", g="gpsimd")
ESEM = dict(t="pe", v="ve", s="se", y="dy", g="dg")


class Sec:
    def __init__(self, P, eng, tag):
        self.P, self.e, self.tag = P, eng, tag
        self.last = None
        self.ndma = 0

    def wait(self, sem, val=None):
        P = self.P
        P.sem(sem)
        val = P.counts[sem] if val is None else val
        if val <= 0:
            return
        w = P.waited[self.tag]
        if w.get(sem, -1) < val:
            self.e.wait_ge(P.sems[sem], val)
            w[sem] = val

    def wait_all(self):
        for sname in list(self.P.sems):
            self.wait(sname)

    def edge(self):
        if self.last is not None:
            P, tag = self.P, self.tag
            sname = ESEM[tag] if tag not in ("y", "g") else "gc"
            self.last.then_inc(P.sem(sname), 1)
            P.counts[sname] += 1
            self.last = None
            self.wait(sname)

    def __getattr__(self, n):
        a = getattr(self.e, n)
        if not callable(a):
            return a
        P, tag = self.P, self.tag
        if n == "dma_start":
            def wd(*args, **kw):
                r = a(*args, **kw)
                qs = ESEM[tag]
                r.then_inc(P.sem(qs), 16)
                P.counts[qs] += 16
                self.ndma += 1
                return r
            return wd
        if n == "collective_compute":
            def wc(*args, **kw):
                r = a(*args, **kw)
                r.then_inc(P.sem("cc"), 1)
                P.counts["cc"] += 1
                return r
            return wc

        def w(*args, **kw):
            r = a(*args, **kw)
            if hasattr(r, "then_inc"):
                self.last = r
            return r
        return w


class Prog:

    def __init__(self, name="prog"):
        self.nc = bass.Bass("TRN2", target_bir_lowering=False)
        self.es = contextlib.ExitStack()
        self.block = self.es.enter_context(self.nc.Block())
        self.sems = {}
        self.counts = {}
        self.waited = {k: {} for k in ENGNAME}
        self.consts = {}
        self.ps = [self.nc.alloc_psum_tensor(f"psb{i}", [128, 512], F32)
                   for i in range(7)]
        self.pt = self.nc.alloc_psum_tensor("psbt", [128, 1024], BF16)
        self.carena = self.nc.alloc_sbuf_tensor("carena", [128, 16], F32)
        self._ccols = {}

    def sem(self, name):
        if name not in self.sems:
            self.sems[name] = self.es.enter_context(self.nc.semaphore(name))
            self.counts[name] = 0
        return self.sems[name]

    def sec(self, tag, fn):
        holder = {}

        def run(e):
            w = Sec(self, e, tag)
            fn(w)
            holder["w"] = w
        getattr(self.block, ENGNAME[tag])(run)
        w = holder["w"]
        sname = ESEM[tag]
        if tag in ("y", "g"):
            if tag == "g" and w.last is not None:
                self.sem("gc")
                w.last.then_inc(self.sems["gc"], 1)
                self.counts["gc"] += 1
                return self.counts["gc"]
            return self.counts[sname]
        if w.last is not None:
            w.last.then_inc(self.sem(sname), 1)
            self.counts[sname] += 1
        return self.counts[sname]

    def barrier(self):
        for tag in ENGNAME:
            self.sec(tag, lambda w: w.wait_all())

    def val(self, sem):
        return self.counts.get(sem, 0)

    def inp(self, name, shape, dtype=F32):
        return self.nc.dram_tensor(name, list(shape), dtype, kind="ExternalInput").ap()

    def outp(self, name, shape, dtype=F32):
        return self.nc.dram_tensor(name, list(shape), dtype, kind="ExternalOutput").ap()

    def dram(self, name, shape, dtype=F32):
        return self.nc.dram_tensor(name, list(shape), dtype, kind="Internal").ap()

    def sb(self, stack, name, shape, dtype, side=None):
        self._n = getattr(self, "_n", 0) + 1
        return stack.enter_context(
            self.nc.sbuf_tensor(f"{name}_{self._n}", list(shape), dtype, side=side))

    def ccol(self, value, rows=128):
        key = float(value)
        if key not in self._ccols:
            slot = len(self._ccols)
            assert slot < 16
            ap = self.carena[:, slot:slot + 1]
            self.sec("g", lambda g: g.memset(ap, float(value)))
            self._ccols[key] = slot
        return self.carena[0:rows, self._ccols[key]:self._ccols[key] + 1]

    def finish(self):
        self.es.close()
        return self.nc



def emit_norm(P, xt, out, scr, *, eps, kind="ln", gain_bc=None, n=1024.0):
    st = scr["st"]
    sq = scr["sq"]
    F = 1024 if n == 1024.0 else int(n)
    inv = 1.0 / n
    c_se = P.sec("s", lambda s: (
        s.wait("ve"), s.wait("dy"), s.wait("dg"),
        s.activation(sq[:, 0:F], xt, AF.Square)))

    def vstats(v):
        v.wait("se", c_se)
        v.reduce_sum(st[:, 0:1], sq[:, 0:F], axis=mybir.AxisListType.X)
        if kind == "ln":
            v.reduce_sum(st[:, 1:2], xt, axis=mybir.AxisListType.X)
            v.edge()
            v.tensor_scalar(out=st[:, 2:3], in0=st[:, 1:2], scalar1=-inv,
                            scalar2=None, op0=OP.mult)
            v.tensor_scalar(out=st[:, 3:4], in0=st[:, 0:1], scalar1=inv,
                            scalar2=None, op0=OP.mult)
            v.edge()
            v.tensor_mul(st[:, 4:5], st[:, 2:3], st[:, 2:3])
            v.edge()
            v.tensor_sub(st[:, 5:6], st[:, 3:4], st[:, 4:5])
        else:
            v.edge()
            v.tensor_scalar(out=st[:, 5:6], in0=st[:, 0:1], scalar1=inv,
                            scalar2=None, op0=OP.mult)
    c_ve = P.sec("v", vstats)
    epsc = P.ccol(eps)
    c_se2 = P.sec("s", lambda s: (
        s.wait("ve", c_ve), s.wait("gc"),
        s.activation(st[:, 6:7], st[:, 5:6], AF.Ln, bias=epsc),
        s.edge(),
        s.activation(st[:, 7:8], st[:, 6:7], AF.Exp, scale=-0.5)))

    def vapply(v):
        v.wait("se", c_se2)
        if kind == "ln":
            v.tensor_scalar(out=out, in0=xt, scalar1=st[:, 2:3],
                            scalar2=st[:, 7:8], op0=OP.add, op1=OP.mult)
        else:
            v.tensor_scalar(out=out, in0=xt, scalar1=st[:, 7:8],
                            scalar2=None, op0=OP.mult)
            if gain_bc is not None:
                v.edge()
                v.tensor_mul(out, out, gain_bc)
    c_vap = P.sec("v", vapply)
    return c_vap


def emit_transpose_tile(P, src_bf, dst_fm, tt, c_ready, ident):
    for half in range(2):
        def tfn(t, half=half):
            t.wait("dg")
            t.wait("ve", c_ready)
            t.wait("ve")
            for i in range(4):
                c = half * 4 + i
                t.transpose(P.pt[:, i * 128:(i + 1) * 128],
                            src_bf[:, c * 128:(c + 1) * 128], ident[:, :])
        cpe = P.sec("t", tfn)

        def vfn(v, half=half):
            v.wait("pe", cpe)
            v.tensor_copy(dst_fm[:, half * 4:(half + 1) * 4,
                                 tt * 128:(tt + 1) * 128],
                          P.pt[:, 0:512].rearrange("p (i f) -> p i f", i=4))
        P.sec("v", vfn)


def mm_chain(t, psum_ap, pairs, tile_position=None):
    nchunks = len(pairs)
    for i, (l, r) in enumerate(pairs):
        kw = {}
        if tile_position is not None:
            kw["tile_position"] = tile_position
        t.matmul(psum_ap, l, r, start=(i == 0), stop=(i == nchunks - 1), **kw)



def emit_ln_to_fm(P, xb, hn, es1, ident, ntiles=NT, gain_bc=None, rms_eps=None):
    xt2 = P.sb(es1, "lnf_xt2", [128, 2, 1024], F32)
    hnt = P.sb(es1, "lnf_hnt", [128, 1024], BF16)
    ht = P.sb(es1, "lnf_ht", [128, 1024], F32) if rms_eps is not None else None
    sq = P.sb(es1, "lnf_sq", [128, 1024], F32)
    st = P.sb(es1, "lnf_st", [128, 8], F32)
    scr = dict(sq=sq, st=st)
    for tt in range(ntiles):
        buf = tt % 2
        P.sec("y", lambda y, tt=tt, buf=buf: (
            y.wait("ve"), y.wait("se"),
            y.dma_start(out=xt2[:, buf, :], in_=xb[tt * 128:(tt + 1) * 128, :])))
        if rms_eps is None:
            c = emit_norm(P, xt2[:, buf, :], hnt[:, :], scr, eps=1e-5, kind="ln")
        else:
            c0 = emit_norm(P, xt2[:, buf, :], ht[:, :], scr, eps=1e-5, kind="ln")
            c = emit_norm(P, ht[:, :], hnt[:, :], scr, eps=rms_eps, kind="rms",
                          gain_bc=gain_bc)
        emit_transpose_tile(P, hnt, hn, tt, c, ident)


def emit_partial_y(P, av, wot, yp, es5):
    w_o = P.sb(es5, "w_o", [128, 4, 1024], BF16)
    yt = P.sb(es5, "yt", [128, 2, 1024], F32)
    P.sec("g", lambda g: g.dma_start(out=w_o[:, :, :],
                                     in_=wot.rearrange("(c p) n -> p c n", p=128)))
    for tt in range(NT):
        buf = tt % 2
        def yfn(t, tt=tt):
            t.wait("dg")
            t.wait("ve")
            for j in range(2):
                mm_chain(t, P.ps[tt % 2 * 2 + j][:, :],
                         [(av[:, cch, tt * 128:(tt + 1) * 128],
                           w_o[:, cch, j * 512:(j + 1) * 512])
                          for cch in range(4)])
        cpe = P.sec("t", yfn)

        def evfn(v, tt=tt, buf=buf):
            v.wait("pe", cpe)
            v.wait("dy")
            v.tensor_copy(yt[:, buf, 0:512], P.ps[tt % 2 * 2][:, :])
            v.tensor_copy(yt[:, buf, 512:1024], P.ps[tt % 2 * 2 + 1][:, :])
        cv = P.sec("v", evfn)
        P.sec("y", lambda y, tt=tt, buf=buf: (
            y.wait("ve", cv),
            y.dma_start(out=yp[tt * 128:(tt + 1) * 128, :], in_=yt[:, buf, :])))


def emit_btab(P, nq, nk, btabs, btab_dram, btabr, bbc, onesr, lnscale_val, st8):
    def vbt(v):
        v.wait("se"); v.wait("ve"); v.wait("pe")
        v.reduce_max(st8[0:8, 0:1], nk, axis=mybir.AxisListType.X)
        for gg in range(8):
            v.reduce_max(btabs[:, gg:gg + 1], nq[0:8, gg * 512:(gg + 1) * 512],
                         axis=mybir.AxisListType.X)
        v.edge()
        v.tensor_scalar(out=btabs[:, :], in0=btabs[:, :], scalar1=st8[0:8, 0:1],
                        scalar2=None, op0=OP.mult)
    cbt = P.sec("v", vbt)
    guardc = P.ccol(1e-30, rows=8)
    lnsc = P.ccol(lnscale_val, rows=8)
    cbt2 = P.sec("s", lambda s: (
        s.wait("ve", cbt), s.wait("gc"),
        s.activation(btabs[:, :], btabs[:, :], AF.Ln, bias=guardc),
        s.edge(),
        s.activation(btabs[:, :], btabs[:, :], AF.Exp, scale=0.5, bias=lnsc),
        s.edge(),
        s.mul(btabs[:, :], btabs[:, :], -1.0)))
    P.sec("y", lambda y: (y.wait("se", cbt2),
                          y.dma_start(out=btab_dram.rearrange("o (g h) -> (o h) g", h=8),
                                      in_=btabs[:, :], allow_slow_non_contiguous=True)))
    cr = P.val("dy")
    P.sec("y", lambda y: (y.wait("dy", cr),
                          y.dma_start(out=btabr[:, :], in_=btab_dram)))
    btr_bf = btabr
    cbf = P.val("dy")
    cpe = P.sec("t", lambda t: (
        t.wait("dy", cbf), t.wait("gc"), t.wait("ve"),
        mm_chain(t, P.ps[3][:, 0:64], [(onesr[:, :], btabr[:, :])])))
    P.sec("v", lambda v: (v.wait("pe", cpe),
                          v.tensor_copy(bbc.ap().rearrange("p g h -> p (g h)"),
                                        P.ps[3][:, 0:64])))


def build_stageA1(debug=False):
    P = Prog("A1")
    nc = P.nc
    xb = P.inp("xb", [N, C])
    wq1t = P.inp("wq1t", [C, 32])
    wk1t = P.inp("wk1t", [C, 32])
    wq2t = P.inp("wq2t", [C, 64])
    wk2t = P.inp("wk2t", [C, 64])
    wvt = P.inp("wvt", [C, 512])
    wot = P.inp("wot", [512, C])
    identc = P.inp("identc", [128, 128])
    e1c = P.inp("e1c", [32, 2, 128])
    e2c = P.inp("e2c", [64, 2, 128])
    eh1c = P.inp("eh1c", [32, 8])
    eh2c = P.inp("eh2c", [64, 8])
    dmc = P.inp("dmc", [128, 4, 512])
    s4c = P.inp("s4c", [128, 2, 128])
    yp = P.outp("yp", [N, C])
    if debug:
        dbg_bt = P.outp("dbg_bt", [8, 8])
        dbg_qt = P.outp("dbg_qt", [128, 2 * 4096])
        dbg_kt = P.outp("dbg_kt", [128, 2 * 4096])
        dbg_av = P.outp("dbg_av", [128, 4 * 4096])
        dbg_nrm = P.outp("dbg_nrm", [8, 2 * 4096])
        dbg_hn = P.outp("dbg_hn", [128, 8 * 4096])

    btab_dram = P.dram("btab_dram", [1, 64])

    es0 = contextlib.ExitStack()
    ident = P.sb(es0, "ident", [128, 128], BF16)
    e1 = P.sb(es0, "e1", [32, 2, 128], BF16)
    e2 = P.sb(es0, "e2", [64, 2, 128], BF16)
    eh1 = P.sb(es0, "eh1", [32, 8], BF16)
    eh2 = P.sb(es0, "eh2", [64, 8], BF16)
    dm = P.sb(es0, "dm", [128, 4, 512], BF16)
    s4 = P.sb(es0, "s4", [128, 2, 128], F32)
    w_q1 = P.sb(es0, "w_q1", [128, 8, 32], BF16)
    w_k1 = P.sb(es0, "w_k1", [128, 8, 32], BF16)
    w_q2 = P.sb(es0, "w_q2", [128, 8, 64], BF16)
    w_k2 = P.sb(es0, "w_k2", [128, 8, 64], BF16)
    w_v = P.sb(es0, "w_v", [128, 8, 512], BF16)
    kt = P.sb(es0, "kt", [128, 2, 4096], BF16)
    qt = P.sb(es0, "qt", [128, 2, 4096], BF16)
    bbc = P.sb(es0, "bbc", [128, 8, 8], F32)

    def loads(g):
        g.dma_start(out=ident[:, :], in_=identc)
        g.dma_start(out=e1[:, :, :], in_=e1c)
        g.dma_start(out=e2[:, :, :], in_=e2c)
        g.dma_start(out=eh1[:, :], in_=eh1c)
        g.dma_start(out=eh2[:, :], in_=eh2c)
        g.dma_start(out=dm[:, :, :], in_=dmc)
        g.dma_start(out=s4[:, :, :], in_=s4c)
        g.dma_start(out=w_q1[:, :, :], in_=wq1t.rearrange("(c p) n -> p c n", p=128))
        g.dma_start(out=w_k1[:, :, :], in_=wk1t.rearrange("(c p) n -> p c n", p=128))
        g.dma_start(out=w_q2[:, :, :], in_=wq2t.rearrange("(c p) n -> p c n", p=128))
        g.dma_start(out=w_k2[:, :, :], in_=wk2t.rearrange("(c p) n -> p c n", p=128))
        g.dma_start(out=w_v[:, :, :], in_=wvt.rearrange("(c p) n -> p c n", p=128))
    P.sec("g", loads)

    es1 = contextlib.ExitStack()
    hn = P.sb(es1, "hn", [128, 8, 4096], BF16)
    xt2 = P.sb(es1, "xt2", [128, 2, 1024], F32)
    hnt = P.sb(es1, "hnt", [128, 1024], BF16)
    sq = P.sb(es1, "sq", [128, 1024], F32)
    st = P.sb(es1, "st", [128, 8], F32)
    scr = dict(sq=sq, st=st)
    for tt in range(NT):
        buf = tt % 2
        P.sec("y", lambda y, tt=tt, buf=buf: (
            y.wait("ve"),
            y.dma_start(out=xt2[:, buf, :], in_=xb[tt * 128:(tt + 1) * 128, :])))
        cld = P.val("dy")
        P.sec("v", lambda v: v.wait("dy", cld))
        c = emit_norm(P, xt2[:, buf, :], hnt[:, :], scr, eps=1e-5, kind="ln")
        emit_transpose_tile(P, hnt, hn, tt, c, ident)

    es2 = contextlib.ExitStack()
    q1s = P.sb(es2, "q1s", [32, 4096], BF16)
    k1s = P.sb(es2, "k1s", [32, 4096], BF16)
    q2s = P.sb(es2, "q2s", [64, 4096], BF16)
    k2s = P.sb(es2, "k2s", [64, 4096], BF16)
    nrm = P.sb(es2, "nrm", [8, 2, 4096], BF16)
    pp1 = P.sb(es2, "pp1", [128, 512], F32)
    pp2 = P.sb(es2, "pp2", [128, 512], F32)
    btabs = P.sb(es2, "btabs", [8, 8], F32)
    btabr = P.sb(es2, "btabr", [1, 64], F32)

    def proj(dst, wsb, m):
        for gg in range(8):
            cpe = P.sec("t", lambda t, gg=gg: (
                t.wait("gc"), t.wait("dg"), t.wait("ve"),
                mm_chain(t, P.ps[0][0:m, 0:512],
                         [(wsb[:, cch, :], hn[:, cch, gg * 512:(gg + 1) * 512])
                          for cch in range(8)])))
            P.sec("v", lambda v, gg=gg: (
                v.wait("pe", cpe),
                v.tensor_copy(dst[:, gg * 512:(gg + 1) * 512], P.ps[0][0:m, 0:512])))
    proj(q1s, w_q1, 32)
    proj(k1s, w_k1, 32)
    proj(q2s, w_q2, 64)
    proj(k2s, w_k2, 64)

    sq1 = P.sb(es2, "sq1", [32, 4096], BF16)
    sq2 = P.sb(es2, "sq2", [64, 4096], BF16)

    def emit_pairnorm(srcA, srcB, slot):
        cs = P.sec("s", lambda s: (
            s.wait("ve"),
            s.activation(sq1[:, :], srcA, AF.Square),
            s.activation(sq2[:, :], srcB, AF.Square)))
        for gg in range(8):
            cpe = P.sec("t", lambda t, gg=gg: (
                t.wait("se", cs), t.wait("ve"),
                mm_chain(t, P.ps[1][0:8, 0:512],
                         [(eh1[:, :], sq1[:, gg * 512:(gg + 1) * 512])]),
                mm_chain(t, P.ps[2][0:8, 0:512],
                         [(eh2[:, :], sq2[:, gg * 512:(gg + 1) * 512])])))
            P.sec("v", lambda v, gg=gg: (
                v.wait("pe", cpe),
                v.tensor_copy(pp1[0:8, :], P.ps[1][0:8, 0:512]),
                v.edge(),
                v.tensor_mul(nrm[:, slot, gg * 512:(gg + 1) * 512],
                             pp1[0:8, :], P.ps[2][0:8, 0:512])))
    emit_pairnorm(q1s[:, :], q2s[:, :], 0)
    emit_pairnorm(k1s[:, :], k2s[:, :], 1)
    SCALE = float((4 * 8) ** (-0.5))

    def vbt(v):
        v.wait("se")
        v.reduce_max(st[0:8, 0:1], nrm[:, 1, :], axis=mybir.AxisListType.X)
        for gg in range(8):
            v.reduce_max(btabs[:, gg:gg + 1], nrm[:, 0, gg * 512:(gg + 1) * 512],
                         axis=mybir.AxisListType.X)
        v.edge()
        v.tensor_scalar(out=btabs[:, :], in0=btabs[:, :], scalar1=st[0:8, 0:1],
                        scalar2=None, op0=OP.mult)
    cbt = P.sec("v", vbt)
    guardc = P.ccol(1e-30, rows=8)
    lnsc = P.ccol(float(np.log(SCALE)), rows=8)
    cbt2 = P.sec("s", lambda s: (
        s.wait("ve", cbt), s.wait("gc"),
        s.activation(btabs[:, :], btabs[:, :], AF.Ln, bias=guardc),
        s.edge(),
        s.activation(btabs[:, :], btabs[:, :], AF.Exp, scale=0.5,
                     bias=lnsc),
        s.edge(),
        s.mul(btabs[:, :], btabs[:, :], -1.0)))
    P.sec("y", lambda y: (y.wait("se", cbt2),
                          y.dma_start(out=btab_dram.rearrange("o (g h) -> (o h) g", h=8),
                                      in_=btabs[:, :], allow_slow_non_contiguous=True)))
    cr = P.val("dy")
    P.sec("y", lambda y: (y.wait("dy", cr),
                          y.dma_start(out=btabr[:, :], in_=btab_dram)))
    onesr = P.sb(es2, "onesr", [1, 128], BF16)
    onescol = P.sb(es0, "onescol", [128, 1], BF16, side="right")
    P.sec("g", lambda g: (g.memset(onesr[:, :], 1.0), g.memset(onescol[:, :], 1.0)))
    btr_bf = P.sb(es2, "btr_bf", [1, 64], BF16)
    cbf = P.sec("v", lambda v: (v.wait("dy"), v.wait("gc"),
                                v.tensor_copy(btr_bf[:, :], btabr[:, :])))
    cpe = P.sec("t", lambda t: (
        t.wait("ve", cbf),
        mm_chain(t, P.ps[3][:, 0:64], [(onesr[:, :], btr_bf[:, :])])))
    P.sec("v", lambda v: (v.wait("pe", cpe),
                          v.tensor_copy(bbc.ap().rearrange("p g h -> p (g h)"),
                                        P.ps[3][:, 0:64])))

    for cc in range(2):
        for gg in range(8):
            def tfn(t, cc=cc, gg=gg):
                t.wait("ve")
                mm_chain(t, P.ps[0][:, :], [(e1[:, cc, :], q1s[:, gg * 512:(gg + 1) * 512])])
                mm_chain(t, P.ps[1][:, :], [(e2[:, cc, :], q2s[:, gg * 512:(gg + 1) * 512])])
                mm_chain(t, P.ps[2][:, :], [(e1[:, cc, :], k1s[:, gg * 512:(gg + 1) * 512])])
                mm_chain(t, P.ps[3][:, :], [(e2[:, cc, :], k2s[:, gg * 512:(gg + 1) * 512])])
            cpe = P.sec("t", tfn)

            def vfn(v, cc=cc, gg=gg):
                v.wait("pe", cpe)
                v.tensor_copy(pp1[:, :], P.ps[0][:, :])
                v.tensor_copy(pp2[:, :], P.ps[2][:, :])
                v.edge()
                v.tensor_mul(qt[:, cc, gg * 512:(gg + 1) * 512], pp1[:, :], P.ps[1][:, :])
                v.tensor_mul(kt[:, cc, gg * 512:(gg + 1) * 512], pp2[:, :], P.ps[3][:, :])
            P.sec("v", vfn)
    if debug:
        P.sec("g", lambda g: (g.wait_all(),
                              g.dma_start(out=dbg_qt, in_=qt.ap().rearrange("p a b -> p (a b)")),
                              g.dma_start(out=dbg_kt, in_=kt.ap().rearrange("p a b -> p (a b)")),
                              g.dma_start(out=dbg_nrm, in_=nrm.ap().rearrange("p a b -> p (a b)")),
                              g.dma_start(out=dbg_bt, in_=btabs[:, :])))
    es2.close()
    P.barrier()

    es3 = contextlib.ExitStack()
    vsb = P.sb(es3, "vsb", [128, 32, 512], BF16, side="right")
    for kc in range(NT):
        cpe = P.sec("t", lambda t, kc=kc: (
            t.wait("ve"),
            mm_chain(t, P.ps[kc % 2][:, :],
                     [(hn[:, cch, kc * 128:(kc + 1) * 128], w_v[:, cch, :])
                      for cch in range(8)])))
        P.sec("v", lambda v, kc=kc: (
            v.wait("pe", cpe),
            v.tensor_copy(vsb[:, kc, :], P.ps[kc % 2][:, :])))
    if debug:
        P.sec("g", lambda g: (g.wait_all(),
                              g.dma_start(out=dbg_hn, in_=hn.ap().rearrange("p a b -> p (a b)"))))
    es1.close()
    P.barrier()

    es_late = contextlib.ExitStack()
    av = P.sb(es_late, "av", [128, 4, 4096], BF16, side="right")
    es4 = contextlib.ExitStack()
    ptb = P.sb(es4, "ptb", [128, 4, 512], BF16)
    lsb = P.sb(es4, "lsb", [128, 512], F32)
    lbc = P.sb(es4, "lbc", [128, 2, 512], F32)
    for hq in range(2):
        for gg in range(8):
            nkc = 4 * gg + 4
            for kc in range(nkc):
                diag = kc - 4 * gg
                def sfn_t(t, hq=hq, gg=gg, kc=kc):
                    t.wait("se")
                    t.wait("ve")
                    for i in range(4):
                        t.matmul(P.ps[i][:, :],
                                 kt[32 * i:32 * (i + 1), hq, kc * 128:(kc + 1) * 128],
                                 qt[32 * i:32 * (i + 1), hq, gg * 512:(gg + 1) * 512],
                                 start=True, stop=True,
                                 tile_position=(32 * i, 0))
                cs_pe = P.sec("t", sfn_t)

                def efn(s, hq=hq, gg=gg, kc=kc):
                    s.wait("pe", cs_pe)
                    for i in range(4):
                        s.activation(ptb[:, i, :], P.ps[i][:, :], AF.Exp,
                                     bias=bbc[:, gg, hq * 4 + i:hq * 4 + i + 1], scale=SCALE)
                ce = P.sec("s", efn)
                cm = ce
                if diag >= 0:
                    def mfn(v, diag=diag):
                        v.wait("se", ce)
                        for i in range(4):
                            v.tensor_mul(ptb[:, i, :], ptb[:, i, :], dm[:, diag, :])
                    cm = P.sec("v", mfn)
                    dep = ("ve", cm)
                else:
                    dep = ("se", ce)

                def avfn(t, hq=hq, gg=gg, kc=kc, dep=dep, nkc=nkc):
                    t.wait(*dep)
                    t.wait("gc")
                    if kc == 0:
                        t.wait("ve")
                    st_ = (kc == 0)
                    sp_ = (kc == nkc - 1)
                    for pair in range(2):
                        for j in range(2):
                            hh = 2 * pair + j
                            t.matmul(P.ps[4 + pair][64 * j:64 * (j + 1), :],
                                     vsb[:, kc, (hq * 4 + hh) * 64:(hq * 4 + hh) * 64 + 64],
                                     ptb[:, hh, :],
                                     start=st_, stop=sp_,
                                     tile_position=(0, 64 * j))
                    for i in range(4):
                        t.matmul(P.ps[6][32 * i:32 * i + 1, :],
                                 onescol[:, :], ptb[:, i, :],
                                 start=st_, stop=sp_,
                                 tile_position=(0, 32 * i))
                P.sec("t", avfn)
            cev = P.sec("v", lambda v: (
                v.wait("pe"),
                v.tensor_copy(lsb[:, :], P.ps[6][:, :])))
            cbcp = P.sec("t", lambda t: (
                t.wait("ve", cev),
                mm_chain(t, P.ps[0][:, :], [(s4[:, 0, :], lsb[:, :])]),
                mm_chain(t, P.ps[1][:, :], [(s4[:, 1, :], lsb[:, :])])))

            crc = P.sec("v", lambda v: (
                v.wait("pe", cbcp),
                v.reciprocal(lbc[:, 0, :], P.ps[0][:, :]),
                v.reciprocal(lbc[:, 1, :], P.ps[1][:, :])))

            def vdiv(v, hq=hq, gg=gg):
                v.wait("ve", crc)
                for pair in range(2):
                    v.tensor_mul(av[:, hq * 2 + pair, gg * 512:(gg + 1) * 512],
                                 P.ps[4 + pair][:, :], lbc[:, pair, :])
            P.sec("v", vdiv)
    if debug:
        P.sec("g", lambda g: (g.wait_all(),
                              g.dma_start(out=dbg_av, in_=av.ap().rearrange("p a b -> p (a b)"))))
    es4.close()
    P.barrier()

    es5 = contextlib.ExitStack()
    w_o = P.sb(es5, "w_o", [128, 4, 1024], BF16)
    yt = P.sb(es5, "yt", [128, 2, 1024], F32)
    P.sec("g", lambda g: g.dma_start(out=w_o[:, :, :],
                                     in_=wot.rearrange("(c p) n -> p c n", p=128)))
    for tt in range(NT):
        buf = tt % 2
        def yfn(t, tt=tt):
            t.wait("dg")
            t.wait("ve")
            for j in range(2):
                mm_chain(t, P.ps[tt % 2 * 2 + j][:, :],
                         [(av[:, cch, tt * 128:(tt + 1) * 128],
                           w_o[:, cch, j * 512:(j + 1) * 512])
                          for cch in range(4)])
        cpe = P.sec("t", yfn)

        def evfn(v, tt=tt, buf=buf):
            v.wait("pe", cpe)
            v.wait("dy")
            v.tensor_copy(yt[:, buf, 0:512], P.ps[tt % 2 * 2][:, :])
            v.tensor_copy(yt[:, buf, 512:1024], P.ps[tt % 2 * 2 + 1][:, :])
        cv = P.sec("v", evfn)
        P.sec("y", lambda y, tt=tt, buf=buf: (
            y.wait("ve", cv),
            y.dma_start(out=yp[tt * 128:(tt + 1) * 128, :], in_=yt[:, buf, :])))
    P.sec("y", lambda y: y.wait_all())
    es5.close()
    es_late.close()
    es3.close()
    es0.close()
    return P.finish()



def build_stageA2():
    P = Prog("A2")
    x_own = P.inp("x_own", [2048, C])
    ypA = P.inp("ypA", [2048, C])
    ypB = P.inp("ypB", [2048, C])
    bo = P.inp("bo", [1, C])
    xo = P.outp("xo", [2048, C])

    es = contextlib.ExitStack()
    bob = P.sb(es, "bob", [128, 1024], F32)
    bor = P.sb(es, "bor", [1, 1024], F32)
    onesr = P.sb(es, "onesr", [1, 128], F32)
    xt = P.sb(es, "xt", [128, 2, 1024], F32)
    ya = P.sb(es, "ya", [128, 2, 1024], F32)
    yb = P.sb(es, "yb", [128, 2, 1024], F32)
    sq = P.sb(es, "sq", [128, 1024], F32)
    st = P.sb(es, "st", [128, 8], F32)
    scr = dict(sq=sq, st=st)

    P.sec("g", lambda g: (g.memset(onesr[:, :], 1.0)))
    P.sec("y", lambda y: y.dma_start(out=bor[:, :], in_=bo))
    cpe = P.sec("t", lambda t: (
        t.wait("gc"), t.wait("dy"),
        mm_chain(t, P.ps[0][:, :], [(onesr[:, :], bor[:, 0:512])]),
        mm_chain(t, P.ps[1][:, :], [(onesr[:, :], bor[:, 512:1024])])))
    P.sec("v", lambda v: (
        v.wait("pe", cpe),
        v.tensor_copy(bob[:, 0:512], P.ps[0][:, :]),
        v.tensor_copy(bob[:, 512:1024], P.ps[1][:, :])))

    for j in range(ONT):
        buf = j % 2
        P.sec("y", lambda y, j=j, buf=buf: (
            y.wait("ve"),
            y.dma_start(out=xt[:, buf, :], in_=x_own[j * 128:(j + 1) * 128, :]),
            y.dma_start(out=ya[:, buf, :], in_=ypA[j * 128:(j + 1) * 128, :]),
            y.dma_start(out=yb[:, buf, :], in_=ypB[j * 128:(j + 1) * 128, :])))
        cld = P.val("dy")

        def addf(v, buf=buf):
            v.wait("dy", cld)
            v.tensor_add(ya[:, buf, :], ya[:, buf, :], yb[:, buf, :])
            v.edge()
            v.tensor_add(ya[:, buf, :], ya[:, buf, :], bob[:, :])
        P.sec("v", addf)
        c = emit_norm(P, ya[:, buf, :], yb[:, buf, :], scr, eps=1e-5, kind="ln")

        def resid(v, buf=buf):
            v.wait("ve", c)
            v.tensor_add(xt[:, buf, :], xt[:, buf, :], yb[:, buf, :])
        cr = P.sec("v", resid)
        P.sec("y", lambda y, j=j, buf=buf: (
            y.wait("ve", cr),
            y.dma_start(out=xo[j * 128:(j + 1) * 128, :], in_=xt[:, buf, :])))
    P.sec("y", lambda y: y.wait_all())
    es.close()
    return P.finish()



def _consts_A():
    ident = np.eye(128, dtype=np.float32)
    e1 = np.zeros((32, 2, 128), np.float32)
    e2 = np.zeros((64, 2, 128), np.float32)
    for hl in range(8):
        cc, i = divmod(hl, 4)
        for r1 in range(4):
            for r2 in range(8):
                m = 32 * i + 8 * r1 + r2
                e1[4 * hl + r1, cc, m] = 1.0
                e2[8 * hl + r2, cc, m] = 1.0
    eh1 = np.zeros((32, 8), np.float32)
    eh2 = np.zeros((64, 8), np.float32)
    for hl in range(8):
        eh1[4 * hl:4 * hl + 4, hl] = 1.0
        eh2[8 * hl:8 * hl + 8, hl] = 1.0
    dm = np.zeros((128, 4, 512), np.float32)
    for di in range(4):
        d0 = -128 * di
        p = np.arange(128)[:, None]
        f = np.arange(512)[None, :]
        dm[:, di, :] = (p <= d0 + f).astype(np.float32)
    s4 = np.zeros((128, 2, 128), np.float32)
    s4[0, 0, 0:64] = 1.0
    s4[32, 0, 64:128] = 1.0
    s4[64, 1, 0:64] = 1.0
    s4[96, 1, 64:128] = 1.0
    return dict(identc=ident, e1c=e1, e2c=e2, eh1c=eh1, eh2c=eh2, dmc=dm, s4c=s4)


def run_stage_A(x, W):
    consts = _consts_A()
    ncA1 = build_stageA1()
    in_maps = []
    for core in range(8):
        b, half = core // 2, core % 2
        h0 = half * 8
        sl4 = slice(h0 * 4, (h0 + 8) * 4)
        sl8 = slice(h0 * 8, (h0 + 8) * 8)
        sl64 = slice(h0 * 64, (h0 + 8) * 64)
        m = dict(
            xb=np.ascontiguousarray(x[b]),
            wq1t=np.ascontiguousarray(W["Wq1"][sl4].T),
            wk1t=np.ascontiguousarray(W["Wk1"][sl4].T),
            wq2t=np.ascontiguousarray(W["Wq2"][sl8].T),
            wk2t=np.ascontiguousarray(W["Wk2"][sl8].T),
            wvt=np.ascontiguousarray(W["Wv"][sl64].T),
            wot=np.ascontiguousarray(W["Wo"].T[sl64]),
            **consts,
        )
        in_maps.append(m)
    res = run_bass_kernel_spmd(ncA1, in_maps, core_ids=list(range(8)))
    yps = [r["yp"] for r in res.results]

    ncA2 = build_stageA2()
    in_maps2 = []
    for core in range(8):
        b, half = core // 2, core % 2
        tok = slice(half * 2048, (half + 1) * 2048)
        in_maps2.append(dict(
            x_own=np.ascontiguousarray(x[b, tok]),
            ypA=np.ascontiguousarray(yps[2 * b][tok]),
            ypB=np.ascontiguousarray(yps[2 * b + 1][tok]),
            bo=W["bo"].reshape(1, C),
        ))
    res2 = run_bass_kernel_spmd(ncA2, in_maps2, core_ids=list(range(8)))
    x1 = np.empty_like(x)
    for core in range(8):
        b, half = core // 2, core % 2
        x1[b, half * 2048:(half + 1) * 2048] = res2.results[core]["xo"]
    return x1


def kernel(**inputs):
    inputs = {k: np.asarray(v, np.float32) for k, v in inputs.items()}
    x = inputs["L0_attn_x"]
    WA = {k.replace("L0_attn_", ""): inputs[k] for k in inputs
          if k.startswith("L0_attn_") and k != "L0_attn_x"}
    WB = {k.replace("L0_ffn_", ""): inputs[k] for k in inputs
          if k.startswith("L0_ffn_")}
    WD = {k.replace("L1_attn_", ""): inputs[k] for k in inputs
          if k.startswith("L1_attn_")}
    WE = {k.replace("L1_ffn_", ""): inputs[k] for k in inputs
          if k.startswith("L1_ffn_")}
    if True:
        return kernel_fused(inputs)
    x1 = run_stage_A(x, WA)
    a = run_stage_B(x1, WB)
    x2 = run_stage_C(a, x1, WB)
    x3 = run_stage_D(x2, WD)
    x4 = run_stage_E(x3, WE)
    return x4




def emit_pair_attention(P, *, k_all, q_all, vsb, av, bbc, dm, s4, onescol,
                        scale, causal, es4):
    ptb = P.sb(es4, "pa_ptb", [128, 2, 512], BF16)
    lsb = P.sb(es4, "pa_lsb", [128, 512], F32)
    lbc = P.sb(es4, "pa_lbc", [128, 512], F32)
    for p in range(4):
        for gg in range(8):
            nkc = 4 * gg + 4 if causal else 32
            for kc in range(nkc):
                diag = kc - 4 * gg if causal else -1
                def sfn(t, p=p, gg=gg, kc=kc):
                    t.wait("se"); t.wait("ve"); t.wait("gc"); t.wait("dg")
                    for j in range(2):
                        t.matmul(P.ps[j][:, :],
                                 k_all[64 * j:64 * (j + 1), p, kc * 128:(kc + 1) * 128],
                                 q_all[64 * j:64 * (j + 1), p, gg * 512:(gg + 1) * 512],
                                 start=True, stop=True, tile_position=(64 * j, 0))
                cs_pe = P.sec("t", sfn)

                def efn(s, p=p, gg=gg):
                    s.wait("pe", cs_pe)
                    for j in range(2):
                        s.activation(ptb[:, j, :], P.ps[j][:, :], AF.Exp,
                                     bias=bbc[:, j, p, gg:gg + 1],
                                     scale=scale)
                ce = P.sec("s", efn)
                if diag >= 0:
                    def mfn(v, diag=diag):
                        v.wait("se", ce)
                        for j in range(2):
                            v.tensor_mul(ptb[:, j, :], ptb[:, j, :], dm[:, diag, :])
                    cm = P.sec("v", mfn)
                    dep = ("ve", cm)
                else:
                    dep = ("se", ce)

                def avfn(t, p=p, kc=kc, dep=dep, nkc=nkc):
                    t.wait(*dep)
                    if kc == 0:
                        t.wait("ve")
                    st_, sp_ = (kc == 0), (kc == nkc - 1)
                    for j in range(2):
                        t.matmul(P.ps[4][64 * j:64 * (j + 1), :],
                                 vsb[:, kc, (2 * p + j) * 64:(2 * p + j) * 64 + 64],
                                 ptb[:, j, :], start=st_, stop=sp_,
                                 tile_position=(0, 64 * j))
                    for j in range(2):
                        t.matmul(P.ps[6][32 * j:32 * j + 1, :], onescol[:, :],
                                 ptb[:, j, :], start=st_, stop=sp_,
                                 tile_position=(0, 32 * j))
                P.sec("t", avfn)
            cev = P.sec("v", lambda v: (v.wait("pe"),
                                        v.tensor_copy(lsb[:, :], P.ps[6][:, :])))
            cbcp = P.sec("t", lambda t: (
                t.wait("ve", cev),
                mm_chain(t, P.ps[0][:, :], [(s4[:, 0, :], lsb[:, :])])))
            crc = P.sec("v", lambda v: (
                v.wait("pe", cbcp),
                v.reciprocal(lbc[:, :], P.ps[0][:, :])))

            def vdiv(v, p=p, gg=gg):
                v.wait("ve", crc)
                v.tensor_mul(av[:, p, gg * 512:(gg + 1) * 512],
                             P.ps[4][:, :], lbc[:, :])
            P.sec("v", vdiv)


def emit_pair_norms(P, src_all, dst, eh64, sqq):
    for p in range(4):
        cs = P.sec("s", lambda s, p=p: (
            s.wait("ve"), s.wait("pe"), s.wait("dg"),
            s.activation(sqq[:, :], src_all[:, p, :], AF.Square)))
        for gg in range(8):
            cpe = P.sec("t", lambda t, gg=gg: (
                t.wait("se", cs), t.wait("ve"), t.wait("gc"),
                mm_chain(t, P.ps[2][0:2, 0:512],
                         [(eh64[:, :], sqq[:, gg * 512:(gg + 1) * 512])])))
            P.sec("v", lambda v, p=p, gg=gg: (
                v.wait("pe", cpe),
                v.tensor_copy(dst[32 * p:32 * p + 2, gg * 512:(gg + 1) * 512],
                              P.ps[2][0:2, 0:512])))


def emit_btab2(P, nq, nk, bbc, onesr, lnscale_val, es):
    btabs = P.sb(es, "bt_btabs", [2, 4, 8], F32)
    mk = P.sb(es, "bt_mk", [2, 4], F32)
    btabr = P.sb(es, "bt_btabr", [1, 64], F32)
    btab_dram = P.dram(f"bt_dram{P.val('se')}", [64])

    def vbt(v):
        v.wait("se"); v.wait("ve"); v.wait("pe")
        for p in range(4):
            v.reduce_max(mk[0:2, p:p + 1], nk[32 * p:32 * p + 2, :],
                         axis=mybir.AxisListType.X)
            for gg in range(8):
                v.reduce_max(btabs[0:2, p, gg:gg + 1],
                             nq[32 * p:32 * p + 2, gg * 512:(gg + 1) * 512],
                             axis=mybir.AxisListType.X)
        v.edge()
        for p in range(4):
            v.tensor_scalar(out=btabs[0:2, p, :], in0=btabs[0:2, p, :],
                            scalar1=mk[0:2, p:p + 1], scalar2=None, op0=OP.mult)
    cbt = P.sec("v", vbt)
    guardc = P.ccol(1e-30, rows=2)
    lnsc = P.ccol(lnscale_val, rows=2)
    cbt2 = P.sec("s", lambda s: (
        s.wait("ve", cbt), s.wait("gc"),
        s.activation(btabs[:, :, :], btabs[:, :, :], AF.Ln, bias=guardc),
        s.edge(),
        s.activation(btabs[:, :, :], btabs[:, :, :], AF.Exp, scale=0.5, bias=lnsc),
        s.edge(),
        s.mul(btabs[:, :, :], btabs[:, :, :], -1.0)))
    P.sec("y", lambda y: (y.wait("se", cbt2),
                          y.dma_start(out=btab_dram.rearrange("(j p g) -> j p g", j=2, p=4),
                                      in_=btabs[:, :, :])))
    cr = P.val("dy")
    P.sec("y", lambda y: (y.wait("dy", cr),
                          y.dma_start(out=btabr[:, :],
                                      in_=btab_dram.rearrange("(o f) -> o f", o=1))))
    cl = P.val("dy")
    cpe = P.sec("t", lambda t: (
        t.wait("dy", cl), t.wait("gc"), t.wait("ve"),
        mm_chain(t, P.ps[3][:, 0:64], [(onesr[:, :], btabr[:, :])])))
    P.sec("v", lambda v: (v.wait("pe", cpe),
                          v.tensor_copy(bbc.ap().rearrange("p j q g -> p (j q g)"),
                                        P.ps[3][:, 0:64])))


def emit_row_bcast(P, row_in, dst, width, onesr, es):
    tmp = P.sb(es, f"rb{P.val('dy')}", [1, width], F32)
    P.sec("y", lambda y: y.dma_start(out=tmp[:, :], in_=row_in))
    cl = P.val("dy")
    nj = (width + 511) // 512
    cpe = P.sec("t", lambda t: (
        t.wait("dy", cl), t.wait("gc"), t.wait("ve"), t.wait("se"),
        [mm_chain(t, P.ps[j][:, 0:min(512, width - 512 * j)],
                  [(onesr[:, :], tmp[:, 512 * j:min(512 * (j + 1), width)])])
         for j in range(nj)]))
    P.sec("v", lambda v: (
        v.wait("pe", cpe),
        [v.tensor_copy(dst[:, 512 * j:min(512 * (j + 1), width)],
                       P.ps[j][:, 0:min(512, width - 512 * j)]) for j in range(nj)]))



def build_stageB1():
    P = Prog("B1")
    xb = P.inp("xb", [N, C])
    ln1w = P.inp("ln1w", [1, C])
    wq = P.inp("wq", [C, 512])
    wqsw = P.inp("wqsw", [C, 512])
    wkv = P.inp("wkv", [C, 256])
    wkup = P.inp("wkup", [256, 512])
    wkupsw = P.inp("wkupsw", [256, 512])
    wvup = P.inp("wvup", [256, 512])
    wot = P.inp("wot", [512, C])
    identc = P.inp("identc", [128, 128])
    dmc = P.inp("dmc", [128, 4, 512])
    s4c = P.inp("s4c", [128, 2, 128])
    eh64c = P.inp("eh64c", [128, 2])
    crepc = P.inp("crepc", [128, N])
    srepc = P.inp("srepc", [128, N])
    yp = P.outp("yp", [N, C])

    es0 = contextlib.ExitStack()
    ident = P.sb(es0, "ident", [128, 128], BF16)
    dm = P.sb(es0, "dm", [128, 4, 512], BF16)
    s4 = P.sb(es0, "s4", [128, 2, 128], F32)
    eh64 = P.sb(es0, "eh64", [128, 2], BF16)
    onesr = P.sb(es0, "onesr", [1, 128], F32)
    onescol = P.sb(es0, "onescol", [128, 1], BF16)
    w_q = P.sb(es0, "w_q", [128, 8, 512], BF16)
    w_qsw = P.sb(es0, "w_qsw", [128, 8, 512], BF16)
    w_kv = P.sb(es0, "w_kv", [128, 8, 256], BF16)
    ln1bc = P.sb(es0, "ln1bc", [128, 1024], F32)
    kv = P.sb(es0, "kv", [128, 2, 4096], BF16)
    bbc = P.sb(es0, "bbc", [128, 2, 4, 8], F32)
    rs2 = P.sb(es0, "rs2", [128, 512], F32)
    bufA = P.sb(es0, "bufA", [128, 4, 4096], BF16, side="right")
    bufB = P.sb(es0, "bufB", [128, 4, 4096], BF16, side="right")
    q_all = bufA
    k_all = bufB

    def loads(g):
        g.dma_start(out=ident[:, :], in_=identc)
        g.dma_start(out=dm[:, :, :], in_=dmc)
        g.dma_start(out=s4[:, :, :], in_=s4c)
        g.dma_start(out=eh64[:, :], in_=eh64c)
        g.dma_start(out=w_q[:, :, :], in_=wq.rearrange("(c p) n -> p c n", p=128))
        g.dma_start(out=w_qsw[:, :, :], in_=wqsw.rearrange("(c p) n -> p c n", p=128))
        g.dma_start(out=w_kv[:, :, :], in_=wkv.rearrange("(c p) n -> p c n", p=128))
        g.memset(onesr[:, :], 1.0)
        g.memset(onescol[:, :], 1.0)
    P.sec("g", loads)
    emit_row_bcast(P, ln1w, ln1bc, 1024, onesr, es0)

    es1 = contextlib.ExitStack()
    rms1 = P.sb(es1, "rms1", [128, 8, 4096], BF16)
    emit_ln_to_fm(P, xb, rms1, es1, ident, rms_eps=1e-6, gain_bc=ln1bc[:, :])

    def rawproj(dstA, dstB, wA, wB, src_fm, nchunk):
        for p in range(4):
            for gg in range(8):
                cpe = P.sec("t", lambda t, p=p, gg=gg: (
                    t.wait("ve"), t.wait("dg"), t.wait("gc"),
                    mm_chain(t, P.ps[0][:, :],
                             [(wA[:, cch, 128 * p:128 * (p + 1)],
                               src_fm[:, cch, gg * 512:(gg + 1) * 512])
                              for cch in range(nchunk)]),
                    mm_chain(t, P.ps[1][:, :],
                             [(wB[:, cch, 128 * p:128 * (p + 1)],
                               src_fm[:, cch, gg * 512:(gg + 1) * 512])
                              for cch in range(nchunk)])))
                P.sec("v", lambda v, p=p, gg=gg: (
                    v.wait("pe", cpe),
                    v.tensor_copy(dstA[:, p, gg * 512:(gg + 1) * 512], P.ps[0][:, :]),
                    v.tensor_copy(dstB[:, p, gg * 512:(gg + 1) * 512], P.ps[1][:, :])))
    rawproj(bufA, bufB, w_q, w_qsw, rms1, 8)

    for gg in range(8):
        cpe = P.sec("t", lambda t, gg=gg: (
            t.wait("ve"),
            mm_chain(t, P.ps[0][:, :],
                     [(w_kv[:, cch, 0:128], rms1[:, cch, gg * 512:(gg + 1) * 512])
                      for cch in range(8)]),
            mm_chain(t, P.ps[1][:, :],
                     [(w_kv[:, cch, 128:256], rms1[:, cch, gg * 512:(gg + 1) * 512])
                      for cch in range(8)])))
        P.sec("v", lambda v, gg=gg: (
            v.wait("pe", cpe),
            v.tensor_scalar(out=kv[:, 0, gg * 512:(gg + 1) * 512], in0=P.ps[0][:, :],
                            scalar1=0.25, scalar2=None, op0=OP.mult),
            v.tensor_scalar(out=kv[:, 1, gg * 512:(gg + 1) * 512], in0=P.ps[1][:, :],
                            scalar1=0.25, scalar2=None, op0=OP.mult)))
    es1.close()
    P.barrier()

    estr = contextlib.ExitStack()
    crep = P.sb(estr, "crep", [128, N], F32)
    srep = P.sb(estr, "srep", [128, N], F32)
    w_kup = P.sb(estr, "w_kup", [128, 2, 512], BF16)
    w_kupsw = P.sb(estr, "w_kupsw", [128, 2, 512], BF16)
    w_vup = P.sb(estr, "w_vup", [128, 2, 512], BF16)
    P.sec("g", lambda g: (
        g.dma_start(out=crep[:, :], in_=crepc),
        g.dma_start(out=srep[:, :], in_=srepc),
        g.dma_start(out=w_kup[:, :, :], in_=wkup.rearrange("(c p) n -> p c n", p=128)),
        g.dma_start(out=w_kupsw[:, :, :], in_=wkupsw.rearrange("(c p) n -> p c n", p=128)),
        g.dma_start(out=w_vup[:, :, :], in_=wvup.rearrange("(c p) n -> p c n", p=128))))

    def rope_inplace(dst, sw):
        for p in range(4):
            for gg in range(8):
                def vr(v, p=p, gg=gg):
                    v.wait("dg")
                    v.wait("pe")
                    sl = (slice(None), p, slice(gg * 512, (gg + 1) * 512))
                    v.tensor_mul(rs2[:, :], sw[sl], srep[:, gg * 512:(gg + 1) * 512])
                    v.edge()
                    v.tensor_mul(dst[sl], dst[sl], crep[:, gg * 512:(gg + 1) * 512])
                    v.edge()
                    v.tensor_add(dst[sl], dst[sl], rs2[:, :])
                P.sec("v", vr)
    rope_inplace(bufA, bufB)

    for p in range(4):
        for gg in range(8):
            cpe = P.sec("t", lambda t, p=p, gg=gg: (
                t.wait("ve"), t.wait("dg"), t.wait("gc"),
                mm_chain(t, P.ps[0][:, :],
                         [(w_kup[:, lc, 128 * p:128 * (p + 1)],
                           kv[:, lc, gg * 512:(gg + 1) * 512]) for lc in range(2)]),
                mm_chain(t, P.ps[1][:, :],
                         [(w_kupsw[:, lc, 128 * p:128 * (p + 1)],
                           kv[:, lc, gg * 512:(gg + 1) * 512]) for lc in range(2)])))

            def vr(v, p=p, gg=gg):
                v.wait("pe", cpe)
                v.tensor_mul(rs2[:, :], P.ps[1][:, :], srep[:, gg * 512:(gg + 1) * 512])
                v.tensor_mul(k_all[:, p, gg * 512:(gg + 1) * 512], P.ps[0][:, :],
                             crep[:, gg * 512:(gg + 1) * 512])
                v.edge()
                v.tensor_add(k_all[:, p, gg * 512:(gg + 1) * 512],
                             k_all[:, p, gg * 512:(gg + 1) * 512], rs2[:, :])
            P.sec("v", vr)

    bufC = P.sb(es0, "bufC", [128, 32, 512], BF16, side="right")
    vsb = bufC
    for kc in range(NT):
        cpe = P.sec("t", lambda t, kc=kc: (
            t.wait("ve"),
            mm_chain(t, P.ps[kc % 2][:, :],
                     [(kv[:, lc, kc * 128:(kc + 1) * 128], w_vup[:, lc, :])
                      for lc in range(2)])))
        P.sec("v", lambda v, kc=kc: (
            v.wait("pe", cpe),
            v.tensor_copy(vsb[:, kc, :], P.ps[kc % 2][:, :])))
    estr.close()
    P.barrier()

    es3 = contextlib.ExitStack()
    nq = P.sb(es3, "nq", [128, 4096], BF16)
    nk = P.sb(es3, "nk", [128, 4096], BF16)
    sqq = P.sb(es3, "sqq", [128, 4096], BF16)
    emit_pair_norms(P, q_all, nq[:, :], eh64, sqq)
    emit_pair_norms(P, k_all, nk[:, :], eh64, sqq)
    emit_btab2(P, nq[:, :], nk[:, :], bbc, onesr, float(np.log(0.125)), es3)
    es3.close()
    P.barrier()

    es4 = contextlib.ExitStack()
    av = P.sb(es4, "av", [128, 4, 4096], BF16, side="right")
    emit_pair_attention(P, k_all=k_all, q_all=q_all, vsb=vsb, av=av,
                        bbc=bbc, dm=dm, s4=s4, onescol=onescol,
                        scale=0.125, causal=True, es4=es4)
    P.barrier()

    es5 = contextlib.ExitStack()
    emit_partial_y(P, av, wot, yp, es5)
    P.sec("y", lambda y: y.wait_all())
    es5.close()
    es4.close()
    es0.close()
    return P.finish()



def build_stageB2():
    P = Prog("B2")
    x_own = P.inp("x_own", [2048, C])
    ypA = P.inp("ypA", [2048, C])
    ypB = P.inp("ypB", [2048, C])
    ao = P.outp("ao", [2048, C])

    es = contextlib.ExitStack()
    xt = P.sb(es, "xt", [128, 2, 1024], F32)
    ya = P.sb(es, "ya", [128, 2, 1024], F32)
    yb = P.sb(es, "yb", [128, 2, 1024], F32)
    ht = P.sb(es, "ht", [128, 1024], F32)
    sq = P.sb(es, "sq", [128, 1024], F32)
    st = P.sb(es, "st", [128, 8], F32)
    scr = dict(sq=sq, st=st)
    for j in range(ONT):
        buf = j % 2
        P.sec("y", lambda y, j=j, buf=buf: (
            y.wait("ve"), y.wait("se"),
            y.dma_start(out=xt[:, buf, :], in_=x_own[j * 128:(j + 1) * 128, :]),
            y.dma_start(out=ya[:, buf, :], in_=ypA[j * 128:(j + 1) * 128, :]),
            y.dma_start(out=yb[:, buf, :], in_=ypB[j * 128:(j + 1) * 128, :])))
        cld = P.val("dy")
        c = emit_norm(P, xt[:, buf, :], ht[:, :], scr, eps=1e-5, kind="ln")

        def addf(v, buf=buf):
            v.wait("dy", cld)
            v.wait("ve", c)
            v.tensor_add(ya[:, buf, :], ya[:, buf, :], yb[:, buf, :])
            v.edge()
            v.tensor_add(ya[:, buf, :], ya[:, buf, :], ht[:, :])
        cr = P.sec("v", addf)
        P.sec("y", lambda y, j=j, buf=buf: (
            y.wait("ve", cr),
            y.dma_start(out=ao[j * 128:(j + 1) * 128, :], in_=ya[:, buf, :])))
    P.sec("y", lambda y: y.wait_all())
    es.close()
    return P.finish()



def build_stageD1():
    P = Prog("D1")
    xb = P.inp("xb", [N, C])
    wq = P.inp("wq", [C, 512])
    wk = P.inp("wk", [C, 512])
    wv = P.inp("wv", [C, 512])
    wot = P.inp("wot", [512, C])
    identc = P.inp("identc", [128, 128])
    s4c = P.inp("s4c", [128, 2, 128])
    eh64c = P.inp("eh64c", [128, 2])
    yp = P.outp("yp", [N, C])

    es0 = contextlib.ExitStack()
    ident = P.sb(es0, "ident", [128, 128], BF16)
    s4 = P.sb(es0, "s4", [128, 2, 128], F32)
    eh64 = P.sb(es0, "eh64", [128, 2], BF16)
    onesr = P.sb(es0, "onesr", [1, 128], F32)
    onescol = P.sb(es0, "onescol", [128, 1], BF16)
    w_q = P.sb(es0, "w_q", [128, 8, 512], BF16)
    w_k = P.sb(es0, "w_k", [128, 8, 512], BF16)
    w_v = P.sb(es0, "w_v", [128, 8, 512], BF16)
    bbc = P.sb(es0, "bbc", [128, 2, 4, 8], F32)
    q_all = P.sb(es0, "q_all", [128, 4, 4096], BF16, side="right")
    k_all = P.sb(es0, "k_all", [128, 4, 4096], BF16, side="right")
    vsb = P.sb(es0, "vsb", [128, 32, 512], BF16, side="right")

    def loads(g):
        g.dma_start(out=ident[:, :], in_=identc)
        g.dma_start(out=s4[:, :, :], in_=s4c)
        g.dma_start(out=eh64[:, :], in_=eh64c)
        g.dma_start(out=w_q[:, :, :], in_=wq.rearrange("(c p) n -> p c n", p=128))
        g.dma_start(out=w_k[:, :, :], in_=wk.rearrange("(c p) n -> p c n", p=128))
        g.dma_start(out=w_v[:, :, :], in_=wv.rearrange("(c p) n -> p c n", p=128))
        g.memset(onesr[:, :], 1.0)
        g.memset(onescol[:, :], 1.0)
    P.sec("g", loads)

    es1 = contextlib.ExitStack()
    hn = P.sb(es1, "hn", [128, 8, 4096], BF16)
    emit_ln_to_fm(P, xb, hn, es1, ident)

    def proj_pair(dst_all, wsb):
        for p in range(4):
            for gg in range(8):
                cpe = P.sec("t", lambda t, p=p, gg=gg: (
                    t.wait("ve"), t.wait("dg"), t.wait("gc"),
                    mm_chain(t, P.ps[0][:, :],
                             [(wsb[:, cch, 128 * p:128 * (p + 1)],
                               hn[:, cch, gg * 512:(gg + 1) * 512])
                              for cch in range(8)])))
                P.sec("v", lambda v, p=p, gg=gg: (
                    v.wait("pe", cpe),
                    v.tensor_copy(dst_all[:, p, gg * 512:(gg + 1) * 512],
                                  P.ps[0][:, :])))
    proj_pair(q_all, w_q)
    proj_pair(k_all, w_k)
    for kc in range(NT):
        cpe = P.sec("t", lambda t, kc=kc: (
            t.wait("ve"),
            mm_chain(t, P.ps[kc % 2][:, :],
                     [(hn[:, cch, kc * 128:(kc + 1) * 128], w_v[:, cch, :])
                      for cch in range(8)])))
        P.sec("v", lambda v, kc=kc: (
            v.wait("pe", cpe),
            v.tensor_copy(vsb[:, kc, :], P.ps[kc % 2][:, :])))
    es1.close()
    P.barrier()

    nq = P.sb(es0, "nq", [128, 4096], BF16, side="right")
    nk = P.sb(es0, "nk", [128, 4096], BF16, side="right")
    es3 = contextlib.ExitStack()
    sqq = P.sb(es3, "sqq", [128, 4096], BF16)
    emit_pair_norms(P, q_all, nq[:, :], eh64, sqq)
    emit_pair_norms(P, k_all, nk[:, :], eh64, sqq)
    emit_btab2(P, nq[:, :], nk[:, :], bbc, onesr, float(np.log(0.125)), es3)
    es3.close()
    P.barrier()

    es4 = contextlib.ExitStack()
    av = P.sb(es4, "av", [128, 4, 4096], BF16, side="right")
    emit_pair_attention(P, k_all=k_all, q_all=q_all, vsb=vsb, av=av,
                        bbc=bbc, dm=None, s4=s4, onescol=onescol,
                        scale=0.125, causal=False, es4=es4)
    P.barrier()

    es5 = contextlib.ExitStack()
    emit_partial_y(P, av, wot, yp, es5)
    P.sec("y", lambda y: y.wait_all())
    es5.close()
    es4.close()
    es0.close()
    return P.finish()


def _consts_BD():
    ident = np.eye(128, dtype=np.float32)
    dm = np.zeros((128, 4, 512), np.float32)
    for di in range(4):
        d0 = -128 * di
        p = np.arange(128)[:, None]
        f = np.arange(512)[None, :]
        dm[:, di, :] = (p <= d0 + f).astype(np.float32)
    s4 = np.zeros((128, 2, 128), np.float32)
    s4[0, 0, 0:64] = 1.0
    s4[32, 0, 64:128] = 1.0
    s4[64, 1, 0:64] = 1.0
    s4[96, 1, 64:128] = 1.0
    eh64 = np.zeros((128, 2), np.float32)
    eh64[0:64, 0] = 1.0
    eh64[64:128, 1] = 1.0
    return dict(identc=ident, dmc=dm, s4c=s4, eh64c=eh64)


def _rope_tables():
    DD = 64
    inv = 1.0 / (100000.0 ** (np.arange(0, DD, 2, dtype=np.float64) / DD))
    f = np.arange(N, dtype=np.float64)[:, None] * inv[None, :]
    c = np.cos(f).astype(np.float32).T
    s = np.sin(f).astype(np.float32).T
    crep = np.tile(c, (4, 1))
    srep = np.empty((128, N), np.float32)
    for j in range(2):
        srep[64 * j:64 * j + 32] = -s
        srep[64 * j + 32:64 * j + 64] = s
    return crep, srep


def run_stage_B(x1, W):
    consts = _consts_BD()
    crep, srep = _rope_tables()
    ncB1 = build_stageB1()
    in_maps = []
    for core in range(8):
        b, half = core // 2, core % 2
        h0 = half * 8
        wq = W["Wq"][:, h0 * 64:(h0 + 8) * 64]
        wqsw = wq.reshape(1024, 8, 2, 32)[:, :, ::-1, :].reshape(1024, 512)
        wkup = W["Wk_up"][h0:h0 + 8].transpose(1, 0, 2).reshape(256, 512)
        wkupsw = wkup.reshape(256, 8, 2, 32)[:, :, ::-1, :].reshape(256, 512)
        wvup = W["Wv_up"][h0:h0 + 8].transpose(1, 0, 2).reshape(256, 512)
        in_maps.append(dict(
            xb=np.ascontiguousarray(x1[b]),
            ln1w=W["ln1_w"].reshape(1, C),
            wq=np.ascontiguousarray(wq), wqsw=np.ascontiguousarray(wqsw),
            wkv=np.ascontiguousarray(W["Wkv"]),
            wkup=np.ascontiguousarray(wkup), wkupsw=np.ascontiguousarray(wkupsw),
            wvup=np.ascontiguousarray(wvup),
            wot=np.ascontiguousarray(W["Wo"][h0 * 64:(h0 + 8) * 64]),
            crepc=crep, srepc=srep, **consts))
    res = run_bass_kernel_spmd(ncB1, in_maps, core_ids=list(range(8)))
    yps = [r["yp"] for r in res.results]

    ncB2 = build_stageB2()
    in_maps2 = []
    for core in range(8):
        b, half = core // 2, core % 2
        tok = slice(half * 2048, (half + 1) * 2048)
        in_maps2.append(dict(
            x_own=np.ascontiguousarray(x1[b, tok]),
            ypA=np.ascontiguousarray(yps[2 * b][tok]),
            ypB=np.ascontiguousarray(yps[2 * b + 1][tok])))
    res2 = run_bass_kernel_spmd(ncB2, in_maps2, core_ids=list(range(8)))
    a = np.empty_like(x1)
    for core in range(8):
        b, half = core // 2, core % 2
        a[b, half * 2048:(half + 1) * 2048] = res2.results[core]["ao"]
    return a


def run_stage_D(x2, W):
    consts = _consts_BD()
    ncD1 = build_stageD1()
    in_maps = []
    for core in range(8):
        b, half = core // 2, core % 2
        h0 = half * 8
        sl = slice(h0 * 64, (h0 + 8) * 64)
        in_maps.append(dict(
            xb=np.ascontiguousarray(x2[b]),
            wq=np.ascontiguousarray(W["Wq"][:, sl]),
            wk=np.ascontiguousarray(W["Wkv"][:, 0:1024][:, sl]),
            wv=np.ascontiguousarray(W["Wkv"][:, 1024:2048][:, sl]),
            wot=np.ascontiguousarray(W["Wo"][sl]),
            **consts))
    res = run_bass_kernel_spmd(ncD1, in_maps, core_ids=list(range(8)))
    yps = [r["yp"] for r in res.results]

    ncA2 = build_stageA2()
    in_maps2 = []
    for core in range(8):
        b, half = core // 2, core % 2
        tok = slice(half * 2048, (half + 1) * 2048)
        in_maps2.append(dict(
            x_own=np.ascontiguousarray(x2[b, tok]),
            ypA=np.ascontiguousarray(yps[2 * b][tok]),
            ypB=np.ascontiguousarray(yps[2 * b + 1][tok]),
            bo=W["bo"].reshape(1, C)))
    res2 = run_bass_kernel_spmd(ncA2, in_maps2, core_ids=list(range(8)))
    x3 = np.empty_like(x2)
    for core in range(8):
        b, half = core // 2, core % 2
        x3[b, half * 2048:(half + 1) * 2048] = res2.results[core]["xo"]
    return x3



def emit_expert(P, *, e_idx, kind, rms2, h1, w1s, w3s, w2all, sil, acc, gam_ap,
                w1_d, w3_d, w2_d, b1col_fn=None, state=None):
    st = state if state is not None else {}
    P.sec("g", lambda g: (
        g.wait("pe", st.get("down_pe", 0)),
        g.dma_start(out=w2all[:, :, :], in_=w2_d)))
    for half in range(2):
        tok0 = half * 1024
        for qtr in range(8):
            par = qtr % 2
            def lq(g, qtr=qtr, par=par):
                g.wait("pe", st.get(("qtr", par), 0))
                g.dma_start(out=w1s[:, par], in_=w1_d[qtr])
                if kind == "swiglu":
                    g.dma_start(out=w3s[:, par], in_=w3_d[qtr])
            P.sec("g", lq)
            cld = P.val("dg")
            for m4 in range(2):
                m = qtr * 2 + m4
                for g2 in range(2):
                    def up(t, m4=m4, par=par, g2=g2, cld=cld, tok0=tok0):
                        t.wait("dg", cld)
                        t.wait("se")
                        t.wait("ve")
                        cols = slice(tok0 + g2 * 512, tok0 + (g2 + 1) * 512)
                        mm_chain(t, P.ps[g2][:, :],
                                 [(w1s[:, par, m4, cch, :], rms2[:, cch, cols])
                                  for cch in range(8)])
                        if kind == "swiglu":
                            mm_chain(t, P.ps[2 + g2][:, :],
                                     [(w3s[:, par, m4, cch, :], rms2[:, cch, cols])
                                      for cch in range(8)])
                    cpe = P.sec("t", up)
                    if kind == "swiglu":
                        cs = P.sec("s", lambda s, g2=g2: (
                            s.wait("pe", cpe),
                            s.activation(sil[:, g2, :], P.ps[g2][:, :], AF.Silu)))
                        P.sec("v", lambda v, m=m, g2=g2: (
                            v.wait("se", cs),
                            v.tensor_mul(h1[:, m, g2 * 512:(g2 + 1) * 512],
                                         sil[:, g2, :], P.ps[2 + g2][:, :])))
                    else:
                        P.sec("s", lambda s, m=m, g2=g2: (
                            s.wait("pe", cpe),
                            s.activation(h1[:, m, g2 * 512:(g2 + 1) * 512],
                                         P.ps[g2][:, :], AF.Gelu,
                                         bias=b1col_fn(m))))
            st[("qtr", par)] = P.val("pe")
        for th in range(8):
            tt = half * 8 + th
            def down(t, th=th):
                t.wait("se"); t.wait("ve"); t.wait("dg")
                for j in range(2):
                    mm_chain(t, P.ps[4 + j][:, :],
                             [(h1[:, m2, th * 128:(th + 1) * 128],
                               w2all[:, m2, j * 512:(j + 1) * 512])
                              for m2 in range(16)])
            cpe = P.sec("t", down)

            def accf(v, tt=tt):
                v.wait("pe", cpe)
                for j in range(2):
                    sl = acc[:, tt, j * 512:(j + 1) * 512]
                    if gam_ap is not None:
                        v.scalar_tensor_tensor(out=sl, in0=P.ps[4 + j][:, :],
                                               scalar=gam_ap(tt), in1=sl,
                                               op0=OP.mult, op1=OP.add)
                    else:
                        v.tensor_add(sl, P.ps[4 + j][:, :], sl)
            P.sec("v", accf)
        st["down_pe"] = P.val("pe")


def lay_up(W):
    return np.ascontiguousarray(
        W.reshape(8, 128, 8, 2, 128).transpose(2, 1, 3, 0, 4))


def lay_down(W):
    return np.ascontiguousarray(W.reshape(16, 128, 1024).transpose(1, 0, 2))



def build_stageC():
    P = Prog("C")
    a_own = P.inp("a_own", [2048, C])
    x1_own = P.inp("x1_own", [2048, C])
    ln2w = P.inp("ln2w", [1, C])
    wr = P.inp("wr", [C, 8])
    rbias = P.inp("rbias", [1, 8])
    sw1 = P.inp("sw1", [8, 128, 2, 8, 128])
    sw2 = P.inp("sw2", [128, 16, 1024])
    sw3 = P.inp("sw3", [8, 128, 2, 8, 128])
    ew1 = P.inp("ew1", [8, 8, 128, 2, 8, 128])
    ew2 = P.inp("ew2", [8, 128, 16, 1024])
    ew3 = P.inp("ew3", [8, 8, 128, 2, 8, 128])
    identc = P.inp("identc", [128, 128])
    xo = P.outp("xo", [2048, C])

    es0 = contextlib.ExitStack()
    ident = P.sb(es0, "ident", [128, 128], BF16)
    onesr = P.sb(es0, "onesr", [1, 128], F32)
    ln2bc = P.sb(es0, "ln2bc", [128, 1024], F32)
    rbbc = P.sb(es0, "rbbc", [128, 8], F32)
    w_r = P.sb(es0, "w_r", [128, 8, 8], BF16)
    rms2 = P.sb(es0, "rms2", [128, 8, 2048], BF16)
    gam = P.sb(es0, "gam", [128, 16, 12], F32)
    acc = P.sb(es0, "acc", [128, 16, 1024], F32, side="right")
    h1 = P.sb(es0, "h1", [128, 16, 1024], BF16, side="right")
    w2all = P.sb(es0, "w2all", [128, 16, 1024], BF16, side="right")

    P.sec("g", lambda g: (
        g.dma_start(out=ident[:, :], in_=identc),
        g.dma_start(out=w_r[:, :, :], in_=wr.rearrange("(c p) n -> p c n", p=128)),
        g.memset(onesr[:, :], 1.0)))
    emit_row_bcast(P, ln2w, ln2bc, 1024, onesr, es0)
    emit_row_bcast(P, rbias, rbbc, 8, onesr, es0)
    P.sec("y", lambda y: y.dma_start(
        out=acc[:, :, :], in_=a_own.rearrange("(t p) c -> p t c", p=128)))
    c_a = P.val("dy")

    es1 = contextlib.ExitStack()
    rmst = P.sb(es1, "rmst", [128, 1024], BF16)
    sq = P.sb(es1, "sq", [128, 1024], F32)
    st = P.sb(es1, "st", [128, 8], F32)
    scr = dict(sq=sq, st=st)
    P.sec("v", lambda v: v.wait("dy", c_a))
    for tt in range(ONT):
        c = emit_norm(P, acc[:, tt, :], rmst[:, :], scr, eps=1e-6, kind="rms",
                      gain_bc=ln2bc[:, :])
        emit_transpose_tile(P, rmst, rms2, tt, c, ident)

    lg = P.sb(es1, "lg", [128, 8], F32)
    pb = P.sb(es1, "pb", [128, 8], F32)
    mx8 = P.sb(es1, "mx8", [128, 8], F32)
    ix8 = P.sb(es1, "ix8", [128, 8], mybir.dt.uint32)
    ixf = P.sb(es1, "ixf", [128, 1], F32)
    tw = P.sb(es1, "tw", [128, 4], F32)
    for tt in range(ONT):
        cpe = P.sec("t", lambda t, tt=tt: (
            t.wait("ve"), t.wait("dg"), t.wait("gc"),
            mm_chain(t, P.ps[0][:, 0:8],
                     [(rms2[:, cch, tt * 128:(tt + 1) * 128], w_r[:, cch, :])
                      for cch in range(8)])))

        def vrouter(v, tt=tt):
            v.wait("pe", cpe)
            v.tensor_add(lg[:, :], P.ps[0][:, 0:8], rbbc[:, :])
            v.edge()
            v.tensor_scalar(out=lg[:, :], in0=lg[:, :], scalar1=50.0, scalar2=-50.0,
                            op0=OP.min, op1=OP.max)
        cv = P.sec("v", vrouter)
        cs = P.sec("s", lambda s: (
            s.wait("ve", cv),
            s.activation(pb[:, :], lg[:, :], AF.Sigmoid)))

        def vsel(v, tt=tt):
            v.wait("se", cs)
            v.max(mx8[:, :], pb[:, :])
            v.edge()
            v.max_index(ix8[:, :], mx8[:, :], pb[:, :])
            v.tensor_scalar(out=tw[:, 0:1], in0=mx8[:, 0:1], scalar1=1e-8,
                            scalar2=1.0 - 1e-8, op0=OP.max, op1=OP.min)
            v.edge()
            v.tensor_scalar(out=tw[:, 1:2], in0=tw[:, 0:1],
                            scalar1=0.5 + 1e-8, scalar2=None, op0=OP.add)
            v.tensor_copy(ixf[:, :], ix8[:, 0:1])
            v.edge()
            v.tensor_scalar(out=tw[:, 1:2], in0=tw[:, 1:2], scalar1=0.5,
                            scalar2=2.0, op0=OP.max, op1=OP.min)
            v.edge()
            v.reciprocal(tw[:, 2:3], tw[:, 1:2])
            v.edge()
            v.tensor_scalar(out=gam[:, tt, 8:9], in0=tw[:, 2:3], scalar1=0.5,
                            scalar2=None, op0=OP.mult)
            v.tensor_mul(tw[:, 3:4], tw[:, 0:1], tw[:, 2:3])
            v.edge()
            for e in range(8):
                v.tensor_scalar(out=gam[:, tt, e:e + 1], in0=ixf[:, :],
                                scalar1=float(e), scalar2=None, op0=OP.is_equal)
            v.edge()
            for e in range(8):
                v.tensor_mul(gam[:, tt, e:e + 1], gam[:, tt, e:e + 1], tw[:, 3:4])
        P.sec("v", vsel)
    es1.close()
    P.barrier()

    es_w = contextlib.ExitStack()
    w1s = P.sb(es_w, "w1s", [128, 2, 2, 8, 128], BF16)
    w3s = P.sb(es_w, "w3s", [128, 2, 2, 8, 128], BF16)
    sil = P.sb(es_w, "sil", [128, 2, 512], F32)
    stx = {}
    emit_expert(P, e_idx=-1, kind="swiglu", rms2=rms2, h1=h1, w1s=w1s, w3s=w3s,
                w2all=w2all, sil=sil, acc=acc,
                gam_ap=lambda tt: gam[:, tt, 8:9],
                w1_d=sw1, w3_d=sw3, w2_d=sw2, state=stx)
    for e in range(8):
        emit_expert(P, e_idx=e, kind="swiglu", rms2=rms2, h1=h1, w1s=w1s, w3s=w3s,
                    w2all=w2all, sil=sil, acc=acc,
                    gam_ap=lambda tt, e=e: gam[:, tt, e:e + 1],
                    w1_d=ew1[e], w3_d=ew3[e], w2_d=ew2[e], state=stx)
    P.barrier()
    es_w.close()

    es4 = contextlib.ExitStack()
    xt = P.sb(es4, "xt", [128, 2, 1024], F32)
    lnt = P.sb(es4, "lnt", [128, 1024], F32)
    sq4 = P.sb(es4, "sq4", [128, 1024], F32)
    st4 = P.sb(es4, "st4", [128, 8], F32)
    scr4 = dict(sq=sq4, st=st4)
    for tt in range(ONT):
        buf = tt % 2
        P.sec("y", lambda y, tt=tt, buf=buf: (
            y.wait("ve"),
            y.dma_start(out=xt[:, buf, :], in_=x1_own[tt * 128:(tt + 1) * 128, :])))
        cld = P.val("dy")
        c = emit_norm(P, acc[:, tt, :], lnt[:, :], scr4, eps=1e-5, kind="ln")

        def resid(v, buf=buf):
            v.wait("ve", c)
            v.wait("dy", cld)
            v.tensor_add(xt[:, buf, :], xt[:, buf, :], lnt[:, :])
        cr = P.sec("v", resid)
        P.sec("y", lambda y, tt=tt, buf=buf: (
            y.wait("ve", cr),
            y.dma_start(out=xo[tt * 128:(tt + 1) * 128, :], in_=xt[:, buf, :])))
    P.sec("y", lambda y: y.wait_all())
    es4.close()
    es0.close()
    return P.finish()



def build_stageE():
    P = Prog("E")
    x_own = P.inp("x_own", [2048, C])
    gwt = P.inp("gwt", [C, 8])
    gb = P.inp("gb", [1, 8])
    rmsw = P.inp("rmsw", [1, 8])
    w1 = P.inp("w1", [8, 8, 128, 2, 8, 128])
    b1 = P.inp("b1", [128, 8, 16])
    w2 = P.inp("w2", [8, 128, 16, 1024])
    b2 = P.inp("b2", [8, C])
    identc = P.inp("identc", [128, 128])
    xo = P.outp("xo", [2048, C])

    es0 = contextlib.ExitStack()
    ident = P.sb(es0, "ident", [128, 128], BF16)
    onesr = P.sb(es0, "onesr", [1, 128], F32)
    onesc8 = P.sb(es0, "onesc8", [8, 1], F32)
    gbbc = P.sb(es0, "gbbc", [128, 8], F32)
    rmswbc = P.sb(es0, "rmswbc", [128, 8], F32)
    w_g = P.sb(es0, "w_g", [128, 8, 8], BF16)
    b1sb = P.sb(es0, "b1sb", [128, 8, 16], F32)
    b2sb = P.sb(es0, "b2sb", [8, 1024], F32)
    b2row = P.sb(es0, "b2row", [1, 1024], F32)
    b2bc = P.sb(es0, "b2bc", [128, 1024], F32)
    hn = P.sb(es0, "hn", [128, 8, 2048], BF16)
    ng = P.sb(es0, "ng", [128, 16, 1], F32)
    w1s = P.sb(es0, "w1s", [128, 2, 8, 128], BF16)
    acc = P.sb(es0, "acc", [128, 16, 1024], F32, side="right")
    h1 = P.sb(es0, "h1", [128, 16, 1024], BF16, side="right")
    w2all = P.sb(es0, "w2all", [128, 16, 1024], BF16, side="right")

    P.sec("g", lambda g: (
        g.dma_start(out=ident[:, :], in_=identc),
        g.dma_start(out=w_g[:, :, :], in_=gwt.rearrange("(c p) n -> p c n", p=128)),
        g.dma_start(out=b1sb[:, :, :], in_=b1),
        g.dma_start(out=b2sb[:, :], in_=b2),
        g.memset(onesr[:, :], 1.0),
        g.memset(onesc8[:, :], 1.0)))
    emit_row_bcast(P, gb, gbbc, 8, onesr, es0)
    emit_row_bcast(P, rmsw, rmswbc, 8, onesr, es0)
    cpe = P.sec("t", lambda t: (
        t.wait("dg"), t.wait("gc"), t.wait("ve"),
        mm_chain(t, P.ps[0][0:1, :], [(onesc8[:, :], b2sb[:, 0:512])]),
        mm_chain(t, P.ps[1][0:1, :], [(onesc8[:, :], b2sb[:, 512:1024])])))
    cv = P.sec("v", lambda v: (
        v.wait("pe", cpe),
        v.tensor_copy(b2row[:, 0:512], P.ps[0][0:1, :]),
        v.tensor_copy(b2row[:, 512:1024], P.ps[1][0:1, :])))
    cpe2 = P.sec("t", lambda t: (
        t.wait("ve", cv),
        mm_chain(t, P.ps[0][:, :], [(onesr[:, :], b2row[:, 0:512])]),
        mm_chain(t, P.ps[1][:, :], [(onesr[:, :], b2row[:, 512:1024])])))
    P.sec("v", lambda v: (
        v.wait("pe", cpe2),
        v.tensor_copy(b2bc[:, 0:512], P.ps[0][:, :]),
        v.tensor_copy(b2bc[:, 512:1024], P.ps[1][:, :])))

    P.sec("v", lambda v: (v.wait("ve"),
                          [v.tensor_copy(acc[:, tt, :], b2bc[:, :])
                           for tt in range(ONT)]))

    es1 = contextlib.ExitStack()
    xt2 = P.sb(es1, "xt2", [128, 2, 1024], F32)
    hnt = P.sb(es1, "hnt", [128, 1024], BF16)
    sq = P.sb(es1, "sq", [128, 1024], F32)
    st = P.sb(es1, "st", [128, 8], F32)
    scr = dict(sq=sq, st=st)
    for tt in range(ONT):
        buf = tt % 2
        P.sec("y", lambda y, tt=tt, buf=buf: (
            y.wait("ve"), y.wait("se"),
            y.dma_start(out=xt2[:, buf, :], in_=x_own[tt * 128:(tt + 1) * 128, :])))
        c = emit_norm(P, xt2[:, buf, :], hnt[:, :], scr, eps=1e-5, kind="ln")
        emit_transpose_tile(P, hnt, hn, tt, c, ident)

    lg = P.sb(es1, "lg", [128, 8], F32)
    gs = P.sb(es1, "gs", [128, 8], F32)
    EPSF = float(np.finfo(np.float32).eps)
    for tt in range(ONT):
        cpe = P.sec("t", lambda t, tt=tt: (
            t.wait("ve"), t.wait("dg"), t.wait("gc"),
            mm_chain(t, P.ps[0][:, 0:8],
                     [(hn[:, cch, tt * 128:(tt + 1) * 128], w_g[:, cch, :])
                      for cch in range(8)])))

        def vg1(v, tt=tt):
            v.wait("pe", cpe)
            v.tensor_add(lg[:, :], P.ps[0][:, 0:8], gbbc[:, :])
            v.edge()
            v.tensor_mul(gs[:, :], lg[:, :], lg[:, :])
            v.edge()
            v.reduce_sum(st[:, 0:1], gs[:, :], axis=mybir.AxisListType.X)
            v.edge()
            v.tensor_scalar(out=st[:, 1:2], in0=st[:, 0:1], scalar1=1.0 / 8,
                            scalar2=None, op0=OP.mult)
        cv1 = P.sec("v", vg1)
        epsc = P.ccol(EPSF)
        cs1 = P.sec("s", lambda s: (
            s.wait("ve", cv1), s.wait("gc"),
            s.activation(st[:, 2:3], st[:, 1:2], AF.Ln, bias=epsc),
            s.edge(),
            s.activation(st[:, 3:4], st[:, 2:3], AF.Exp, scale=-0.5)))

        def vg2(v, tt=tt):
            v.wait("se", cs1)
            v.tensor_scalar(out=lg[:, :], in0=lg[:, :], scalar1=st[:, 3:4],
                            scalar2=None, op0=OP.mult)
            v.edge()
            v.tensor_mul(lg[:, :], lg[:, :], rmswbc[:, :])
            v.edge()
            v.reduce_max(st[:, 4:5], lg[:, :], axis=mybir.AxisListType.X)
            v.edge()
            v.tensor_scalar(out=st[:, 5:6], in0=st[:, 4:5], scalar1=-2.0,
                            scalar2=None, op0=OP.mult)
        cv2 = P.sec("v", vg2)
        cs2 = P.sec("s", lambda s: (
            s.wait("ve", cv2),
            s.activation(gs[:, :], lg[:, :], AF.Exp, bias=st[:, 5:6], scale=2.0)))

        def vg3(v, tt=tt):
            v.wait("se", cs2)
            v.reduce_sum(st[:, 6:7], gs[:, :], axis=mybir.AxisListType.X)
            v.edge()
            v.tensor_scalar(out=st[:, 7:8], in0=st[:, 6:7], scalar1=1e-6,
                            scalar2=1.0, op0=OP.mult, op1=OP.add)
            v.edge()
            v.reciprocal(ng[:, tt, :], st[:, 7:8])
        P.sec("v", vg3)
    es1.close()
    P.barrier()

    es_w = contextlib.ExitStack()
    w1s = P.sb(es_w, "w1s", [128, 2, 2, 8, 128], BF16)
    stx = {}
    for e in range(8):
        emit_expert(P, e_idx=e, kind="gelu", rms2=hn, h1=h1, w1s=w1s, w3s=None,
                    w2all=w2all, sil=None, acc=acc, gam_ap=None,
                    w1_d=w1[e], w3_d=None, w2_d=w2[e],
                    b1col_fn=lambda m, e=e: b1sb[:, e, m:m + 1], state=stx)
    P.barrier()
    es_w.close()

    es4 = contextlib.ExitStack()
    xt4 = P.sb(es4, "xt4", [128, 2, 1024], F32)
    lnt = P.sb(es4, "lnt", [128, 1024], F32)
    sq4 = P.sb(es4, "sq4", [128, 1024], F32)
    st4 = P.sb(es4, "st4", [128, 8], F32)
    scr4 = dict(sq=sq4, st=st4)
    for tt in range(ONT):
        buf = tt % 2
        P.sec("y", lambda y, tt=tt, buf=buf: (
            y.wait("ve"),
            y.dma_start(out=xt4[:, buf, :], in_=x_own[tt * 128:(tt + 1) * 128, :])))
        cld = P.val("dy")
        csc = P.sec("v", lambda v, tt=tt: (
            v.wait("pe"), v.wait("se"),
            v.tensor_scalar(out=acc[:, tt, :], in0=acc[:, tt, :],
                            scalar1=ng[:, tt, :], scalar2=None, op0=OP.mult)))
        c = emit_norm(P, acc[:, tt, :], lnt[:, :], scr4, eps=1e-5, kind="ln")

        def resid(v, tt=tt, buf=buf, cld=cld):
            v.wait("ve", c)
            v.wait("dy", cld)
            v.tensor_add(xt4[:, buf, :], xt4[:, buf, :], lnt[:, :])
        cr = P.sec("v", resid)
        P.sec("y", lambda y, tt=tt, buf=buf: (
            y.wait("ve", cr),
            y.dma_start(out=xo[tt * 128:(tt + 1) * 128, :], in_=xt4[:, buf, :])))
    P.sec("y", lambda y: y.wait_all())
    es4.close()
    es0.close()
    return P.finish()


def run_stage_C(a, x1, W):
    ncC = build_stageC()
    ident = np.eye(128, dtype=np.float32)
    in_maps = []
    for core in range(8):
        b, half = core // 2, core % 2
        tok = slice(half * 2048, (half + 1) * 2048)
        in_maps.append(dict(
            a_own=np.ascontiguousarray(a[b, tok]),
            x1_own=np.ascontiguousarray(x1[b, tok]),
            ln2w=W["ln2_w"].reshape(1, C),
            wr=np.ascontiguousarray(W["Wr"]),
            rbias=W["r_bias"].reshape(1, 8),
            sw1=lay_up(W["sw1"]), sw2=lay_down(W["sw2"]),
            sw3=lay_up(W["sw3"]),
            ew1=np.stack([lay_up(W["ew1"][e]) for e in range(8)]),
            ew2=np.stack([lay_down(W["ew2"][e]) for e in range(8)]),
            ew3=np.stack([lay_up(W["ew3"][e]) for e in range(8)]),
            identc=ident))
    res = run_bass_kernel_spmd(ncC, in_maps, core_ids=list(range(8)))
    x2 = np.empty_like(x1)
    for core in range(8):
        b, half = core // 2, core % 2
        x2[b, half * 2048:(half + 1) * 2048] = res.results[core]["xo"]
    return x2


def run_stage_E(x3, W):
    ncE = build_stageE()
    ident = np.eye(128, dtype=np.float32)
    in_maps = []
    for core in range(8):
        b, half = core // 2, core % 2
        tok = slice(half * 2048, (half + 1) * 2048)
        in_maps.append(dict(
            x_own=np.ascontiguousarray(x3[b, tok]),
            gwt=np.ascontiguousarray(W["gate_w"].T),
            gb=W["gate_b"].reshape(1, 8),
            rmsw=W["rms_w"].reshape(1, 8),
            w1=np.stack([lay_up(W["w1"][e]) for e in range(8)]),
            b1=np.ascontiguousarray(W["b1"].reshape(8, 16, 128).transpose(2, 0, 1)),
            w2=np.stack([lay_down(W["w2"][e]) for e in range(8)]),
            b2=np.ascontiguousarray(W["b2"]),
            identc=ident))
    res = run_bass_kernel_spmd(ncE, in_maps, core_ids=list(range(8)))
    x4 = np.empty_like(x3)
    for core in range(8):
        b, half = core // 2, core % 2
        x4[b, half * 2048:(half + 1) * 2048] = res.results[core]["xo"]
    return x4



PAIRS = [[0, 1], [2, 3], [4, 5], [6, 7]]


def emit_gather_send(P, own_fm, gin, gout):
    P.sec("y", lambda y: (
        y.wait("ve"), y.wait("se"), y.wait("cc"),
        y.dma_start(out=gin.rearrange("(c p) t -> p c t", p=128),
                    in_=own_fm[:, :, :])))
    c = P.val("dy")
    P.sec("g", lambda g: (
        g.wait("dy", c), g.wait("cc"),
        g.collective_compute("AllGather", OP.bypass, replica_groups=PAIRS,
                             ins=[gin.opt()], outs=[gout.opt()])))
    return P.val("cc")


def emit_gather_recv(P, full_fm, gout, ccv):
    P.sec("y", lambda y: (
        y.wait("cc", ccv),
        y.dma_start(out=full_fm[:, :, 0:2048],
                    in_=gout[0:1024].rearrange("(c p) t -> p c t", p=128)),
        y.dma_start(out=full_fm[:, :, 2048:4096],
                    in_=gout[1024:2048].rearrange("(c p) t -> p c t", p=128))))
    return P.val("dy")


def emit_gather_fm(P, own_fm, full_fm, gin, gout, ntiles_own=16):
    ccv = emit_gather_send(P, own_fm, gin, gout)
    return emit_gather_recv(P, full_fm, gout, ccv)


def emit_rs_y(P, ypd, ysd):
    c = P.val("dy")
    P.sec("g", lambda g: (
        g.wait("dy", c), g.wait("cc"),
        g.collective_compute("ReduceScatter", OP.add, replica_groups=PAIRS,
                             ins=[ypd.opt()], outs=[ysd.opt()])))
    return P.val("cc")


def emit_back_AD(P, com, ysd, bo, xsrc, xdst, ccv):
    es = contextlib.ExitStack()
    onesr = com["onesr"]
    yt = P.sb(es, "bk_yt", [128, 2, 1024], F32)
    xt = P.sb(es, "bk_xt", [128, 2, 1024], F32)
    lno = P.sb(es, "bk_lno", [128, 1024], F32)
    sq = P.sb(es, "bk_sq", [128, 1024], F32)
    st = P.sb(es, "bk_st", [128, 8], F32)
    scr = dict(sq=sq, st=st)
    bob = None
    if bo is not None:
        bob = P.sb(es, "bk_bob", [128, 1024], F32)
        emit_row_bcast(P, bo, bob, 1024, onesr, es)
    for j in range(ONT):
        buf = j % 2
        P.sec("y", lambda y, j=j, buf=buf: (
            y.wait("ve"), y.wait("se"), y.wait("cc", ccv),
            y.dma_start(out=yt[:, buf, :], in_=ysd[j * 128:(j + 1) * 128, :]),
            y.dma_start(out=xt[:, buf, :], in_=xsrc[j * 128:(j + 1) * 128, :])))
        cld = P.val("dy")
        if bo is not None:
            P.sec("v", lambda v, buf=buf: (
                v.wait("dy", cld), v.wait("pe"),
                v.tensor_add(yt[:, buf, :], yt[:, buf, :], bob[:, :])))
        else:
            P.sec("v", lambda v: v.wait("dy", cld))
        c = emit_norm(P, yt[:, buf, :], lno[:, :], scr, eps=1e-5, kind="ln")

        def resid(v, buf=buf):
            v.wait("ve", c)
            v.tensor_add(xt[:, buf, :], xt[:, buf, :], lno[:, :])
        cr = P.sec("v", resid)
        P.sec("y", lambda y, j=j, buf=buf: (
            y.wait("ve", cr),
            y.dma_start(out=xdst[j * 128:(j + 1) * 128, :], in_=xt[:, buf, :])))
    es.close()


def emit_back_B(P, com, ysd, x1src, adst, ccv):
    es = contextlib.ExitStack()
    yt = P.sb(es, "bb_yt", [128, 2, 1024], F32)
    xt = P.sb(es, "bb_xt", [128, 2, 1024], F32)
    ht = P.sb(es, "bb_ht", [128, 1024], F32)
    sq = P.sb(es, "bb_sq", [128, 1024], F32)
    st = P.sb(es, "bb_st", [128, 8], F32)
    scr = dict(sq=sq, st=st)
    for j in range(ONT):
        buf = j % 2
        P.sec("y", lambda y, j=j, buf=buf: (
            y.wait("ve"), y.wait("se"), y.wait("cc", ccv),
            y.dma_start(out=yt[:, buf, :], in_=ysd[j * 128:(j + 1) * 128, :]),
            y.dma_start(out=xt[:, buf, :], in_=x1src[j * 128:(j + 1) * 128, :])))
        cld = P.val("dy")
        P.sec("v", lambda v: v.wait("dy", cld))
        c = emit_norm(P, xt[:, buf, :], ht[:, :], scr, eps=1e-5, kind="ln")

        def addf(v, buf=buf):
            v.wait("ve", c)
            v.tensor_add(yt[:, buf, :], yt[:, buf, :], ht[:, :])
        cr = P.sec("v", addf)
        P.sec("y", lambda y, j=j, buf=buf: (
            y.wait("ve", cr),
            y.dma_start(out=adst[j * 128:(j + 1) * 128, :], in_=yt[:, buf, :])))
    es.close()


def emit_midA(P, com, w, hn, yp):
    ident, onesr, onescol, dm, s4 = (com["ident"], com["onesr"], com["onescol"],
                                     com["dm"], com["s4"])
    es0 = contextlib.ExitStack()
    e1 = P.sb(es0, "a_e1", [32, 2, 128], BF16)
    e2 = P.sb(es0, "a_e2", [64, 2, 128], BF16)
    eh1 = P.sb(es0, "a_eh1", [32, 8], BF16)
    eh2 = P.sb(es0, "a_eh2", [64, 8], BF16)
    w_q1 = P.sb(es0, "a_wq1", [128, 8, 32], BF16)
    w_k1 = P.sb(es0, "a_wk1", [128, 8, 32], BF16)
    w_q2 = P.sb(es0, "a_wq2", [128, 8, 64], BF16)
    w_k2 = P.sb(es0, "a_wk2", [128, 8, 64], BF16)
    w_v = P.sb(es0, "a_wv", [128, 8, 512], BF16)
    kt = P.sb(es0, "a_kt", [128, 2, 4096], BF16)
    qt = P.sb(es0, "a_qt", [128, 2, 4096], BF16)
    bbc = P.sb(es0, "a_bbc", [128, 8, 8], F32)
    btab_dram = P.dram("a_btd", [1, 64])

    def loads(g):
        g.wait("cc")
        g.dma_start(out=e1[:, :, :], in_=w["e1c"])
        g.dma_start(out=e2[:, :, :], in_=w["e2c"])
        g.dma_start(out=eh1[:, :], in_=w["eh1c"])
        g.dma_start(out=eh2[:, :], in_=w["eh2c"])
        g.dma_start(out=w_q1[:, :, :], in_=w["wq1t"].rearrange("(c p) n -> p c n", p=128))
        g.dma_start(out=w_k1[:, :, :], in_=w["wk1t"].rearrange("(c p) n -> p c n", p=128))
        g.dma_start(out=w_q2[:, :, :], in_=w["wq2t"].rearrange("(c p) n -> p c n", p=128))
        g.dma_start(out=w_k2[:, :, :], in_=w["wk2t"].rearrange("(c p) n -> p c n", p=128))
        g.dma_start(out=w_v[:, :, :], in_=w["wvt"].rearrange("(c p) n -> p c n", p=128))
    P.sec("g", loads)

    es2 = contextlib.ExitStack()
    q1s = P.sb(es2, "a_q1s", [32, 4096], BF16)
    k1s = P.sb(es2, "a_k1s", [32, 4096], BF16)
    q2s = P.sb(es2, "a_q2s", [64, 4096], BF16)
    k2s = P.sb(es2, "a_k2s", [64, 4096], BF16)
    nrm = P.sb(es2, "a_nrm", [8, 2, 4096], BF16)
    pp1 = P.sb(es2, "a_pp1", [128, 512], F32)
    pp2 = P.sb(es2, "a_pp2", [128, 512], F32)
    btabs = P.sb(es2, "a_btabs", [8, 8], F32)
    btabr = P.sb(es2, "a_btabr", [1, 64], F32)
    st = P.sb(es2, "a_st", [128, 8], F32)
    sq1 = P.sb(es2, "a_sq1", [32, 4096], BF16)
    sq2 = P.sb(es2, "a_sq2", [64, 4096], BF16)

    def proj(dst, wsb, m):
        for gg in range(8):
            cpe = P.sec("t", lambda t, gg=gg: (
                t.wait("gc"), t.wait("dg"), t.wait("ve"),
                mm_chain(t, P.ps[0][0:m, 0:512],
                         [(wsb[:, cch, :], hn[:, cch, gg * 512:(gg + 1) * 512])
                          for cch in range(8)])))
            P.sec("v", lambda v, gg=gg: (
                v.wait("pe", cpe),
                v.tensor_copy(dst[:, gg * 512:(gg + 1) * 512], P.ps[0][0:m, 0:512])))
    proj(q1s, w_q1, 32)
    proj(k1s, w_k1, 32)
    proj(q2s, w_q2, 64)
    proj(k2s, w_k2, 64)

    def emit_pairnorm(srcA, srcB, slot):
        cs = P.sec("s", lambda s: (
            s.wait("ve"),
            s.activation(sq1[:, :], srcA, AF.Square),
            s.activation(sq2[:, :], srcB, AF.Square)))
        for gg in range(8):
            cpe = P.sec("t", lambda t, gg=gg: (
                t.wait("se", cs), t.wait("ve"),
                mm_chain(t, P.ps[1][0:8, 0:512],
                         [(eh1[:, :], sq1[:, gg * 512:(gg + 1) * 512])]),
                mm_chain(t, P.ps[2][0:8, 0:512],
                         [(eh2[:, :], sq2[:, gg * 512:(gg + 1) * 512])])))
            P.sec("v", lambda v, gg=gg, slot=slot: (
                v.wait("pe", cpe),
                v.tensor_copy(pp1[0:8, :], P.ps[1][0:8, 0:512]),
                v.edge(),
                v.tensor_mul(nrm[:, slot, gg * 512:(gg + 1) * 512],
                             pp1[0:8, :], P.ps[2][0:8, 0:512])))
    emit_pairnorm(q1s[:, :], q2s[:, :], 0)
    emit_pairnorm(k1s[:, :], k2s[:, :], 1)
    SCALE = float((4 * 8) ** (-0.5))

    def vbt(v):
        v.wait("se")
        v.reduce_max(st[0:8, 0:1], nrm[:, 1, :], axis=mybir.AxisListType.X)
        for gg in range(8):
            v.reduce_max(btabs[:, gg:gg + 1], nrm[:, 0, gg * 512:(gg + 1) * 512],
                         axis=mybir.AxisListType.X)
        v.edge()
        v.tensor_scalar(out=btabs[:, :], in0=btabs[:, :], scalar1=st[0:8, 0:1],
                        scalar2=None, op0=OP.mult)
    cbt = P.sec("v", vbt)
    guardc = P.ccol(1e-30, rows=8)
    lnsc = P.ccol(float(np.log(SCALE)), rows=8)
    cbt2 = P.sec("s", lambda s: (
        s.wait("ve", cbt), s.wait("gc"),
        s.activation(btabs[:, :], btabs[:, :], AF.Ln, bias=guardc),
        s.edge(),
        s.activation(btabs[:, :], btabs[:, :], AF.Exp, scale=0.5, bias=lnsc),
        s.edge(),
        s.mul(btabs[:, :], btabs[:, :], -1.0)))
    P.sec("y", lambda y: (y.wait("se", cbt2),
                          y.dma_start(out=btab_dram.rearrange("o (g h) -> (o h) g", h=8),
                                      in_=btabs[:, :], allow_slow_non_contiguous=True)))
    cr = P.val("dy")
    P.sec("y", lambda y: (y.wait("dy", cr),
                          y.dma_start(out=btabr[:, :], in_=btab_dram)))
    btr_bf = P.sb(es2, "a_btrbf", [1, 64], BF16)
    cbf = P.sec("v", lambda v: (v.wait("dy"), v.wait("gc"),
                                v.tensor_copy(btr_bf[:, :], btabr[:, :])))
    onesrb = P.sb(es2, "a_onesrb", [1, 128], BF16)
    P.sec("g", lambda g: g.memset(onesrb[:, :], 1.0))
    cpe = P.sec("t", lambda t: (
        t.wait("ve", cbf), t.wait("gc"),
        mm_chain(t, P.ps[3][:, 0:64], [(onesrb[:, :], btr_bf[:, :])])))
    P.sec("v", lambda v: (v.wait("pe", cpe),
                          v.tensor_copy(bbc.ap().rearrange("p g h -> p (g h)"),
                                        P.ps[3][:, 0:64])))

    for cc2 in range(2):
        for gg in range(8):
            def tfn(t, cc2=cc2, gg=gg):
                t.wait("ve")
                mm_chain(t, P.ps[0][:, :], [(e1[:, cc2, :], q1s[:, gg * 512:(gg + 1) * 512])])
                mm_chain(t, P.ps[1][:, :], [(e2[:, cc2, :], q2s[:, gg * 512:(gg + 1) * 512])])
                mm_chain(t, P.ps[2][:, :], [(e1[:, cc2, :], k1s[:, gg * 512:(gg + 1) * 512])])
                mm_chain(t, P.ps[3][:, :], [(e2[:, cc2, :], k2s[:, gg * 512:(gg + 1) * 512])])
            cpe = P.sec("t", tfn)

            def vfn(v, cc2=cc2, gg=gg):
                v.wait("pe", cpe)
                v.tensor_copy(pp1[:, :], P.ps[0][:, :])
                v.tensor_copy(pp2[:, :], P.ps[2][:, :])
                v.edge()
                v.tensor_mul(qt[:, cc2, gg * 512:(gg + 1) * 512], pp1[:, :], P.ps[1][:, :])
                v.tensor_mul(kt[:, cc2, gg * 512:(gg + 1) * 512], pp2[:, :], P.ps[3][:, :])
            P.sec("v", vfn)
    es2.close()
    P.barrier()

    es3 = contextlib.ExitStack()
    vsb = P.sb(es3, "a_vsb", [128, 32, 512], BF16, side="right")
    for kc in range(NT):
        cpe = P.sec("t", lambda t, kc=kc: (
            t.wait("ve"),
            mm_chain(t, P.ps[kc % 2][:, :],
                     [(hn[:, cch, kc * 128:(kc + 1) * 128], w_v[:, cch, :])
                      for cch in range(8)])))
        P.sec("v", lambda v, kc=kc: (
            v.wait("pe", cpe),
            v.tensor_copy(vsb[:, kc, :], P.ps[kc % 2][:, :])))
    P.barrier()

    esv = contextlib.ExitStack()
    av = P.sb(esv, "a_av", [128, 4, 4096], BF16, side="right")
    es4 = contextlib.ExitStack()
    ptb = P.sb(es4, "a_ptb", [128, 4, 512], BF16)
    lsb = P.sb(es4, "a_lsb", [128, 512], F32)
    lbc = P.sb(es4, "a_lbc", [128, 2, 512], F32)
    for hq in range(2):
        for gg in range(8):
            nkc = 4 * gg + 4
            for kc in range(nkc):
                diag = kc - 4 * gg
                def sfn_t(t, hq=hq, gg=gg, kc=kc):
                    t.wait("se")
                    t.wait("ve")
                    t.wait("gc")
                    for i in range(4):
                        t.matmul(P.ps[i][:, :],
                                 kt[32 * i:32 * (i + 1), hq, kc * 128:(kc + 1) * 128],
                                 qt[32 * i:32 * (i + 1), hq, gg * 512:(gg + 1) * 512],
                                 start=True, stop=True,
                                 tile_position=(32 * i, 0))
                cs_pe = P.sec("t", sfn_t)

                def efn(s, hq=hq, gg=gg):
                    s.wait("pe", cs_pe)
                    for i in range(4):
                        s.activation(ptb[:, i, :], P.ps[i][:, :], AF.Exp,
                                     bias=bbc[:, gg, hq * 4 + i:hq * 4 + i + 1],
                                     scale=SCALE)
                ce = P.sec("s", efn)
                if diag >= 0:
                    def mfn(v, diag=diag):
                        v.wait("se", ce)
                        for i in range(4):
                            v.tensor_mul(ptb[:, i, :], ptb[:, i, :], dm[:, diag, :])
                    cm = P.sec("v", mfn)
                    dep = ("ve", cm)
                else:
                    dep = ("se", ce)

                def avfn(t, hq=hq, gg=gg, kc=kc, dep=dep, nkc=nkc):
                    t.wait(*dep)
                    t.wait("gc")
                    if kc == 0:
                        t.wait("ve")
                    st_ = (kc == 0)
                    sp_ = (kc == nkc - 1)
                    for pair in range(2):
                        for j in range(2):
                            hh = 2 * pair + j
                            t.matmul(P.ps[4 + pair][64 * j:64 * (j + 1), :],
                                     vsb[:, kc, (hq * 4 + hh) * 64:(hq * 4 + hh) * 64 + 64],
                                     ptb[:, hh, :],
                                     start=st_, stop=sp_,
                                     tile_position=(0, 64 * j))
                    for i in range(4):
                        t.matmul(P.ps[6][32 * i:32 * i + 1, :],
                                 onescol[:, :], ptb[:, i, :],
                                 start=st_, stop=sp_,
                                 tile_position=(0, 32 * i))
                P.sec("t", avfn)
            cev = P.sec("v", lambda v: (
                v.wait("pe"),
                v.tensor_copy(lsb[:, :], P.ps[6][:, :])))
            cbcp = P.sec("t", lambda t: (
                t.wait("ve", cev),
                mm_chain(t, P.ps[0][:, :], [(s4[:, 0, :], lsb[:, :])]),
                mm_chain(t, P.ps[1][:, :], [(s4[:, 1, :], lsb[:, :])])))
            crc = P.sec("v", lambda v: (
                v.wait("pe", cbcp),
                v.reciprocal(lbc[:, 0, :], P.ps[0][:, :]),
                v.reciprocal(lbc[:, 1, :], P.ps[1][:, :])))

            def vdiv(v, hq=hq, gg=gg):
                v.wait("ve", crc)
                for pair in range(2):
                    v.tensor_mul(av[:, hq * 2 + pair, gg * 512:(gg + 1) * 512],
                                 P.ps[4 + pair][:, :], lbc[:, pair, :])
            P.sec("v", vdiv)
    es4.close()
    P.barrier()

    es5 = contextlib.ExitStack()
    emit_partial_y(P, av, w["wot"], yp, es5)
    es5.close()
    esv.close()
    es3.close()
    es0.close()
    P.barrier()


def emit_midB(P, com, w, rms1, bufA, bufB):
    es0 = contextlib.ExitStack()
    w_q = P.sb(es0, "b_wq", [128, 8, 512], BF16)
    w_qsw = P.sb(es0, "b_wqsw", [128, 8, 512], BF16)
    w_kv = P.sb(es0, "b_wkv", [128, 8, 256], BF16)
    kv = P.sb(es0, "b_kv", [128, 2, 4096], BF16)
    bbc = P.sb(es0, "b_bbc", [128, 2, 4, 8], F32)
    rs2 = P.sb(es0, "b_rs2", [128, 512], F32)
    q_all, k_all = bufA, bufB

    P.sec("g", lambda g: (
        g.wait("cc"),
        g.dma_start(out=w_q[:, :, :], in_=w["wq"].rearrange("(c p) n -> p c n", p=128)),
        g.dma_start(out=w_qsw[:, :, :], in_=w["wqsw"].rearrange("(c p) n -> p c n", p=128)),
        g.dma_start(out=w_kv[:, :, :], in_=w["wkv"].rearrange("(c p) n -> p c n", p=128))))

    def rawproj(dstA, dstB, wA, wB, src_fm, nchunk):
        for p in range(4):
            for gg in range(8):
                cpe = P.sec("t", lambda t, p=p, gg=gg: (
                    t.wait("ve"), t.wait("dg"), t.wait("gc"),
                    mm_chain(t, P.ps[0][:, :],
                             [(wA[:, cch, 128 * p:128 * (p + 1)],
                               src_fm[:, cch, gg * 512:(gg + 1) * 512])
                              for cch in range(nchunk)]),
                    mm_chain(t, P.ps[1][:, :],
                             [(wB[:, cch, 128 * p:128 * (p + 1)],
                               src_fm[:, cch, gg * 512:(gg + 1) * 512])
                              for cch in range(nchunk)])))
                P.sec("v", lambda v, p=p, gg=gg: (
                    v.wait("pe", cpe),
                    v.tensor_copy(dstA[:, p, gg * 512:(gg + 1) * 512], P.ps[0][:, :]),
                    v.tensor_copy(dstB[:, p, gg * 512:(gg + 1) * 512], P.ps[1][:, :])))
    rawproj(bufA, bufB, w_q, w_qsw, rms1, 8)

    for gg in range(8):
        cpe = P.sec("t", lambda t, gg=gg: (
            t.wait("ve"),
            mm_chain(t, P.ps[0][:, :],
                     [(w_kv[:, cch, 0:128], rms1[:, cch, gg * 512:(gg + 1) * 512])
                      for cch in range(8)]),
            mm_chain(t, P.ps[1][:, :],
                     [(w_kv[:, cch, 128:256], rms1[:, cch, gg * 512:(gg + 1) * 512])
                      for cch in range(8)])))
        P.sec("v", lambda v, gg=gg: (
            v.wait("pe", cpe),
            v.tensor_scalar(out=kv[:, 0, gg * 512:(gg + 1) * 512], in0=P.ps[0][:, :],
                            scalar1=0.25, scalar2=None, op0=OP.mult),
            v.tensor_scalar(out=kv[:, 1, gg * 512:(gg + 1) * 512], in0=P.ps[1][:, :],
                            scalar1=0.25, scalar2=None, op0=OP.mult)))
    P.barrier()
    return dict(es0=es0, q_all=q_all, k_all=k_all, kv=kv, bbc=bbc, rs2=rs2)


def emit_midB2(P, com, w, ctx, yp):
    dm, s4, eh64, onesr, onescol = (com["dm"], com["s4"], com["eh64"],
                                    com["onesr"], com["onescol"])
    q_all, k_all, kv, bbc, rs2 = (ctx["q_all"], ctx["k_all"], ctx["kv"],
                                  ctx["bbc"], ctx["rs2"])
    estr = contextlib.ExitStack()
    crep = P.sb(estr, "b_crep", [128, N], F32)
    srep = P.sb(estr, "b_srep", [128, N], F32)
    w_kup = P.sb(estr, "b_wkup", [128, 2, 512], BF16)
    w_kupsw = P.sb(estr, "b_wkupsw", [128, 2, 512], BF16)
    w_vup = P.sb(estr, "b_wvup", [128, 2, 512], BF16)
    P.sec("g", lambda g: (
        g.dma_start(out=crep[:, :], in_=w["crepc"]),
        g.dma_start(out=srep[:, :], in_=w["srepc"]),
        g.dma_start(out=w_kup[:, :, :], in_=w["wkup"].rearrange("(c p) n -> p c n", p=128)),
        g.dma_start(out=w_kupsw[:, :, :], in_=w["wkupsw"].rearrange("(c p) n -> p c n", p=128)),
        g.dma_start(out=w_vup[:, :, :], in_=w["wvup"].rearrange("(c p) n -> p c n", p=128))))

    for p in range(4):
        for gg in range(8):
            def vr(v, p=p, gg=gg):
                v.wait("dg")
                v.wait("pe")
                sl = (slice(None), p, slice(gg * 512, (gg + 1) * 512))
                v.tensor_mul(rs2[:, :], k_all[sl], srep[:, gg * 512:(gg + 1) * 512])
                v.edge()
                v.tensor_mul(q_all[sl], q_all[sl], crep[:, gg * 512:(gg + 1) * 512])
                v.edge()
                v.tensor_add(q_all[sl], q_all[sl], rs2[:, :])
            P.sec("v", vr)

    for p in range(4):
        for gg in range(8):
            cpe = P.sec("t", lambda t, p=p, gg=gg: (
                t.wait("ve"), t.wait("dg"), t.wait("gc"),
                mm_chain(t, P.ps[0][:, :],
                         [(w_kup[:, lc, 128 * p:128 * (p + 1)],
                           kv[:, lc, gg * 512:(gg + 1) * 512]) for lc in range(2)]),
                mm_chain(t, P.ps[1][:, :],
                         [(w_kupsw[:, lc, 128 * p:128 * (p + 1)],
                           kv[:, lc, gg * 512:(gg + 1) * 512]) for lc in range(2)])))

            def vr2(v, p=p, gg=gg):
                v.wait("pe", cpe)
                v.tensor_mul(rs2[:, :], P.ps[1][:, :], srep[:, gg * 512:(gg + 1) * 512])
                v.tensor_mul(k_all[:, p, gg * 512:(gg + 1) * 512], P.ps[0][:, :],
                             crep[:, gg * 512:(gg + 1) * 512])
                v.edge()
                v.tensor_add(k_all[:, p, gg * 512:(gg + 1) * 512],
                             k_all[:, p, gg * 512:(gg + 1) * 512], rs2[:, :])
            P.sec("v", vr2)

    bufC = P.sb(ctx["esR"], "b_bufC", [128, 32, 512], BF16, side="right")
    vsb = bufC
    for kc in range(NT):
        cpe = P.sec("t", lambda t, kc=kc: (
            t.wait("ve"),
            mm_chain(t, P.ps[kc % 2][:, :],
                     [(kv[:, lc, kc * 128:(kc + 1) * 128], w_vup[:, lc, :])
                      for lc in range(2)])))
        P.sec("v", lambda v, kc=kc: (
            v.wait("pe", cpe),
            v.tensor_copy(vsb[:, kc, :], P.ps[kc % 2][:, :])))
    estr.close()
    P.barrier()

    es3 = contextlib.ExitStack()
    nq = P.sb(es3, "b_nq", [128, 4096], BF16)
    nk = P.sb(es3, "b_nk", [128, 4096], BF16)
    sqq = P.sb(es3, "b_sqq", [128, 4096], BF16)
    emit_pair_norms(P, q_all, nq[:, :], eh64, sqq)
    emit_pair_norms(P, k_all, nk[:, :], eh64, sqq)
    emit_btab2(P, nq[:, :], nk[:, :], bbc, onesr, float(np.log(0.125)), es3)
    es3.close()
    P.barrier()

    esv = contextlib.ExitStack()
    av = P.sb(esv, "b_av", [128, 4, 4096], BF16, side="right")
    es4 = contextlib.ExitStack()
    emit_pair_attention(P, k_all=k_all, q_all=q_all, vsb=vsb, av=av,
                        bbc=bbc, dm=dm, s4=s4, onescol=onescol,
                        scale=0.125, causal=True, es4=es4)
    P.barrier()
    es5 = contextlib.ExitStack()
    emit_partial_y(P, av, w["wot"], yp, es5)
    es5.close()
    es4.close()
    esv.close()
    ctx["es0"].close()
    P.barrier()


def emit_midD(P, com, w, hn, q_all, k_all, vsb, yp):
    s4, eh64, onesr, onescol = com["s4"], com["eh64"], com["onesr"], com["onescol"]
    es0 = contextlib.ExitStack()
    w_q = P.sb(es0, "d_wq", [128, 8, 512], BF16)
    w_k = P.sb(es0, "d_wk", [128, 8, 512], BF16)
    w_v = P.sb(es0, "d_wv", [128, 8, 512], BF16)
    bbc = P.sb(es0, "d_bbc", [128, 2, 4, 8], F32)
    P.sec("g", lambda g: (
        g.wait("cc"),
        g.dma_start(out=w_q[:, :, :], in_=w["wq"].rearrange("(c p) n -> p c n", p=128)),
        g.dma_start(out=w_k[:, :, :], in_=w["wk"].rearrange("(c p) n -> p c n", p=128)),
        g.dma_start(out=w_v[:, :, :], in_=w["wv"].rearrange("(c p) n -> p c n", p=128))))

    def proj_pair(dst_all, wsb):
        for p in range(4):
            for gg in range(8):
                cpe = P.sec("t", lambda t, p=p, gg=gg: (
                    t.wait("ve"), t.wait("dg"), t.wait("gc"),
                    mm_chain(t, P.ps[0][:, :],
                             [(wsb[:, cch, 128 * p:128 * (p + 1)],
                               hn[:, cch, gg * 512:(gg + 1) * 512])
                              for cch in range(8)])))
                P.sec("v", lambda v, p=p, gg=gg: (
                    v.wait("pe", cpe),
                    v.tensor_copy(dst_all[:, p, gg * 512:(gg + 1) * 512],
                                  P.ps[0][:, :])))
    proj_pair(q_all, w_q)
    proj_pair(k_all, w_k)
    for kc in range(NT):
        cpe = P.sec("t", lambda t, kc=kc: (
            t.wait("ve"),
            mm_chain(t, P.ps[kc % 2][:, :],
                     [(hn[:, cch, kc * 128:(kc + 1) * 128], w_v[:, cch, :])
                      for cch in range(8)])))
        P.sec("v", lambda v, kc=kc: (
            v.wait("pe", cpe),
            v.tensor_copy(vsb[:, kc, :], P.ps[kc % 2][:, :])))
    P.barrier()
    if "hn_done" in w:
        w["hn_done"]()

    es3 = contextlib.ExitStack()
    nq = P.sb(es3, "d_nq", [128, 4096], BF16)
    nk = P.sb(es3, "d_nk", [128, 4096], BF16)
    sqq = P.sb(es3, "d_sqq", [128, 4096], BF16)
    emit_pair_norms(P, q_all, nq[:, :], eh64, sqq)
    emit_pair_norms(P, k_all, nk[:, :], eh64, sqq)
    emit_btab2(P, nq[:, :], nk[:, :], bbc, onesr, float(np.log(0.125)), es3)
    es3.close()
    P.barrier()

    es4 = contextlib.ExitStack()
    av2 = P.sb(es4, "d_av", [128, 4, 4096], BF16, side="right")
    emit_pair_attention(P, k_all=k_all, q_all=q_all, vsb=vsb, av=av2,
                        bbc=bbc, dm=None, s4=s4, onescol=onescol,
                        scale=0.125, causal=False, es4=es4)
    P.barrier()
    es5 = contextlib.ExitStack()
    emit_partial_y(P, av2, w["wot"], yp, es5)
    es5.close()
    es4.close()
    es0.close()
    P.barrier()


def emit_bodyC(P, com, w, a_src, x1_src, xdst):
    ident, onesr = com["ident"], com["onesr"]
    es0 = contextlib.ExitStack()
    ln2bc = P.sb(es0, "c_ln2bc", [128, 1024], F32)
    rbbc = P.sb(es0, "c_rbbc", [128, 8], F32)
    w_r = P.sb(es0, "c_wr", [128, 8, 8], BF16)
    rms2 = P.sb(es0, "c_rms2", [128, 8, 2048], BF16)
    gam = P.sb(es0, "c_gam", [128, 16, 12], F32)
    acc = P.sb(es0, "c_acc", [128, 16, 1024], F32, side="right")
    h1 = P.sb(es0, "c_h1", [128, 16, 1024], BF16, side="right")
    w2all = P.sb(es0, "c_w2all", [128, 16, 1024], BF16, side="right")

    P.sec("g", lambda g: (
        g.wait("cc"),
        g.dma_start(out=w_r[:, :, :], in_=w["wr"].rearrange("(c p) n -> p c n", p=128))))
    emit_row_bcast(P, w["ln2w"], ln2bc, 1024, onesr, es0)
    emit_row_bcast(P, w["rbias"], rbbc, 8, onesr, es0)
    P.sec("y", lambda y: (
        y.wait("ve"), y.wait("se"), y.wait("cc"),
        y.dma_start(out=acc[:, :, :], in_=a_src.rearrange("(t p) c -> p t c", p=128))))
    c_a = P.val("dy")

    es1 = contextlib.ExitStack()
    rmst = P.sb(es1, "c_rmst", [128, 1024], BF16)
    sq = P.sb(es1, "c_sq", [128, 1024], F32)
    st = P.sb(es1, "c_st", [128, 8], F32)
    scr = dict(sq=sq, st=st)
    P.sec("v", lambda v: v.wait("dy", c_a))
    for tt in range(ONT):
        c = emit_norm(P, acc[:, tt, :], rmst[:, :], scr, eps=1e-6, kind="rms",
                      gain_bc=ln2bc[:, :])
        emit_transpose_tile(P, rmst, rms2, tt, c, ident)

    lg = P.sb(es1, "c_lg", [128, 8], F32)
    pb = P.sb(es1, "c_pb", [128, 8], F32)
    mx8 = P.sb(es1, "c_mx8", [128, 8], F32)
    ix8 = P.sb(es1, "c_ix8", [128, 8], mybir.dt.uint32)
    ixf = P.sb(es1, "c_ixf", [128, 1], F32)
    tw = P.sb(es1, "c_tw", [128, 4], F32)
    for tt in range(ONT):
        cpe = P.sec("t", lambda t, tt=tt: (
            t.wait("ve"), t.wait("dg"), t.wait("gc"),
            mm_chain(t, P.ps[0][:, 0:8],
                     [(rms2[:, cch, tt * 128:(tt + 1) * 128], w_r[:, cch, :])
                      for cch in range(8)])))

        def vrouter(v, tt=tt):
            v.wait("pe", cpe)
            v.tensor_add(lg[:, :], P.ps[0][:, 0:8], rbbc[:, :])
            v.edge()
            v.tensor_scalar(out=lg[:, :], in0=lg[:, :], scalar1=50.0, scalar2=-50.0,
                            op0=OP.min, op1=OP.max)
        cv = P.sec("v", vrouter)
        cs = P.sec("s", lambda s: (
            s.wait("ve", cv),
            s.activation(pb[:, :], lg[:, :], AF.Sigmoid)))

        def vsel(v, tt=tt):
            v.wait("se", cs)
            v.max(mx8[:, :], pb[:, :])
            v.edge()
            v.max_index(ix8[:, :], mx8[:, :], pb[:, :])
            v.tensor_scalar(out=tw[:, 0:1], in0=mx8[:, 0:1], scalar1=1e-8,
                            scalar2=1.0 - 1e-8, op0=OP.max, op1=OP.min)
            v.edge()
            v.tensor_scalar(out=tw[:, 1:2], in0=tw[:, 0:1],
                            scalar1=0.5 + 1e-8, scalar2=None, op0=OP.add)
            v.tensor_copy(ixf[:, :], ix8[:, 0:1])
            v.edge()
            v.tensor_scalar(out=tw[:, 1:2], in0=tw[:, 1:2], scalar1=0.5,
                            scalar2=2.0, op0=OP.max, op1=OP.min)
            v.edge()
            v.reciprocal(tw[:, 2:3], tw[:, 1:2])
            v.edge()
            v.tensor_scalar(out=gam[:, tt, 8:9], in0=tw[:, 2:3], scalar1=0.5,
                            scalar2=None, op0=OP.mult)
            v.tensor_mul(tw[:, 3:4], tw[:, 0:1], tw[:, 2:3])
            v.edge()
            for e in range(8):
                v.tensor_scalar(out=gam[:, tt, e:e + 1], in0=ixf[:, :],
                                scalar1=float(e), scalar2=None, op0=OP.is_equal)
            v.edge()
            for e in range(8):
                v.tensor_mul(gam[:, tt, e:e + 1], gam[:, tt, e:e + 1], tw[:, 3:4])
        P.sec("v", vsel)
    es1.close()
    P.barrier()

    es_w = contextlib.ExitStack()
    w1s = P.sb(es_w, "c_w1s", [128, 2, 2, 8, 128], BF16)
    w3s = P.sb(es_w, "c_w3s", [128, 2, 2, 8, 128], BF16)
    sil = P.sb(es_w, "c_sil", [128, 2, 512], F32)
    stx = {}
    emit_expert(P, e_idx=-1, kind="swiglu", rms2=rms2, h1=h1, w1s=w1s, w3s=w3s,
                w2all=w2all, sil=sil, acc=acc,
                gam_ap=lambda tt: gam[:, tt, 8:9],
                w1_d=w["sw1"], w3_d=w["sw3"], w2_d=w["sw2"], state=stx)
    for e in range(8):
        emit_expert(P, e_idx=e, kind="swiglu", rms2=rms2, h1=h1, w1s=w1s, w3s=w3s,
                    w2all=w2all, sil=sil, acc=acc,
                    gam_ap=lambda tt, e=e: gam[:, tt, e:e + 1],
                    w1_d=w["ew1"][e], w3_d=w["ew3"][e], w2_d=w["ew2"][e], state=stx)
    P.barrier()
    es_w.close()

    es4 = contextlib.ExitStack()
    xt = P.sb(es4, "c_xt", [128, 2, 1024], F32)
    lnt = P.sb(es4, "c_lnt", [128, 1024], F32)
    sq4 = P.sb(es4, "c_sq4", [128, 1024], F32)
    st4 = P.sb(es4, "c_st4", [128, 8], F32)
    scr4 = dict(sq=sq4, st=st4)
    for tt in range(ONT):
        buf = tt % 2
        P.sec("y", lambda y, tt=tt, buf=buf: (
            y.wait("ve"),
            y.dma_start(out=xt[:, buf, :], in_=x1_src[tt * 128:(tt + 1) * 128, :])))
        cld = P.val("dy")
        c = emit_norm(P, acc[:, tt, :], lnt[:, :], scr4, eps=1e-5, kind="ln")

        def resid(v, buf=buf, cld=cld):
            v.wait("ve", c)
            v.wait("dy", cld)
            v.tensor_add(xt[:, buf, :], xt[:, buf, :], lnt[:, :])
        cr = P.sec("v", resid)
        P.sec("y", lambda y, tt=tt, buf=buf: (
            y.wait("ve", cr),
            y.dma_start(out=xdst[tt * 128:(tt + 1) * 128, :], in_=xt[:, buf, :])))
    es4.close()
    es0.close()
    P.barrier()


def emit_bodyE(P, com, w, xsrc, xdst):
    ident, onesr = com["ident"], com["onesr"]
    es0 = contextlib.ExitStack()
    onesc8 = P.sb(es0, "e_onesc8", [8, 1], F32)
    gbbc = P.sb(es0, "e_gbbc", [128, 8], F32)
    rmswbc = P.sb(es0, "e_rmswbc", [128, 8], F32)
    w_g = P.sb(es0, "e_wg", [128, 8, 8], BF16)
    b1sb = P.sb(es0, "e_b1sb", [128, 8, 16], F32)
    b2sb = P.sb(es0, "e_b2sb", [8, 1024], F32)
    b2row = P.sb(es0, "e_b2row", [1, 1024], F32)
    b2bc = P.sb(es0, "e_b2bc", [128, 1024], F32)
    hn = P.sb(es0, "e_hn", [128, 8, 2048], BF16)
    ng = P.sb(es0, "e_ng", [128, 16, 1], F32)
    acc = P.sb(es0, "e_acc", [128, 16, 1024], F32, side="right")
    h1 = P.sb(es0, "e_h1", [128, 16, 1024], BF16, side="right")
    w2all = P.sb(es0, "e_w2all", [128, 16, 1024], BF16, side="right")

    P.sec("g", lambda g: (
        g.wait("cc"),
        g.dma_start(out=w_g[:, :, :], in_=w["gwt"].rearrange("(c p) n -> p c n", p=128)),
        g.dma_start(out=b1sb[:, :, :], in_=w["b1"]),
        g.dma_start(out=b2sb[:, :], in_=w["b2"]),
        g.memset(onesc8[:, :], 1.0)))
    emit_row_bcast(P, w["gb"], gbbc, 8, onesr, es0)
    emit_row_bcast(P, w["rmsw"], rmswbc, 8, onesr, es0)
    cpe = P.sec("t", lambda t: (
        t.wait("dg"), t.wait("gc"), t.wait("ve"),
        mm_chain(t, P.ps[0][0:1, :], [(onesc8[:, :], b2sb[:, 0:512])]),
        mm_chain(t, P.ps[1][0:1, :], [(onesc8[:, :], b2sb[:, 512:1024])])))
    cv = P.sec("v", lambda v: (
        v.wait("pe", cpe),
        v.tensor_copy(b2row[:, 0:512], P.ps[0][0:1, :]),
        v.tensor_copy(b2row[:, 512:1024], P.ps[1][0:1, :])))
    cpe2 = P.sec("t", lambda t: (
        t.wait("ve", cv),
        mm_chain(t, P.ps[0][:, :], [(onesr[:, :], b2row[:, 0:512])]),
        mm_chain(t, P.ps[1][:, :], [(onesr[:, :], b2row[:, 512:1024])])))
    P.sec("v", lambda v: (
        v.wait("pe", cpe2),
        v.tensor_copy(b2bc[:, 0:512], P.ps[0][:, :]),
        v.tensor_copy(b2bc[:, 512:1024], P.ps[1][:, :])))
    P.sec("v", lambda v: (v.wait("ve"),
                          [v.tensor_copy(acc[:, tt, :], b2bc[:, :])
                           for tt in range(ONT)]))

    es1 = contextlib.ExitStack()
    xt2 = P.sb(es1, "e_xt2", [128, 2, 1024], F32)
    hnt = P.sb(es1, "e_hnt", [128, 1024], BF16)
    sq = P.sb(es1, "e_sq", [128, 1024], F32)
    st = P.sb(es1, "e_st", [128, 8], F32)
    scr = dict(sq=sq, st=st)
    for tt in range(ONT):
        buf = tt % 2
        P.sec("y", lambda y, tt=tt, buf=buf: (
            y.wait("ve"), y.wait("se"),
            y.dma_start(out=xt2[:, buf, :], in_=xsrc[tt * 128:(tt + 1) * 128, :])))
        c = emit_norm(P, xt2[:, buf, :], hnt[:, :], scr, eps=1e-5, kind="ln")
        emit_transpose_tile(P, hnt, hn, tt, c, ident)

    lg = P.sb(es1, "e_lg", [128, 8], F32)
    gs = P.sb(es1, "e_gs", [128, 8], F32)
    EPSF = float(np.finfo(np.float32).eps)
    for tt in range(ONT):
        cpe = P.sec("t", lambda t, tt=tt: (
            t.wait("ve"), t.wait("dg"), t.wait("gc"),
            mm_chain(t, P.ps[0][:, 0:8],
                     [(hn[:, cch, tt * 128:(tt + 1) * 128], w_g[:, cch, :])
                      for cch in range(8)])))

        def vg1(v, tt=tt):
            v.wait("pe", cpe)
            v.tensor_add(lg[:, :], P.ps[0][:, 0:8], gbbc[:, :])
            v.edge()
            v.tensor_mul(gs[:, :], lg[:, :], lg[:, :])
            v.edge()
            v.reduce_sum(st[:, 0:1], gs[:, :], axis=mybir.AxisListType.X)
            v.edge()
            v.tensor_scalar(out=st[:, 1:2], in0=st[:, 0:1], scalar1=1.0 / 8,
                            scalar2=None, op0=OP.mult)
        cv1 = P.sec("v", vg1)
        epsc = P.ccol(EPSF)
        cs1 = P.sec("s", lambda s: (
            s.wait("ve", cv1), s.wait("gc"),
            s.activation(st[:, 2:3], st[:, 1:2], AF.Ln, bias=epsc),
            s.edge(),
            s.activation(st[:, 3:4], st[:, 2:3], AF.Exp, scale=-0.5)))

        def vg2(v, tt=tt):
            v.wait("se", cs1)
            v.tensor_scalar(out=lg[:, :], in0=lg[:, :], scalar1=st[:, 3:4],
                            scalar2=None, op0=OP.mult)
            v.edge()
            v.tensor_mul(lg[:, :], lg[:, :], rmswbc[:, :])
            v.edge()
            v.reduce_max(st[:, 4:5], lg[:, :], axis=mybir.AxisListType.X)
            v.edge()
            v.tensor_scalar(out=st[:, 5:6], in0=st[:, 4:5], scalar1=-2.0,
                            scalar2=None, op0=OP.mult)
        cv2 = P.sec("v", vg2)
        cs2 = P.sec("s", lambda s: (
            s.wait("ve", cv2),
            s.activation(gs[:, :], lg[:, :], AF.Exp, bias=st[:, 5:6], scale=2.0)))

        def vg3(v, tt=tt):
            v.wait("se", cs2)
            v.reduce_sum(st[:, 6:7], gs[:, :], axis=mybir.AxisListType.X)
            v.edge()
            v.tensor_scalar(out=st[:, 7:8], in0=st[:, 6:7], scalar1=1e-6,
                            scalar2=1.0, op0=OP.mult, op1=OP.add)
            v.edge()
            v.reciprocal(ng[:, tt, :], st[:, 7:8])
        P.sec("v", vg3)
    es1.close()
    P.barrier()

    es_w = contextlib.ExitStack()
    w1s = P.sb(es_w, "e_w1s", [128, 2, 2, 8, 128], BF16)
    stx = {}
    for e in range(8):
        emit_expert(P, e_idx=e, kind="gelu", rms2=hn, h1=h1, w1s=w1s, w3s=None,
                    w2all=w2all, sil=None, acc=acc, gam_ap=None,
                    w1_d=w["w1"][e], w3_d=None, w2_d=w["w2"][e],
                    b1col_fn=lambda m, e=e: b1sb[:, e, m:m + 1], state=stx)
    P.barrier()
    es_w.close()

    es4 = contextlib.ExitStack()
    xt4 = P.sb(es4, "e_xt4", [128, 2, 1024], F32)
    lnt = P.sb(es4, "e_lnt", [128, 1024], F32)
    sq4 = P.sb(es4, "e_sq4", [128, 1024], F32)
    st4 = P.sb(es4, "e_st4", [128, 8], F32)
    scr4 = dict(sq=sq4, st=st4)
    for tt in range(ONT):
        buf = tt % 2
        P.sec("y", lambda y, tt=tt, buf=buf: (
            y.wait("ve"),
            y.dma_start(out=xt4[:, buf, :], in_=xsrc[tt * 128:(tt + 1) * 128, :])))
        cld = P.val("dy")
        csc = P.sec("v", lambda v, tt=tt: (
            v.wait("pe"), v.wait("se"),
            v.tensor_scalar(out=acc[:, tt, :], in0=acc[:, tt, :],
                            scalar1=ng[:, tt, :], scalar2=None, op0=OP.mult)))
        c = emit_norm(P, acc[:, tt, :], lnt[:, :], scr4, eps=1e-5, kind="ln")

        def resid(v, tt=tt, buf=buf, cld=cld):
            v.wait("ve", c)
            v.wait("dy", cld)
            v.tensor_add(xt4[:, buf, :], xt4[:, buf, :], lnt[:, :])
        cr = P.sec("v", resid)
        P.sec("y", lambda y, tt=tt, buf=buf: (
            y.wait("ve", cr),
            y.dma_start(out=xdst[tt * 128:(tt + 1) * 128, :], in_=xt4[:, buf, :])))
    es4.close()
    es0.close()


def build_fused():
    P = Prog("MK")
    x_own = P.inp("x_own", [2048, C])
    xo = P.outp("xo", [2048, C])
    wa = dict(
        wq1t=P.inp("a_wq1t", [C, 32]), wk1t=P.inp("a_wk1t", [C, 32]),
        wq2t=P.inp("a_wq2t", [C, 64]), wk2t=P.inp("a_wk2t", [C, 64]),
        wvt=P.inp("a_wvt", [C, 512]), wot=P.inp("a_wot", [512, C]),
        e1c=P.inp("a_e1c", [32, 2, 128]), e2c=P.inp("a_e2c", [64, 2, 128]),
        eh1c=P.inp("a_eh1c", [32, 8]), eh2c=P.inp("a_eh2c", [64, 8]))
    bo_a = P.inp("a_bo", [1, C])
    wb = dict(
        wq=P.inp("b_wq", [C, 512]), wqsw=P.inp("b_wqsw", [C, 512]),
        wkv=P.inp("b_wkv", [C, 256]), wkup=P.inp("b_wkup", [256, 512]),
        wkupsw=P.inp("b_wkupsw", [256, 512]), wvup=P.inp("b_wvup", [256, 512]),
        wot=P.inp("b_wot", [512, C]),
        crepc=P.inp("b_crepc", [128, N]), srepc=P.inp("b_srepc", [128, N]))
    ln1w = P.inp("b_ln1w", [1, C])
    wc = dict(
        ln2w=P.inp("c_ln2w", [1, C]), wr=P.inp("c_wr", [C, 8]),
        rbias=P.inp("c_rbias", [1, 8]),
        sw1=P.inp("c_sw1", [8, 128, 2, 8, 128]), sw2=P.inp("c_sw2", [128, 16, 1024]),
        sw3=P.inp("c_sw3", [8, 128, 2, 8, 128]),
        ew1=P.inp("c_ew1", [8, 8, 128, 2, 8, 128]), ew2=P.inp("c_ew2", [8, 128, 16, 1024]),
        ew3=P.inp("c_ew3", [8, 8, 128, 2, 8, 128]))
    wd = dict(
        wq=P.inp("d_wq", [C, 512]), wk=P.inp("d_wk", [C, 512]),
        wv=P.inp("d_wv", [C, 512]), wot=P.inp("d_wot", [512, C]))
    bo_d = P.inp("d_bo", [1, C])
    we = dict(
        gwt=P.inp("e_gwt", [C, 8]), gb=P.inp("e_gb", [1, 8]),
        rmsw=P.inp("e_rmsw", [1, 8]), w1=P.inp("e_w1", [8, 8, 128, 2, 8, 128]),
        b1=P.inp("e_b1", [128, 8, 16]), w2=P.inp("e_w2", [8, 128, 16, 1024]),
        b2=P.inp("e_b2", [8, C]))
    identc = P.inp("identc", [128, 128])
    dmc = P.inp("dmc", [128, 4, 512])
    s4c = P.inp("s4c", [128, 2, 128])
    eh64c = P.inp("eh64c", [128, 2])

    x1s = P.dram("x1s", [2048, C])
    x2s = P.dram("x2s", [2048, C])
    x3s = P.dram("x3s", [2048, C])
    a_s = P.dram("a_s", [2048, C])
    gin = P.dram("gin", [1024, 2048], BF16)
    gout = P.dram("gout", [2048, 2048], BF16)
    ypd = P.dram("ypd", [N, C])
    ysd = P.dram("ysd", [2048, C])

    esc = contextlib.ExitStack()
    ident = P.sb(esc, "ident", [128, 128], BF16)
    onesr = P.sb(esc, "onesr", [1, 128], F32)
    onescol = P.sb(esc, "onescol", [128, 1], BF16)
    dm = P.sb(esc, "dm", [128, 4, 512], BF16)
    s4 = P.sb(esc, "s4", [128, 2, 128], F32)
    eh64 = P.sb(esc, "eh64", [128, 2], BF16)
    P.sec("g", lambda g: (
        g.dma_start(out=ident[:, :], in_=identc),
        g.dma_start(out=dm[:, :, :], in_=dmc),
        g.dma_start(out=s4[:, :, :], in_=s4c),
        g.dma_start(out=eh64[:, :], in_=eh64c),
        g.memset(onesr[:, :], 1.0),
        g.memset(onescol[:, :], 1.0)))
    com = dict(ident=ident, onesr=onesr, onescol=onescol, dm=dm, s4=s4, eh64=eh64)
    P.sem("cc")

    esA = contextlib.ExitStack()
    hn_own = P.sb(esA, "A_hnown", [128, 8, 2048], BF16)
    emit_ln_to_fm(P, x_own, hn_own, esA, ident, ntiles=ONT)
    ccg = emit_gather_send(P, hn_own, gin, gout)
    P.barrier()
    esA.close()
    esAf = contextlib.ExitStack()
    hn = P.sb(esAf, "A_hn", [128, 8, 4096], BF16, side="right")
    emit_gather_recv(P, hn, gout, ccg)
    P.barrier()
    emit_midA(P, com, wa, hn, ypd)
    ccv = emit_rs_y(P, ypd, ysd)
    esAf.close()
    emit_back_AD(P, com, ysd, bo_a, x_own, x1s, ccv)
    P.barrier()

    esB = contextlib.ExitStack()
    ln1bc = P.sb(esB, "B_ln1bc", [128, 1024], F32)
    emit_row_bcast(P, ln1w, ln1bc, 1024, onesr, esB)
    esBR = contextlib.ExitStack()
    bufA = P.sb(esBR, "B_bufA", [128, 4, 4096], BF16, side="right")
    bufB = P.sb(esBR, "B_bufB", [128, 4, 4096], BF16, side="right")
    esB1 = contextlib.ExitStack()
    rms1own = P.sb(esB1, "B_rms1own", [128, 8, 2048], BF16)
    emit_ln_to_fm(P, x1s, rms1own, esB1, ident, ntiles=ONT,
                  rms_eps=1e-6, gain_bc=ln1bc[:, :])
    ccg = emit_gather_send(P, rms1own, gin, gout)
    P.barrier()
    esB1.close()
    esBf = contextlib.ExitStack()
    rms1 = P.sb(esBf, "B_rms1", [128, 8, 4096], BF16, side="right")
    emit_gather_recv(P, rms1, gout, ccg)
    P.barrier()
    ctxB = emit_midB(P, com, wb, rms1, bufA, bufB)
    ctxB["esR"] = esBR
    esBf.close()
    P.barrier()
    emit_midB2(P, com, wb, ctxB, ypd)
    ccv = emit_rs_y(P, ypd, ysd)
    emit_back_B(P, com, ysd, x1s, a_s, ccv)
    esB.close()
    esBR.close()
    P.barrier()

    emit_bodyC(P, com, wc, a_s, x1s, x2s)

    esDR = contextlib.ExitStack()
    qD = P.sb(esDR, "D_q", [128, 4, 4096], BF16, side="right")
    kD = P.sb(esDR, "D_k", [128, 4, 4096], BF16, side="right")
    vD = P.sb(esDR, "D_v", [128, 32, 512], BF16, side="right")
    esD = contextlib.ExitStack()
    hn_ownD = P.sb(esD, "D_hnown", [128, 8, 2048], BF16)
    emit_ln_to_fm(P, x2s, hn_ownD, esD, ident, ntiles=ONT)
    ccg = emit_gather_send(P, hn_ownD, gin, gout)
    P.barrier()
    esD.close()
    esDf = contextlib.ExitStack()
    hnD = P.sb(esDf, "D_hn", [128, 8, 4096], BF16, side="right")
    emit_gather_recv(P, hnD, gout, ccg)
    P.barrier()
    wd["hn_done"] = lambda: esDf.close()
    emit_midD(P, com, wd, hnD, qD, kD, vD, ypd)
    ccv = emit_rs_y(P, ypd, ysd)
    emit_back_AD(P, com, ysd, bo_d, x2s, x3s, ccv)
    esDR.close()
    P.barrier()

    emit_bodyE(P, com, we, x3s, xo)
    P.sec("y", lambda y: y.wait_all())
    esc.close()
    return P.finish()


def kernel_fused(inputs):
    x = inputs["L0_attn_x"]
    consts = _consts_A()
    cbd = _consts_BD()
    crep, srep = _rope_tables()
    nc = build_fused()
    in_maps = []
    for core in range(8):
        b, half = core // 2, core % 2
        h0 = half * 8
        sl4 = slice(h0 * 4, (h0 + 8) * 4)
        sl8 = slice(h0 * 8, (h0 + 8) * 8)
        sl64 = slice(h0 * 64, (h0 + 8) * 64)
        tok = slice(half * 2048, (half + 1) * 2048)
        m = dict(
            x_own=np.ascontiguousarray(x[b, tok]),
            a_wq1t=np.ascontiguousarray(inputs["L0_attn_Wq1"][sl4].T),
            a_wk1t=np.ascontiguousarray(inputs["L0_attn_Wk1"][sl4].T),
            a_wq2t=np.ascontiguousarray(inputs["L0_attn_Wq2"][sl8].T),
            a_wk2t=np.ascontiguousarray(inputs["L0_attn_Wk2"][sl8].T),
            a_wvt=np.ascontiguousarray(inputs["L0_attn_Wv"][sl64].T),
            a_wot=np.ascontiguousarray(inputs["L0_attn_Wo"].T[sl64]),
            a_bo=inputs["L0_attn_bo"].reshape(1, C),
            a_e1c=consts["e1c"], a_e2c=consts["e2c"],
            a_eh1c=consts["eh1c"], a_eh2c=consts["eh2c"],
            identc=consts["identc"], dmc=consts["dmc"], s4c=consts["s4c"],
            eh64c=cbd["eh64c"],
        )
        wq = inputs["L0_ffn_Wq"][:, h0 * 64:(h0 + 8) * 64]
        wqsw = wq.reshape(1024, 8, 2, 32)[:, :, ::-1, :].reshape(1024, 512)
        wkup = inputs["L0_ffn_Wk_up"][h0:h0 + 8].transpose(1, 0, 2).reshape(256, 512)
        wkupsw = wkup.reshape(256, 8, 2, 32)[:, :, ::-1, :].reshape(256, 512)
        wvup = inputs["L0_ffn_Wv_up"][h0:h0 + 8].transpose(1, 0, 2).reshape(256, 512)
        m.update(
            b_wq=np.ascontiguousarray(wq), b_wqsw=np.ascontiguousarray(wqsw),
            b_wkv=np.ascontiguousarray(inputs["L0_ffn_Wkv"]),
            b_wkup=np.ascontiguousarray(wkup), b_wkupsw=np.ascontiguousarray(wkupsw),
            b_wvup=np.ascontiguousarray(wvup),
            b_wot=np.ascontiguousarray(inputs["L0_ffn_Wo"][h0 * 64:(h0 + 8) * 64]),
            b_crepc=crep, b_srepc=srep, b_ln1w=inputs["L0_ffn_ln1_w"].reshape(1, C),
            c_ln2w=inputs["L0_ffn_ln2_w"].reshape(1, C),
            c_wr=np.ascontiguousarray(inputs["L0_ffn_Wr"]),
            c_rbias=inputs["L0_ffn_r_bias"].reshape(1, 8),
            c_sw1=lay_up(inputs["L0_ffn_sw1"]),
            c_sw2=lay_down(inputs["L0_ffn_sw2"]),
            c_sw3=lay_up(inputs["L0_ffn_sw3"]),
            c_ew1=np.stack([lay_up(inputs["L0_ffn_ew1"][e]) for e in range(8)]),
            c_ew2=np.stack([lay_down(inputs["L0_ffn_ew2"][e]) for e in range(8)]),
            c_ew3=np.stack([lay_up(inputs["L0_ffn_ew3"][e]) for e in range(8)]),
            d_wq=np.ascontiguousarray(inputs["L1_attn_Wq"][:, sl64]),
            d_wk=np.ascontiguousarray(inputs["L1_attn_Wkv"][:, 0:1024][:, sl64]),
            d_wv=np.ascontiguousarray(inputs["L1_attn_Wkv"][:, 1024:2048][:, sl64]),
            d_wot=np.ascontiguousarray(inputs["L1_attn_Wo"][sl64]),
            d_bo=inputs["L1_attn_bo"].reshape(1, C),
            e_gwt=np.ascontiguousarray(inputs["L1_ffn_gate_w"].T),
            e_gb=inputs["L1_ffn_gate_b"].reshape(1, 8),
            e_rmsw=inputs["L1_ffn_rms_w"].reshape(1, 8),
            e_w1=np.stack([lay_up(inputs["L1_ffn_w1"][e]) for e in range(8)]),
            e_b1=np.ascontiguousarray(inputs["L1_ffn_b1"].reshape(8, 16, 128).transpose(2, 0, 1)),
            e_w2=np.stack([lay_down(inputs["L1_ffn_w2"][e]) for e in range(8)]),
            e_b2=np.ascontiguousarray(inputs["L1_ffn_b2"]),
        )
        in_maps.append(m)
    res = run_bass_kernel_spmd(nc, in_maps, core_ids=list(range(8)))
    x4 = np.empty_like(x)
    for core in range(8):
        b, half = core // 2, core % 2
        x4[b, half * 2048:(half + 1) * 2048] = res.results[core]["xo"]
    return x4
